# version 13
# baseline (speedup 1.0000x reference)
import numpy as np
import ml_dtypes
from contextlib import ExitStack

import concourse.bass as bass
import concourse.mybir as mybir
from concourse.bass_utils import run_bass_kernel_spmd

F32 = mybir.dt.float32
BF16 = mybir.dt.bfloat16
AF = mybir.ActivationFunctionType
ALU = mybir.AluOpType
AX = mybir.AxisListType

T = 4096
B = 8
D = 1024
TC = 512
NTC = 8
NT = 32
NC = 8
LND = float(np.log(1024.0) * 0.5)

ENGINES = ("sync", "tensor", "vector", "scalar", "gpsimd")


class Em:
    def __init__(self, nc, es: ExitStack, tag=""):
        self.nc = nc
        self.es = es
        self.tag = tag
        self.plan = {e: [] for e in ENGINES}
        self.csem = {}
        self.dsem = {}
        self.last_write = {}
        self.readers = {}
        self.waited = {e: {} for e in ENGINES}
        self.sems = {}
        self._semn = 0
        self.tagk = ""

    def _mk_sem(self, name):
        h = self.es.enter_context(self.nc.semaphore(f"{self.tag}{name}"))
        sid = self._semn
        self._semn += 1
        self.sems[sid] = h
        return sid

    def _csem(self, eng):
        if eng not in self.csem:
            self.csem[eng] = [self._mk_sem(f"c_{eng}"), 0]
        return self.csem[eng]

    def _dsem(self, slot):
        if slot not in self.dsem:
            self.dsem[slot] = [self._mk_sem(f"d_{slot}"), 0]
        return self.dsem[slot]

    def _need_waits(self, eng, reads, writes):
        toks = {}
        for r in reads:
            t = self.last_write.get(r)
            if t is not None:
                toks[t[0]] = max(toks.get(t[0], 0), t[1])
        for w in writes:
            t = self.last_write.get(w)
            if t is not None:
                toks[t[0]] = max(toks.get(t[0], 0), t[1])
            for t in self.readers.get(w, ()):
                toks[t[0]] = max(toks.get(t[0], 0), t[1])
        out = []
        wt = self.waited[eng]
        for sid, val in toks.items():
            if wt.get(sid, 0) < val:
                out.append((sid, val))
                wt[sid] = val
        return out

    def _post(self, eng, tok, reads, writes):
        for w in writes:
            self.last_write[w] = tok
            self.readers[w] = []
        for r in reads:
            self.readers.setdefault(r, []).append(tok)

    def _k(self, key):
        return key if key.startswith("g.") else self.tagk + key

    def op(self, eng, fns, reads=(), writes=()):
        if callable(fns):
            fns = [fns]
        reads = [self._k(r) for r in reads]
        writes = [self._k(w) for w in writes]
        waits = self._need_waits(eng, reads, writes)
        cs = self._csem(eng)
        cs[1] += 1
        self.plan[eng].append(("op", waits, fns, cs[0], 1))
        self._post(eng, (cs[0], cs[1]), reads, writes)

    def dma(self, eng, fn, reads=(), writes=(), slot=None):
        reads = [self._k(r) for r in reads]
        writes = [self._k(w) for w in writes]
        if slot is None:
            slot = writes[0] if writes else self._k("dma")
        else:
            slot = self._k(slot)
        waits = self._need_waits(eng, reads, writes)
        ds = self._dsem(slot)
        ds[1] += 16
        self.plan[eng].append(("op", waits, [fn], ds[0], 16))
        self._post(eng, (ds[0], ds[1]), reads, writes)

    def barrier_on(self, eng, keys):
        waits = self._need_waits(eng, keys, ())
        if waits:
            self.plan[eng].append(("waits", waits))

    def phase_barrier(self):
        for eng in ENGINES:
            waits = []
            wt = self.waited[eng]
            for sid in self.sems:
                final = 0
                for cs in self.csem.values():
                    if cs[0] == sid:
                        final = cs[1]
                for ds in self.dsem.values():
                    if ds[0] == sid:
                        final = ds[1]
                if final and wt.get(sid, 0) < final:
                    waits.append((sid, final))
                    wt[sid] = final
            if waits:
                self.plan[eng].append(("waits", waits))

    def wait_all(self, eng):
        waits = []
        wt = self.waited[eng]
        for sid, h in self.sems.items():
            final = None
            for e2, cs in self.csem.items():
                if cs[0] == sid:
                    final = cs[1]
            for e2, ds in self.dsem.items():
                if ds[0] == sid:
                    final = ds[1]
            if final and wt.get(sid, 0) < final:
                waits.append((sid, final))
                wt[sid] = final
        self.plan[eng].append(("waits", waits))

    def finalize(self, block):
        for eng in ENGINES:
            entries = self.plan[eng]
            if not entries:
                continue
            deco = getattr(block, eng)

            def body(e, entries=entries):
                for ent in entries:
                    if ent[0] == "waits":
                        for sid, val in ent[1]:
                            e.wait_ge(self.sems[sid], val)
                    else:
                        _, waits, fns, sid, inc = ent
                        for wsid, val in waits:
                            e.wait_ge(self.sems[wsid], val)
                        for i, fn in enumerate(fns):
                            instr = fn(e)
                            if i == len(fns) - 1:
                                instr.then_inc(self.sems[sid], inc)
            deco(body)


def emit_ln_tile(em, src, srck, ybf, ybfk, bufs):
    (s1, s1k), (ssq, ssqk), (sq, sqk), (st, stk), cb = bufs
    em.op("vector", lambda e: e.tensor_reduce(s1[:], src[:], AX.X, ALU.add),
          reads=[srck], writes=[s1k])
    em.op("scalar", lambda e: e.activation(sq[:], src[:], AF.Square, accum_out=ssq[:]),
          reads=[srck], writes=[sqk, ssqk])
    em.op("vector", lambda e: e.tensor_mul(st[:, 0:1], s1[:], s1[:]),
          reads=[s1k], writes=[stk])
    em.op("vector", lambda e: e.scalar_tensor_tensor(
        st[:, 1:2], st[:, 0:1], -1.0 / D, ssq[:], ALU.mult, ALU.add),
        reads=[stk, ssqk], writes=[stk])
    em.op("scalar", lambda e: e.activation(st[:, 2:3], st[:, 1:2], AF.Ln, bias=cb[:, 0:1]),
          reads=[stk, "lncb"], writes=[stk])
    em.op("scalar", lambda e: e.activation(st[:, 2:3], st[:, 2:3], AF.Exp, scale=-0.5, bias=cb[:, 1:2]),
          reads=[stk, "lncb"], writes=[stk])
    em.op("vector", lambda e: e.scalar_tensor_tensor(
        st[:, 3:4], s1[:], -1.0 / D, st[:, 2:3], ALU.mult, ALU.mult),
        reads=[s1k, stk], writes=[stk])
    em.op("vector", lambda e: e.tensor_scalar(
        ybf[:], src[:], st[:, 2:3], st[:, 3:4], ALU.mult, ALU.add),
        reads=[srck, stk], writes=[ybfk])


def emit_ln_stats(em, src, srck, st, c0, bufs):
    s1, ssq, sq, cb = bufs
    em.op("vector", lambda e: e.tensor_reduce(s1[:], src, AX.X, ALU.add),
          reads=[srck], writes=["s1"])
    em.op("scalar", lambda e: e.activation(sq[0][:], src, AF.Square, accum_out=ssq[:]),
          reads=[srck], writes=[sq[1], "ssq"])
    em.op("vector", lambda e: e.tensor_mul(st[:, 4:5], s1[:], s1[:]),
          reads=["s1"], writes=["st"])
    em.op("vector", lambda e: e.scalar_tensor_tensor(
        st[:, 5:6], st[:, 4:5], -1.0 / D, ssq[:], ALU.mult, ALU.add),
        reads=["st", "ssq"], writes=["st"])
    em.op("scalar", lambda e: e.activation(st[:, c0:c0 + 1], st[:, 5:6], AF.Ln, bias=cb[:, 0:1]),
          reads=["st", "lncb"], writes=["st"])
    em.op("scalar", lambda e: e.activation(st[:, c0:c0 + 1], st[:, c0:c0 + 1], AF.Exp,
                                           scale=-0.5, bias=cb[:, 1:2]),
          reads=["st", "lncb"], writes=["st"])
    em.op("vector", lambda e: e.scalar_tensor_tensor(
        st[:, c0 + 1:c0 + 2], s1[:], -1.0 / D, st[:, c0:c0 + 1], ALU.mult, ALU.mult),
        reads=["s1", "st"], writes=["st"])


def emit_phase_a(em, nc, es, xall, xc, x1, wqkvT, qkvb, projT, pbr, cosd, sind,
                 r2d, id16, tag=""):
    sb = lambda n, s, d: es.enter_context(nc.sbuf_tensor(tag + n, s, d))
    ps = lambda n, s, d: es.enter_context(nc.psum_tensor(tag + n, s, d))

    sscr = sb("sscr", [128, 8 * T], BF16)
    cosa = sb("cosa", [128, T], BF16)
    sina = sb("sina", [128, T], BF16)
    pjs = sb("pjs", [128, 8192], BF16)
    xlnT = [sb(f"xlnT{i}", [128, 8 * TC], BF16) for i in range(2)]
    wq3 = sb("wq3", [128, 3 * 1024], BF16)
    qTp = sb("qTp", [128, 4096], BF16)
    kTp = sb("kTp", [128, 4096], BF16)
    Vp = sb("Vp", [128, 2 * 32 * 65], BF16)
    r2 = sb("r2", [128, 128], BF16)
    sbid = sb("sbid", [128, 128], BF16)
    qkbs = sb("qkbs", [1, 384], F32)
    ones5 = sb("ones5", [1, 512], F32)
    wstg = [sb("wstg0", [128, 1024], F32)] * 2
    xt = [sb(f"xt{i}", [128, 1024], F32) for i in range(2)]
    xt4 = [xt[0], xt[1], sb("xt2", [128, 1024], F32), sb("xt3", [128, 1024], F32)]
    ot = sb("ot0", [128, 1024], F32)
    ybf2 = [sb(f"ybf{i}", [128, 1024], BF16) for i in range(2)]
    st4 = [sb(f"st4{i}", [128, 16], F32) for i in range(2)]
    lncb = sb("lncb", [128, 2], F32)
    qsb = sb("qsb", [128, 512], BF16)
    tcos = sb("tcos", [128, 512], F32)
    tsin = sb("tsin", [128, 512], F32)
    onec = sb("onec", [128, 1], BF16)
    oner = sb("oner", [1, 128], F32)
    nmx = sb("nmx", [1, 32], F32)
    cst = sb("cst", [1, 8], F32)
    bias_p = sb("bias_p", [128, 1], F32)
    pt = [sb(f"pt{i}", [128, 1024], BF16) for i in range(4)]
    linv = sb("linv", [1, 512], F32)
    lbc = sb("lbc", [128, 1024], F32)
    onorm = sb("onorm", [128, 512], F32)

    ps_s2 = [ps("ps_s2a", [128, 1024], F32), ps("ps_s2b", [128, 1024], F32)]
    ps_o2 = [ps("ps_oa", [128, 512], F32), ps("ps_ob", [128, 512], F32)]
    ps_bc = ps("ps_bc", [128, 512], F32)
    ps_t = ps("ps_t", [128, 1024], BF16)
    ps_a = ps_s2[0]
    ps_r = ps_s2[1]
    ps_pj = ps_s2[0]

    sq2 = [(wstg[0], "wstg0"), (ot, "ot0")]

    sscrv = sscr[:].rearrange("p (c n) -> p c n", n=T)
    pbs = None
    vstg = pt[2]
    q2 = pt[3]
    Vpv = Vp[:].rearrange("p (h s c) -> p h s c", h=2, s=32)

    em.dma("sync", lambda e: e.dma_start(out=sbid[:], in_=id16), writes=["sbid"])
    em.dma("sync", lambda e: e.dma_start(out=cosa[:], in_=cosd), writes=["cosa"])
    em.dma("sync", lambda e: e.dma_start(out=sina[:], in_=sind), writes=["sina"])
    em.op("vector", lambda e: e.memset(onec[:], 1.0), writes=["onec"])
    em.op("vector", lambda e: e.memset(oner[:], 1.0), writes=["oner"])
    em.op("vector", lambda e: e.memset(ones5[:], 1.0), writes=["ones5"])
    em.op("vector", lambda e: e.memset(lncb[:, 0:1], D * 1e-5), writes=["lncb"])
    em.op("vector", lambda e: e.memset(lncb[:, 1:2], LND), writes=["lncb"])
    em.op("vector", lambda e: e.memset(Vpv[:, :, :, 64:65], 1.0), writes=["Vp"])
    b, bk = wstg[0], "wstg0"
    em.dma("sync", lambda e: e.dma_start(out=b[:, 0:128], in_=r2d), writes=[bk])
    em.op("vector", lambda e: e.tensor_copy(r2[:], b[:, 0:128]), reads=[bk], writes=["r2"])
    em.dma("sync", lambda e: e.dma_start(out=b[0:1, 0:384], in_=qkvb),
           reads=["r2"], writes=[bk])
    em.op("vector", lambda e: e.tensor_copy(qkbs[:], b[0:1, 0:384]), reads=[bk], writes=["qkbs"])
    for t3 in range(3):
        em.dma("sync", lambda e, t3=t3: e.dma_start(
            out=b[:], in_=wqkvT[t3 * 128:(t3 + 1) * 128, :]), reads=["qkbs"], writes=[bk])
        em.op("vector", lambda e, t3=t3: e.tensor_copy(
            wq3[:, t3 * 1024:(t3 + 1) * 1024], b[:]), reads=[bk], writes=["wq3"])
    for kc in range(8):
        em.dma("sync", lambda e, kc=kc: e.dma_start(
            out=b[:], in_=projT[kc * 128:(kc + 1) * 128, :]), reads=["pjslast"], writes=[bk])
        em.op("vector", lambda e, kc=kc: e.tensor_copy(
            pjs[:, kc * 1024:(kc + 1) * 1024], b[:]), reads=[bk], writes=["pjslast"])

    for bb in range(B):
        for g in range(NTC):
            xl, xlk = xlnT[g % 2], f"xlnT{g % 2}"
            xlv = xl[:].rearrange("p (c n) -> p c n", n=TC)
            def _stageA(g, xlk2):
                st, stk = st4[g % 2], f"st4{g % 2}"
                for jj in range(4):
                    row0 = bb * T + g * 512 + jj * 128
                    xb, xbk = xt4[jj + 4 * (g % 2) if False else jj], f"xt{jj}"
                    em.dma("sync", lambda e, row0=row0, xb=xb: e.dma_start(
                        out=xb[:], in_=xall[row0:row0 + 128, :]), writes=[xbk])
                    em.op("vector", lambda e, xb=xb, jj=jj, st=st: e.tensor_reduce(
                        st[:, jj:jj + 1], xb[:], AX.X, ALU.add),
                        reads=[xbk], writes=[stk])
                    sq, sqk = sq2[g % 2]
                    em.op("scalar", lambda e, xb=xb, jj=jj, st=st, sq=sq: e.activation(
                        sq[:], xb[:], AF.Square, accum_out=st[:, 4 + jj:5 + jj]),
                        reads=[xbk], writes=[sqk, stk])

            def _stageB(g, xlv, xlk):
                st, stk = st4[g % 2], f"st4{g % 2}"
                em.op("vector", lambda e, st=st: e.tensor_mul(
                    st[:, 8:12], st[:, 0:4], st[:, 0:4]), reads=[stk], writes=[stk])
                em.op("vector", lambda e, st=st: e.scalar_tensor_tensor(
                    st[:, 8:12], st[:, 8:12], -1.0 / D, st[:, 4:8], ALU.mult, ALU.add),
                    reads=[stk], writes=[stk])
                em.op("scalar", lambda e, st=st: e.activation(
                    st[:, 8:12], st[:, 8:12], AF.Ln, bias=lncb[:, 0:1]),
                    reads=[stk, "lncb"], writes=[stk])
                em.op("scalar", lambda e, st=st: e.activation(
                    st[:, 8:12], st[:, 8:12], AF.Exp, scale=-0.5, bias=lncb[:, 1:2]),
                    reads=[stk, "lncb"], writes=[stk])
                em.op("vector", lambda e, st=st: e.scalar_tensor_tensor(
                    st[:, 12:16], st[:, 0:4], -1.0 / D, st[:, 8:12], ALU.mult, ALU.mult),
                    reads=[stk], writes=[stk])
                for jj in range(4):
                    xb, xbk = xt4[jj], f"xt{jj}"
                    ybf, ybfk = ybf2[jj % 2], f"ybf{jj % 2}"
                    em.op("vector", lambda e, xb=xb, jj=jj, st=st, ybf=ybf: e.tensor_scalar(
                        ybf[:], xb[:], st[:, 8 + jj:9 + jj], st[:, 12 + jj:13 + jj],
                        ALU.mult, ALU.add),
                        reads=[xbk, stk], writes=[ybfk])
                    em.op("tensor", [
                        (lambda e, k=k, ybf=ybf: e.transpose(ps_t[:, k * 128:(k + 1) * 128],
                                                    ybf[:, k * 128:(k + 1) * 128], sbid[:]))
                        for k in range(8)],
                        reads=[ybfk, "sbid"], writes=["ps_t"])
                    em.op("vector", lambda e, jj=jj, xlv=xlv: e.tensor_copy(
                        xlv[:, :, jj * 128: jj * 128 + 128],
                        ps_t[:].rearrange("p (c n) -> p c n", n=128)),
                        reads=["ps_t"], writes=[xlk])

            if g == 0:
                _stageA(g, None)
            _stageB(g, xlv, xlk)
            if g < NTC - 1:
                xl_n, xlk_n = xlnT[(g + 1) % 2], f"xlnT{(g + 1) % 2}"
                _stageA(g + 1, None)
            gc = slice(g * 512, (g + 1) * 512)
            for t3, dst, dstk in [(0, qTp, "qTp"), (1, kTp, "kTp")]:
                h0 = t3 * 512
                fns = [
                    (lambda e, kc=kc, t3=t3, xlv=xlv, h0=h0: e.matmul(
                        ps_a[:, h0:h0 + 512], wq3[:, t3 * 1024 + kc * 128: t3 * 1024 + (kc + 1) * 128],
                        xlv[:, kc, :], start=(kc == 0), stop=False))
                    for kc in range(8)]
                fns.append(lambda e, t3=t3, h0=h0: e.matmul(
                    ps_a[:, h0:h0 + 512], qkbs[:, t3 * 128:(t3 + 1) * 128], ones5[:],
                    start=False, stop=True))
                em.op("tensor", fns, reads=["wq3", xlk, "qkbs", "ones5"],
                      writes=["ps_s2a.lo" if t3 == 0 else "ps_s2a.hi"])
                half = 0 if t3 == 0 else 512
                hk = "ps_s2a.lo" if t3 == 0 else "ps_s2a.hi"
                rps = ps_o2[t3]
                rk = "ps_oa" if t3 == 0 else "ps_ob"
                qb, qbk = (qsb, "qsb") if (2 * g + t3) % 2 == 0 else (pt[0], "pt0")
                em.op("scalar", lambda e, half=half, qb=qb: e.copy(
                    qb[:, 0:512], ps_a[:, half:half + 512]),
                    reads=[hk], writes=[qbk])
                em.op("tensor", lambda e, rps=rps, qb=qb: e.matmul(
                    rps[:], r2[:], qb[:, 0:512], start=True, stop=True),
                      reads=["r2", qbk], writes=[rk])
                em.op("vector", lambda e, g=g, half=half: e.tensor_mul(
                    tcos[:], ps_a[:, half:half + 512], cosa[:, g * 512:(g + 1) * 512]),
                    reads=[hk, "cosa", rk], writes=["tcos"])
                em.op("vector", lambda e, g=g, rps=rps: e.tensor_mul(
                    tsin[:], rps[:], sina[:, g * 512:(g + 1) * 512]),
                    reads=[rk, "sina", hk], writes=["tsin"])
                em.op("vector", lambda e, dst=dst, gc=gc: e.tensor_add(
                    dst[:, gc], tcos[:], tsin[:]),
                    reads=["tcos", "tsin"], writes=[dstk])
            fns = [
                (lambda e, kc=kc, xlv=xlv: e.matmul(
                    ps_s2[1][:, 512:1024], wq3[:, 2 * 1024 + kc * 128: 2 * 1024 + (kc + 1) * 128],
                    xlv[:, kc, :], start=(kc == 0), stop=False))
                for kc in range(8)]
            fns.append(lambda e: e.matmul(
                ps_s2[1][:, 512:1024], qkbs[:, 256:384], ones5[:], start=False, stop=True))
            em.op("tensor", fns, reads=["wq3", xlk, "qkbs", "ones5"], writes=["ps_s2b.hi"])
            em.op("scalar", lambda e: e.copy(vstg[:, 0:512], ps_s2[1][:, 512:1024]),
                  reads=["ps_s2b.hi"], writes=["pt2"])
            em.op("tensor", [
                (lambda e, k=k: e.transpose(ps_t[:, k * 128:(k + 1) * 128],
                                            vstg[:, k * 128:(k + 1) * 128], sbid[:]))
                for k in range(4)],
                reads=["pt2", "sbid"], writes=["ps_t"])
            for k in range(4):
                sbk = g * 4 + k
                em.op("vector", lambda e, k=k, sbk=sbk: e.tensor_copy(
                    Vpv[:, 0, sbk, 0:64], ps_t[:, k * 128:k * 128 + 64]),
                    reads=["ps_t"], writes=["Vp"])
                em.op("vector", lambda e, k=k, sbk=sbk: e.tensor_copy(
                    Vpv[:, 1, sbk, 0:64], ps_t[:, k * 128 + 64:k * 128 + 128]),
                    reads=["ps_t"], writes=["Vp"])
        for src, srck, off in [(qTp, "qTp", 0), (kTp, "kTp", 8)]:
            for tc in range(NTC):
                qb, qbk = (q2, "pt3") if tc % 2 == 0 else (pt[1], "pt1")
                em.op("scalar", lambda e, src=src, tc=tc, qb=qb: e.activation(
                    qb[:, 0:512], src[:, tc * TC:(tc + 1) * TC], AF.Square),
                    reads=[srck], writes=[qbk])
                em.op("tensor", lambda e, qb=qb: e.matmul(
                    ps_bc[0:1, :], onec[:], qb[:, 0:512], start=True, stop=True),
                    reads=[qbk, "onec"], writes=["ps_bc"])
                em.op("vector", lambda e, tc=tc, off=off: e.tensor_reduce(
                    nmx[:, off + tc:off + tc + 1], ps_bc[0:1, :], AX.X, ALU.max),
                    reads=["ps_bc"], writes=["nmx"])
        em.op("vector", lambda e: e.tensor_reduce(cst[:, 0:1], nmx[:, 0:8], AX.X, ALU.max),
              reads=["nmx"], writes=["cst"])
        em.op("vector", lambda e: e.tensor_reduce(cst[:, 1:2], nmx[:, 8:16], AX.X, ALU.max),
              reads=["nmx"], writes=["cst"])
        em.op("vector", lambda e: e.tensor_mul(cst[:, 2:3], cst[:, 0:1], cst[:, 1:2]),
              reads=["cst"], writes=["cst"])
        em.op("scalar", lambda e: e.activation(cst[:, 3:4], cst[:, 2:3], AF.Ln),
              reads=["cst"], writes=["cst"])
        em.op("scalar", lambda e: e.activation(cst[:, 4:5], cst[:, 3:4], AF.Exp, scale=0.5),
              reads=["cst"], writes=["cst"])
        em.op("vector", lambda e: e.tensor_scalar(
            cst[:, 5:6], cst[:, 4:5], -1.02 / 8.0, -1e-3, ALU.mult, ALU.add),
            reads=["cst"], writes=["cst"])
        em.op("tensor", lambda e: e.matmul(ps_bc[:, 0:1], oner[:], cst[:, 5:6],
                                           start=True, stop=True),
              reads=["oner", "cst"], writes=["ps_bc"])
        em.op("scalar", lambda e: e.copy(bias_p[:], ps_bc[:, 0:1]),
              reads=["ps_bc"], writes=["bias_p"])

        pvq = []
        epiq = []
        def _flush(queue, n_keep=0):
            while len(queue) > n_keep:
                eng, fn, rd, wr = queue.pop(0)
                em.op(eng, fn, reads=rd, writes=wr)
        zu = tcos
        lrs = [(onorm[0:1, 0:512], "onorm"), (lbc[0:1, 512:1024], "lbc")]
        for qc in range(NTC):
            for sbi in range(32):
                if sbi >= 2:
                    _flush(epiq, max(0, len(epiq) - 2) if sbi < 31 else 0)
                par = sbi % 2
                pss = ps_s2[par]
                ks = [f"ps_s2{'ab'[par]}.lo", f"ps_s2{'ab'[par]}.hi"]
                pb, pbk = pt[sbi % 4], f"pt{sbi % 4}"
                em.op("tensor", [
                    (lambda e, sbi=sbi, hh=hh, qc=qc, pss=pss: e.matmul(
                        pss[:, hh * TC:(hh + 1) * TC],
                        kTp[hh * 64:hh * 64 + 64, sbi * 128:(sbi + 1) * 128],
                        qTp[hh * 64:hh * 64 + 64, qc * TC:(qc + 1) * TC],
                        start=True, stop=True))
                    for hh in range(2)],
                    reads=["kTp", "qTp"], writes=ks)
                em.op("scalar", lambda e, pb=pb, pss=pss: e.activation(
                    pb[:], pss[:], AF.Exp, bias=bias_p[:, 0:1], scale=0.125),
                    reads=ks + ["bias_p"], writes=[pbk])
                pvq.append(("tensor", [
                    (lambda e, sbi=sbi, hh=hh, pb=pb: e.matmul(
                        ps_o2[hh][0:65, :], Vpv[:, hh, sbi, :],
                        pb[:, hh * TC:(hh + 1) * TC],
                        start=(sbi == 0), stop=(sbi == 31)))
                    for hh in range(2)],
                    ["Vp", pbk], ["ps_oa", "ps_ob"]))
                _flush(pvq, 3)
            _flush(pvq)
            em.op("vector", lambda e: e.tensor_copy(zu[0:64, :], ps_o2[0][0:64, :]),
                  reads=["ps_oa"], writes=["tcos"])
            em.op("vector", lambda e: e.tensor_copy(zu[64:128, :], ps_o2[1][0:64, :]),
                  reads=["ps_ob"], writes=["tcos"])
            em.op("vector", lambda e: e.tensor_copy(lrs[0][0], ps_o2[0][64:65, :]),
                  reads=["ps_oa"], writes=["onorm"])
            em.op("vector", lambda e: e.tensor_copy(lrs[1][0], ps_o2[1][64:65, :]),
                  reads=["ps_ob"], writes=["onorm"])
            for hh in range(2):
                r0 = hh * 64
                lv, lvk = lrs[hh]
                lb = lbc[r0:r0 + 64, hh * 512:hh * 512 + 512]
                pbc = ps_bc[r0:r0 + 64, :]
                tpb = (hh % 2) * 2048 + bb * 256
                em.op("scalar", lambda e, lv=lv: e.activation(lv, lv, AF.Ln),
                      reads=[lvk], writes=[lvk])
                em.op("scalar", lambda e, lv=lv: e.activation(
                    lv, lv, AF.Exp, scale=-1.0), reads=[lvk], writes=[lvk])
                epiq.append(("tensor", (lambda e, lv=lv, pbc=pbc: e.matmul(
                    pbc, oner[:, 0:64], lv, start=True, stop=True)),
                    ["oner", lvk], ["ps_bc"]))
                epiq.append(("vector", (lambda e, lb=lb, pbc=pbc: e.tensor_copy(lb, pbc)),
                    ["ps_bc"], ["lbc"]))
                epiq.append(("vector", (lambda e, r0=r0, lb=lb: e.tensor_mul(
                    onorm[r0:r0 + 64, :], zu[r0:r0 + 64, :], lb)),
                    ["tcos", "lbc"], ["onorm"]))
                for ql in range(16):
                    p0 = (ql % 2) * 64
                    ch = ql // 2
                    tcol = tpb + qc * 32
                    epiq.append(("gpsimd", (lambda e, ql=ql, p0=p0, ch=ch, tcol=tcol, r0=r0: e.tensor_copy(
                        sscrv[p0:p0 + 64, ch, tcol:tcol + 32],
                        onorm[r0:r0 + 64, :].rearrange("p (a b) -> p a b", b=16)[:, :, ql])),
                        ["onorm"], ["sscr"]))
        _flush(epiq)
    pbs = lbc
    em.dma("sync", lambda e: e.dma_start(out=pbs[0:1, :], in_=pbr), writes=["lbc"])
    for j in range(32):
        xb, xbk = xt[j % 2], f"xt{j % 2}"
        em.dma("sync", lambda e, j=j, xb=xb: e.dma_start(
            out=xb[:], in_=xc[j * 128:(j + 1) * 128, :]), writes=[xbk])
        fns = []
        for oc in range(2):
            fns += [
                (lambda e, kc=kc, j=j, oc=oc: e.matmul(
                    ps_pj[:, oc * 512:(oc + 1) * 512], sscrv[:, kc, j * 128:(j + 1) * 128],
                    pjs[:, kc * 1024 + oc * 512: kc * 1024 + (oc + 1) * 512],
                    start=(kc == 0), stop=False))
                for kc in range(8)]
            fns.append(lambda e, oc=oc: e.matmul(
                ps_pj[:, oc * 512:(oc + 1) * 512], oner[:, 0:128], pbs[0:1, oc * 512:(oc + 1) * 512],
                start=False, stop=True))
        em.op("tensor", fns, reads=["sscr", "pjslast", "oner", "lbc"],
              writes=["ps_s2a.lo", "ps_s2a.hi"])
        em.op("vector", lambda e, xb=xb: e.tensor_add(ot[:], xb[:], ps_pj[:]),
              reads=["ps_s2a.lo", "ps_s2a.hi", xbk], writes=["ot0"])
        em.dma("sync", lambda e, j=j: e.dma_start(
            out=x1[j * 128:(j + 1) * 128, :], in_=ot[:]), reads=["ot0"],
            writes=["g.x1"], slot="st_ot0")


def emit_phase_b(em, nc, es, x1, x2, qwT, kwT, vwT, owT, qkvb8, wbc, bbc, obr, id16, tag=""):
    sb = lambda n, s, d: es.enter_context(nc.sbuf_tensor(tag + n, s, d))
    ps = lambda n, s, d: es.enter_context(nc.psum_tensor(tag + n, s, d))

    xiT = sb("xiT", [128, 8 * T], BF16)
    aoT = sb("aoT", [128, 8 * T], BF16)
    sbid = sb("sbid", [128, 128], BF16)
    wstg = [sb("wstg0", [128, 1024], F32)] * 2
    qwh = sb("qwh", [128, 1024], BF16)
    kwh = sb("kwh", [128, 1024], BF16)
    vwh = sb("vwh", [128, 1024], BF16)
    qkb = sb("qkb", [128, 24], F32)
    wbcs = sb("wbcs", [128, 1024], F32)
    bbcs = sb("bbcs", [128, 1024], F32)
    obs = sb("obs", [1, 1024], F32)
    xt = [sb(f"xt{i}", [128, 1024], F32) for i in range(2)]
    ot = sb("ot0", [128, 1024], F32)
    xmt = sb("xmt", [128, 1024], F32)
    ybf = sb("ybf", [128, 1024], BF16)
    s1 = sb("s1", [128, 1], F32)
    ssq = sb("ssq", [128, 1], F32)
    st = sb("st", [128, 8], F32)
    lncb = sb("lncb", [128, 2], F32)
    sstash = sb("sstash", [128, 64], F32)
    qTh = sb("qTh", [128, 4096], BF16)
    kTh = sb("kTh", [128, 4096], BF16)
    Vh = sb("Vh", [128, 4096], BF16)
    lg = [sb(f"lg{i}", [128, 512], BF16) for i in range(2)]
    onec = sb("onec", [128, 1], BF16)
    oner = sb("oner", [1, 128], F32)
    nmx = sb("nmx", [1, 32], F32)
    cst = sb("cst", [1, 8], F32)
    bias_h = sb("bias_h", [128, 1], F32)
    pt = [sb(f"pt{i}", [128, 1024], BF16) for i in range(4)]
    linv = sb("linv", [1, 512], F32)
    lbc = sb("lbc", [128, 512], F32)

    ps_s2 = [ps("ps_s2a", [128, 1024], F32), ps("ps_s2b", [128, 1024], F32)]
    ps_o = ps("ps_o", [128, 512], F32)
    ps_l = ps("ps_l", [128, 512], F32)
    ps_t = ps("ps_t", [128, 1024], BF16)
    ps_a = ps_s2[0]
    ps_bc = ps_l
    ps_pj = ps_s2[0]

    sq = (wstg[0], "wstg0")
    lnbufs = (s1, ssq, sq, lncb)

    owTs = xiT
    xiTv = xiT[:].rearrange("p (c n) -> p c n", n=T)
    aoTv = aoT[:].rearrange("p (c n) -> p c n", n=T)
    Vhv = Vh[:].rearrange("p (s c) -> p s c", s=32)
    vstg = pt[2]
    q2 = pt[3]
    zu = wstg[0]

    em.dma("sync", lambda e: e.dma_start(out=sbid[:], in_=id16), writes=["sbid"])
    em.dma("sync", lambda e: e.dma_start(out=qkb[:], in_=qkvb8), writes=["qkb"])
    em.dma("sync", lambda e: e.dma_start(out=wbcs[:], in_=wbc), writes=["wbcs"])
    em.dma("sync", lambda e: e.dma_start(out=bbcs[:], in_=bbc), writes=["bbcs"])
    em.dma("sync", lambda e: e.dma_start(out=obs[:], in_=obr), writes=["obs"])
    em.op("vector", lambda e: e.memset(onec[:], 1.0), writes=["onec"])
    em.op("vector", lambda e: e.memset(oner[:], 1.0), writes=["oner"])
    em.op("vector", lambda e: e.memset(lncb[:, 0:1], D * 1e-5), writes=["lncb"])
    em.op("vector", lambda e: e.memset(lncb[:, 1:2], LND), writes=["lncb"])
    for j in range(NT):
        b, bk = xt[j % 2], f"xt{j % 2}"
        em.dma("sync", lambda e, j=j, b=b: e.dma_start(
            out=b[:], in_=x1[j * 128:(j + 1) * 128, :]), reads=["g.x1"], writes=[bk])
        emit_ln_stats(em, b[:], bk, st, 0, lnbufs)
        em.op("vector", lambda e, j=j: e.tensor_scalar(
            sstash[:, 2 * j:2 * j + 1], st[:, 0:1], 1.0, 1.0, ALU.mult, ALU.add),
            reads=["st"], writes=["sstash"])
        em.op("vector", lambda e, j=j: e.tensor_copy(
            sstash[:, 2 * j + 1:2 * j + 2], st[:, 1:2]),
            reads=["st"], writes=["sstash"])
        em.op("vector", lambda e, b=b: e.tensor_scalar(
            xmt[:], b[:], st[:, 0:1], st[:, 1:2], ALU.mult, ALU.add),
            reads=[bk, "st"], writes=["xmt"])
        emit_ln_stats(em, xmt[:], "xmt", st, 2, lnbufs)
        em.op("vector", lambda e: e.tensor_scalar(
            xmt[:], xmt[:], st[:, 2:3], st[:, 3:4], ALU.mult, ALU.add),
            reads=["xmt", "st"], writes=["xmt"])
        em.op("vector", lambda e: e.tensor_mul(xmt[:], xmt[:], wbcs[:]),
              reads=["xmt", "wbcs"], writes=["xmt"])
        em.op("vector", lambda e: e.tensor_add(ybf[:], xmt[:], bbcs[:]),
              reads=["xmt", "bbcs"], writes=["ybf"])
        em.op("tensor", [
            (lambda e, k=k: e.transpose(ps_t[:, k * 128:(k + 1) * 128],
                                        ybf[:, k * 128:(k + 1) * 128], sbid[:]))
            for k in range(8)],
            reads=["ybf", "sbid"], writes=["ps_t"])
        em.op("vector", lambda e, j=j: e.tensor_copy(
            xiTv[:, :, j * 128: j * 128 + 128],
            ps_t[:].rearrange("p (c n) -> p c n", n=128)),
            reads=["ps_t"], writes=["xiT"])

    for h in range(8):
        for wdram, slab, sk in [(qwT, qwh, "qwh"), (kwT, kwh, "kwh"), (vwT, vwh, "vwh")]:
            b, bk = wstg[0], "wstg0"
            em.dma("sync", lambda e, wdram=wdram, h=h, b=b: e.dma_start(
                out=b[:], in_=wdram[h * 128:(h + 1) * 128, :]), writes=[bk])
            em.op("vector", lambda e, slab=slab, b=b: e.tensor_copy(slab[:], b[:]),
                  reads=[bk], writes=[sk])
        for tc in range(NTC):
            ex = tc // 2
            nkc = 8 >> ex
            em.op("tensor", [
                (lambda e, kc=kc, tc=tc, last=(kc == nkc - 1): e.matmul(
                    ps_a[:, 0:512], qwh[:, kc * 128:(kc + 1) * 128],
                    xiTv[:, kc, tc * TC:(tc + 1) * TC], start=(kc == 0), stop=last))
                for kc in range(nkc)],
                reads=["qwh", "xiT"], writes=["ps_s2a"])
            em.op("vector", lambda e, tc=tc, h=h: e.tensor_scalar_add(
                qTh[:, tc * TC:(tc + 1) * TC], ps_a[:, 0:512], qkb[:, h:h + 1]),
                reads=["ps_s2a", "qkb"], writes=["qTh"])
            em.op("tensor", [
                (lambda e, kc=kc, tc=tc, last=(kc == nkc - 1): e.matmul(
                    ps_a[:, 512:1024], kwh[:, kc * 128:(kc + 1) * 128],
                    xiTv[:, kc, tc * TC:(tc + 1) * TC], start=(kc == 0), stop=last))
                for kc in range(nkc)],
                reads=["kwh", "xiT"], writes=["ps_s2a"])
            em.op("vector", lambda e, tc=tc, h=h: e.tensor_scalar_add(
                kTh[:, tc * TC:(tc + 1) * TC], ps_a[:, 512:1024], qkb[:, 8 + h:8 + h + 1]),
                reads=["ps_s2a", "qkb"], writes=["kTh"])
            em.op("tensor", [
                (lambda e, kc=kc, tc=tc, last=(kc == nkc - 1): e.matmul(
                    ps_s2[1][:, 0:512], vwh[:, kc * 128:(kc + 1) * 128],
                    xiTv[:, kc, tc * TC:(tc + 1) * TC], start=(kc == 0), stop=last))
                for kc in range(nkc)],
                reads=["vwh", "xiT"], writes=["ps_s2b"])
            em.op("vector", lambda e, tc=tc, h=h: e.tensor_scalar_add(
                vstg[:, 0:512], ps_s2[1][:, 0:512], qkb[:, 16 + h:16 + h + 1]),
                reads=["ps_s2b", "qkb"], writes=["pt2"])
            em.op("tensor", [
                (lambda e, k=k: e.transpose(ps_t[:, k * 128:(k + 1) * 128],
                                            vstg[:, k * 128:(k + 1) * 128], sbid[:]))
                for k in range(4)],
                reads=["pt2", "sbid"], writes=["ps_t"])
            em.op("vector", lambda e, tc=tc: e.tensor_copy(
                Vhv[:, tc * 4:tc * 4 + 4, :],
                ps_t[:, 0:512].rearrange("p (s c) -> p s c", c=128)),
                reads=["ps_t"], writes=["Vh"])
        for src, srck, off in [(qTh, "qTh", 0), (kTh, "kTh", 8)]:
            for tc in range(NTC):
                qb, qbk = (q2, "pt3") if tc % 2 == 0 else (pt[1], "pt1")
                em.op("scalar", lambda e, src=src, tc=tc, qb=qb: e.activation(
                    qb[:, 0:512], src[:, tc * TC:(tc + 1) * TC], AF.Square),
                    reads=[srck], writes=[qbk])
                em.op("tensor", lambda e, qb=qb: e.matmul(
                    ps_bc[0:1, :], onec[:], qb[:, 0:512], start=True, stop=True),
                    reads=[qbk, "onec"], writes=["ps_l"])
                em.op("vector", lambda e, tc=tc, off=off: e.tensor_reduce(
                    nmx[:, off + tc:off + tc + 1], ps_bc[0:1, :], AX.X, ALU.max),
                    reads=["ps_l"], writes=["nmx"])
        em.op("vector", lambda e: e.tensor_reduce(cst[:, 0:1], nmx[:, 0:8], AX.X, ALU.max),
              reads=["nmx"], writes=["cst"])
        em.op("vector", lambda e: e.tensor_reduce(cst[:, 1:2], nmx[:, 8:16], AX.X, ALU.max),
              reads=["nmx"], writes=["cst"])
        em.op("vector", lambda e: e.tensor_mul(cst[:, 2:3], cst[:, 0:1], cst[:, 1:2]),
              reads=["cst"], writes=["cst"])
        em.op("scalar", lambda e: e.activation(cst[:, 3:4], cst[:, 2:3], AF.Ln),
              reads=["cst"], writes=["cst"])
        em.op("scalar", lambda e: e.activation(cst[:, 4:5], cst[:, 3:4], AF.Exp, scale=0.5),
              reads=["cst"], writes=["cst"])
        em.op("vector", lambda e: e.tensor_scalar(
            cst[:, 5:6], cst[:, 4:5], -1.02 / 32.0, -1e-3, ALU.mult, ALU.add),
            reads=["cst"], writes=["cst"])
        em.op("tensor", lambda e: e.matmul(ps_bc[:, 0:1], oner[:], cst[:, 5:6],
                                           start=True, stop=True),
              reads=["oner", "cst"], writes=["ps_l"])
        em.op("scalar", lambda e: e.copy(bias_h[:], ps_bc[:, 0:1]),
              reads=["ps_l"], writes=["bias_h"])

        pvq = []
        epiq = []
        def _flush(queue, n_keep=0):
            while len(queue) > n_keep:
                eng, fn, rd, wr = queue.pop(0)
                em.op(eng, fn, reads=rd, writes=wr)
        for qc in range(NTC):
            for sp in range(0, 32, 2):
                if sp >= 2:
                    _flush(epiq, max(0, len(epiq) - 2) if sp < 30 else 0)
                pi = (sp // 2) % 2
                pss, pssk = ps_s2[pi], f"ps_s2{'ab'[pi]}"
                pb, pbk = pt[(sp // 2) % 4], f"pt{(sp // 2) % 4}"
                em.op("tensor", [
                    (lambda e, sbi=sp + d, d=d, qc=qc, pss=pss: e.matmul(
                        pss[:, d * TC:(d + 1) * TC],
                        kTh[:, sbi * 128:(sbi + 1) * 128],
                        qTh[:, qc * TC:(qc + 1) * TC], start=True, stop=True))
                    for d in range(2)],
                    reads=["kTh", "qTh"], writes=[pssk])
                em.op("scalar", lambda e, pb=pb, pss=pss: e.activation(
                    pb[:], pss[:], AF.Exp, bias=bias_h[:, 0:1], scale=1.0 / 32.0),
                    reads=[pssk, "bias_h"], writes=[pbk])
                gi = sp // 16
                if sp % 16 == 0:
                    em.op("vector", lambda e, pb=pb, gi=gi: e.tensor_add(
                        lg[gi][:], pb[:, 0:512], pb[:, 512:1024]),
                        reads=[pbk], writes=[f"lg{gi}"])
                else:
                    em.op("vector", lambda e, pb=pb, gi=gi: e.tensor_add(
                        lg[gi][:], lg[gi][:], pb[:, 0:512]),
                        reads=[pbk, f"lg{gi}"], writes=[f"lg{gi}"])
                    em.op("vector", lambda e, pb=pb, gi=gi: e.tensor_add(
                        lg[gi][:], lg[gi][:], pb[:, 512:1024]),
                        reads=[pbk, f"lg{gi}"], writes=[f"lg{gi}"])
                fns = [
                    (lambda e, sbi=sp + d, d=d, pb=pb: e.matmul(
                        ps_o[:], Vhv[:, sbi, :], pb[:, d * TC:(d + 1) * TC],
                        start=(sbi == 0), stop=(sbi == 31)))
                    for d in range(2)]
                pvq.append(("tensor", fns, ["Vh", pbk], ["ps_o"]))
                while len(pvq) > 1:
                    eng, fn, rd, wr = pvq.pop(0)
                    em.op(eng, fn, reads=rd, writes=wr)
            _flush(pvq)
            em.op("vector", lambda e: e.tensor_add(lg[0][:], lg[0][:], lg[1][:]),
                  reads=["lg0", "lg1"], writes=["lg0"])
            em.op("tensor", lambda e: e.matmul(
                ps_l[0:1, :], onec[:], lg[0][:], start=True, stop=True),
                reads=["lg0", "onec"], writes=["ps_l"])
            em.op("vector", lambda e: e.tensor_copy(zu[:, 0:512], ps_o[:]),
                  reads=["ps_o"], writes=["wstg0"])
            em.op("scalar", lambda e: e.activation(linv[:], ps_l[0:1, :], AF.Ln),
                  reads=["ps_l"], writes=["linv"])
            em.op("scalar", lambda e: e.activation(linv[:], linv[:], AF.Exp, scale=-1.0),
                  reads=["linv"], writes=["linv"])
            em.op("tensor", lambda e: e.matmul(
                ps_bc[:, :], oner[:], linv[:], start=True, stop=True),
                reads=["oner", "linv"], writes=["ps_l"])
            em.op("vector", lambda e: e.tensor_copy(lbc[:], ps_bc[:, :]),
                  reads=["ps_l"], writes=["lbc"])
            epiq.append(("vector", (lambda e, h=h, qc=qc: e.tensor_mul(
                aoTv[:, h, qc * TC:(qc + 1) * TC], zu[:, 0:512], lbc[:])),
                ["wstg0", "lbc"], ["aoT"]))
        _flush(epiq)
    for kc in range(8):
        b, bk = wstg[0], "wstg0"
        em.dma("sync", lambda e, kc=kc, b=b: e.dma_start(
            out=b[:], in_=owT[kc * 128:(kc + 1) * 128, :]), writes=[bk])
        em.op("vector", lambda e, kc=kc, b=b: e.tensor_copy(
            owTs[:, kc * 1024:(kc + 1) * 1024], b[:]), reads=[bk], writes=["xiT"])

    for j in range(NT):
        ex = j // 8
        m = 1024 >> ex
        nkc = 8 >> ex
        b, bk = xt[j % 2], f"xt{j % 2}"
        em.dma("sync", lambda e, j=j, b=b: e.dma_start(
            out=b[:], in_=x1[j * 128:(j + 1) * 128, :]), reads=["g.x1"], writes=[bk])
        em.op("vector", lambda e, j=j, b=b: e.tensor_scalar(
            ot[:], b[:], sstash[:, 2 * j:2 * j + 1], sstash[:, 2 * j + 1:2 * j + 2],
            ALU.mult, ALU.add),
            reads=[bk, "sstash"], writes=["ot0"])
        fns = []
        for oc in range((m + 511) // 512):
            ow = min(512, m - oc * 512)
            fns += [
                (lambda e, kc=kc, j=j, oc=oc, ow=ow: e.matmul(
                    ps_pj[:, oc * 512:oc * 512 + ow], aoTv[:, kc, j * 128:(j + 1) * 128],
                    owTs[:, kc * 1024 + oc * 512: kc * 1024 + oc * 512 + ow],
                    start=(kc == 0), stop=False))
                for kc in range(nkc)]
            fns.append(lambda e, j=j, oc=oc, ow=ow: e.matmul(
                ps_pj[:, oc * 512:oc * 512 + ow], oner[:, 0:128], obs[:, oc * 512:oc * 512 + ow],
                start=False, stop=True))
        em.op("tensor", fns, reads=["aoT", "xiT", "oner", "obs"], writes=["ps_s2a"])
        em.op("vector", lambda e, m=m: e.tensor_add(
            ot[:, 0:m], ot[:, 0:m], ps_pj[:, 0:m]),
            reads=["ps_s2a", "ot0"], writes=["ot0"])
        em.dma("sync", lambda e, j=j: e.dma_start(
            out=x2[j * 128:(j + 1) * 128, :], in_=ot[:]), reads=["ot0"],
            writes=["g.x2"], slot="st_ot0")


def _mk_qkv_evac(qTp, kTp, vstg, Vpv, ps_t, sbid, TCc):
    def _emit(em, job):
        kind, tc, psx, half, pxk = job
        if kind == "q":
            em.op("scalar", lambda e: e.copy(qTp[:, tc * TCc:(tc + 1) * TCc], psx[:, half:half + 512]),
                  reads=[pxk], writes=["qTp"])
        elif kind == "k":
            em.op("scalar", lambda e: e.copy(kTp[:, tc * TCc:(tc + 1) * TCc], psx[:, half:half + 512]),
                  reads=[pxk], writes=["kTp"])
        else:
            em.op("scalar", lambda e: e.copy(vstg[:, 0:512], psx[:, half:half + 512]),
                  reads=[pxk], writes=["pt2"])
            em.op("tensor", [
                (lambda e, k=k: e.transpose(ps_t[:, k * 128:(k + 1) * 128],
                                            vstg[:, k * 128:(k + 1) * 128], sbid[:]))
                for k in range(4)],
                reads=["pt2", "sbid"], writes=["ps_t"])
            for k in range(4):
                sbk = tc * 4 + k
                em.op("vector", lambda e, k=k, sbk=sbk: e.tensor_copy(
                    Vpv[:, 0, sbk, 0:64], ps_t[:, k * 128:k * 128 + 64]),
                    reads=["ps_t"], writes=["Vp"])
                em.op("vector", lambda e, k=k, sbk=sbk: e.tensor_copy(
                    Vpv[:, 1, sbk, 0:64], ps_t[:, k * 128 + 64:k * 128 + 128]),
                    reads=["ps_t"], writes=["Vp"])
    return _emit


def emit_phase_c(em, nc, es, x2, x3, wq, wk, wv, wo, bob, id16, tag=""):
    sb = lambda n, s, d: es.enter_context(nc.sbuf_tensor(tag + n, s, d))
    ps = lambda n, s, d: es.enter_context(nc.psum_tensor(tag + n, s, d))

    xiT = sb("xiT", [128, 8 * T], BF16)
    zT = sb("zT", [128, 8 * T], BF16)
    wos = sb("wos", [128, 8192], BF16)
    bo = sb("bo", [1, 1024], F32)
    sbid = sb("sbid", [128, 128], BF16)
    wstg = [sb("wstg0", [128, 1024], F32)] * 2
    wqp = sb("wqp", [128, 1024], BF16)
    wkp = sb("wkp", [128, 1024], BF16)
    wvp = sb("wvp", [128, 1024], BF16)
    xt = [sb(f"xt{i}", [128, 1024], F32) for i in range(2)]
    ot = [sb("ot0", [128, 1024], F32)] * 2
    ybf2 = [sb(f"ybf{i}", [128, 1024], BF16) for i in range(2)]
    s12 = [sb(f"s1{i}", [128, 1], F32) for i in range(2)]
    ssq2 = [sb(f"ssq{i}", [128, 1], F32) for i in range(2)]
    st2 = [sb(f"st{i}", [128, 8], F32) for i in range(2)]
    qTp = sb("qTp", [128, 4096], BF16)
    kTp = sb("kTp", [128, 4096], BF16)
    Vp = sb("Vp", [128, 2 * 32 * 65], BF16)
    onec = sb("onec", [128, 1], BF16)
    oner = sb("oner", [1, 128], F32)
    nmx = sb("nmx", [1, 32], F32)
    cst = sb("cst", [1, 8], F32)
    bias_p = sb("bias_p", [128, 1], F32)
    pt = [sb(f"pt{i}", [128, 1024], BF16) for i in range(4)]

    ps_s2 = [ps("ps_s2a", [128, 1024], F32), ps("ps_s2b", [128, 1024], F32)]
    ps_o2 = [ps("ps_oa", [128, 512], F32), ps("ps_ob", [128, 512], F32)]
    ps_bc = ps("ps_bc", [128, 512], F32)
    ps_t = ps("ps_t", [128, 1024], BF16)
    ps_pj = ps_s2[0]

    lncb = sb("lncb", [128, 2], F32)
    lnsets = [
        ((s12[0], "s10"), (ssq2[0], "ssq0"), (wstg[0], "wstg0"), (st2[0], "st0"), lncb),
        ((s12[1], "s11"), (ssq2[1], "ssq1"), (ot[0], "ot0"), (st2[1], "st1"), lncb),
    ]

    xiTv = xiT[:].rearrange("p (c n) -> p c n", n=T)
    zTv = zT[:].rearrange("p (c n) -> p c n", n=T)
    Vpv = Vp[:].rearrange("p (h s c) -> p h s c", h=2, s=32)
    vstg = pt[2]
    q2 = pt[3]
    lbc = wstg[0]
    linv2 = [(xt[0], "xt0"), (xt[1], "xt1")]
    _emit_qkv_evac = _mk_qkv_evac(qTp, kTp, vstg, Vpv, ps_t, sbid, TC)

    em.dma("sync", lambda e: e.dma_start(out=sbid[:], in_=id16), writes=["sbid"])
    em.dma("sync", lambda e: e.dma_start(out=bo[:], in_=bob[0:1, :]), writes=["bo"])
    em.op("vector", lambda e: e.memset(onec[:], 1.0), writes=["onec"])
    em.op("vector", lambda e: e.memset(lncb[:, 0:1], D * 1e-5), writes=["lncb"])
    em.op("vector", lambda e: e.memset(lncb[:, 1:2], LND), writes=["lncb"])
    em.op("vector", lambda e: e.memset(oner[:], 1.0), writes=["oner"])
    em.op("vector", lambda e: e.memset(Vpv[:, :, :, 64:65], 1.0), writes=["Vp"])
    for kc in range(8):
        b = wstg[0]
        bk = "wstg0"
        em.dma("sync", lambda e, kc=kc, b=b: e.dma_start(
            out=b[:], in_=wo[kc * 128:(kc + 1) * 128, :]), writes=[bk])
        em.op("vector", lambda e, kc=kc, b=b: e.tensor_copy(
            wos[:, kc * 1024:(kc + 1) * 1024], b[:]), reads=[bk], writes=["wos"])

    for j in range(NT):
        b = xt[j % 2]
        bk = f"xt{j % 2}"
        em.dma("sync", lambda e, j=j, b=b: e.dma_start(
            out=b[:], in_=x2[j * 128:(j + 1) * 128, :]), reads=["g.x2"], writes=[bk])
        ybf, ybfk = ybf2[j % 2], f"ybf{j % 2}"
        emit_ln_tile(em, b, bk, ybf, ybfk, lnsets[j % 2])
        em.op("tensor", [
            (lambda e, k=k, ybf=ybf: e.transpose(ps_t[:, k * 128:(k + 1) * 128],
                                        ybf[:, k * 128:(k + 1) * 128], sbid[:]))
            for k in range(8)],
            reads=[ybfk, "sbid"], writes=["ps_t"])
        em.op("vector", lambda e, j=j: e.tensor_copy(
            xiTv[:, :, j * 128: j * 128 + 128],
            ps_t[:].rearrange("p (c n) -> p c n", n=128)),
            reads=["ps_t"], writes=["xiT"])

    for hp in range(8):
        for wdram, slab, sk in [(wq, wqp, "wqp"), (wk, wkp, "wkp"), (wv, wvp, "wvp")]:
            b = wstg[0]
            bk = "wstg0"
            em.dma("sync", lambda e, wdram=wdram, hp=hp, b=b: e.dma_start(
                out=b[:], in_=wdram[hp * 128:(hp + 1) * 128, :]), writes=[bk])
            em.op("vector", lambda e, slab=slab, b=b: e.tensor_copy(slab[:], b[:]),
                  reads=[bk], writes=[sk])
        qkv_jobs = []
        for tc in range(NTC):
            qkv_jobs.append(("q", tc, wqp, "wqp"))
            qkv_jobs.append(("k", tc, wkp, "wkp"))
            qkv_jobs.append(("v", tc, wvp, "wvp"))
        pending = None
        for gi, (kind, tc, wslab, wkey) in enumerate(qkv_jobs):
            psx = ps_s2[(gi // 2) % 2]
            half = (gi % 2) * 512
            pxk = f"ps_s2{'ab'[(gi // 2) % 2]}"
            em.op("tensor", [
                (lambda e, kc=kc, tc=tc, wslab=wslab, psx=psx, half=half: e.matmul(
                    psx[:, half:half + 512], wslab[:, kc * 128:(kc + 1) * 128],
                    xiTv[:, kc, tc * TC:(tc + 1) * TC], start=(kc == 0), stop=(kc == 7)))
                for kc in range(8)],
                reads=[wkey, "xiT"], writes=[pxk])
            if pending is not None:
                _emit_qkv_evac(em, pending)
            pending = (kind, tc, psx, half, pxk)
        _emit_qkv_evac(em, pending)
        for src, srck, off in [(qTp, "qTp", 0), (kTp, "kTp", 8)]:
            for tc in range(NTC):
                qb, qbk = (q2, "pt3") if tc % 2 == 0 else (pt[1], "pt1")
                em.op("scalar", lambda e, src=src, tc=tc, qb=qb: e.activation(
                    qb[:, 0:512], src[:, tc * TC:(tc + 1) * TC], AF.Square),
                    reads=[srck], writes=[qbk])
                em.op("tensor", lambda e, qb=qb: e.matmul(
                    ps_bc[0:1, :], onec[:], qb[:, 0:512],
                    start=True, stop=True), reads=[qbk, "onec"], writes=["ps_bc"])
                em.op("vector", lambda e, tc=tc, off=off: e.tensor_reduce(
                    nmx[:, off + tc:off + tc + 1], ps_bc[0:1, :], AX.X, ALU.max),
                    reads=["ps_bc"], writes=["nmx"])
        em.op("vector", lambda e: e.tensor_reduce(cst[:, 0:1], nmx[:, 0:8], AX.X, ALU.max),
              reads=["nmx"], writes=["cst"])
        em.op("vector", lambda e: e.tensor_reduce(cst[:, 1:2], nmx[:, 8:16], AX.X, ALU.max),
              reads=["nmx"], writes=["cst"])
        em.op("vector", lambda e: e.tensor_mul(cst[:, 2:3], cst[:, 0:1], cst[:, 1:2]),
              reads=["cst"], writes=["cst"])
        em.op("scalar", lambda e: e.activation(cst[:, 3:4], cst[:, 2:3], AF.Ln),
              reads=["cst"], writes=["cst"])
        em.op("scalar", lambda e: e.activation(cst[:, 4:5], cst[:, 3:4], AF.Exp, scale=0.5),
              reads=["cst"], writes=["cst"])
        em.op("vector", lambda e: e.tensor_scalar(
            cst[:, 5:6], cst[:, 4:5], -1.02 / 8.0, -1e-3, ALU.mult, ALU.add),
            reads=["cst"], writes=["cst"])
        em.op("tensor", lambda e: e.matmul(ps_bc[:, 0:1], oner[:], cst[:, 5:6],
                                           start=True, stop=True),
              reads=["oner", "cst"], writes=["ps_bc"])
        em.op("scalar", lambda e: e.copy(bias_p[:], ps_bc[:, 0:1]),
              reads=["ps_bc"], writes=["bias_p"])

        pvq = []
        epiq = []
        def _flush(queue, n_keep=0):
            while len(queue) > n_keep:
                eng, fn, rd, wr = queue.pop(0)
                em.op(eng, fn, reads=rd, writes=wr)
        for qc in range(NTC):
            nsb = 4 * qc + 4
            for sbi in range(nsb):
                if sbi >= 2:
                    _flush(epiq, max(0, len(epiq) - 1) if sbi < nsb - 1 else 0)

                par = sbi % 2
                pss, pssk = ps_s2[par], f"ps_s2{'ab'[par]}"
                pb, pbk = pt[sbi % 4], f"pt{sbi % 4}"
                em.op("tensor", [
                    (lambda e, sbi=sbi, hh=hh, qc=qc, pss=pss: e.matmul(
                        pss[:, hh * TC:(hh + 1) * TC],
                        kTp[hh * 64:hh * 64 + 64, sbi * 128:(sbi + 1) * 128],
                        qTp[hh * 64:hh * 64 + 64, qc * TC:(qc + 1) * TC],
                        start=True, stop=True))
                    for hh in range(2)],
                    reads=["kTp", "qTp"], writes=[pssk])
                em.op("scalar", lambda e, pb=pb, pss=pss: e.activation(
                    pb[:], pss[:], AF.Exp, bias=bias_p[:, 0:1], scale=0.125),
                    reads=[pssk, "bias_p"], writes=[pbk])
                if sbi >= 4 * qc:
                    em.op("gpsimd", lambda e, pb=pb, qc=qc, sbi=sbi: e.affine_select(
                        pb[:].rearrange("p (h n) -> p h n", h=2),
                        pb[:].rearrange("p (h n) -> p h n", h=2),
                        [[0, 2], [1, TC]], ALU.is_ge, 0.0,
                        base=qc * TC - sbi * 128, channel_multiplier=-1),
                        reads=[pbk], writes=[pbk])
                pvq.append(("tensor", [
                    (lambda e, sbi=sbi, hh=hh, nsb=nsb, pb=pb: e.matmul(
                        ps_o2[hh][0:65, :], Vpv[:, hh, sbi, :],
                        pb[:, hh * TC:(hh + 1) * TC],
                        start=(sbi == 0), stop=(sbi == nsb - 1)))
                    for hh in range(2)],
                    ["Vp", pbk], ["ps_oa", "ps_ob"]))
                _flush(pvq, 3)
            _flush(pvq)
            zu = ot[0]
            em.op("vector", lambda e: e.tensor_copy(zu[0:64, 0:512], ps_o2[0][0:64, :]),
                  reads=["ps_oa"], writes=["ot0"])
            em.op("vector", lambda e: e.tensor_copy(zu[64:128, 0:512], ps_o2[1][0:64, :]),
                  reads=["ps_ob"], writes=["ot0"])
            em.op("vector", lambda e: e.tensor_copy(xt[0][0:1, 0:512], ps_o2[0][64:65, :]),
                  reads=["ps_oa"], writes=["xt0"])
            em.op("vector", lambda e: e.tensor_copy(xt[1][0:1, 0:512], ps_o2[1][64:65, :]),
                  reads=["ps_ob"], writes=["xt1"])
            for hh in range(2):
                r0 = hh * 64
                lv, lvk = linv2[hh]
                lb = lbc[r0:r0 + 64, hh * 512:hh * 512 + 512]
                pbc = ps_bc[r0:r0 + 64, :]
                em.op("scalar", lambda e, lv=lv: e.activation(
                    lv[0:1, 0:512], lv[0:1, 0:512], AF.Ln),
                    reads=[lvk], writes=[lvk])
                em.op("scalar", lambda e, lv=lv: e.activation(
                    lv[0:1, 0:512], lv[0:1, 0:512], AF.Exp, scale=-1.0),
                    reads=[lvk], writes=[lvk])
                epiq.append(("tensor", (lambda e, lv=lv, pbc=pbc: e.matmul(
                    pbc, oner[:, 0:64], lv[0:1, 0:512],
                    start=True, stop=True)),
                    ["oner", lvk], ["ps_bc"]))
                epiq.append(("vector", (lambda e, lb=lb, pbc=pbc: e.tensor_copy(
                    lb, pbc)),
                    ["ps_bc"], ["wstg0"]))
                epiq.append(("vector", (lambda e, hp=hp, r0=r0, qc=qc, lb=lb: e.tensor_mul(
                    zTv[r0:r0 + 64, hp, qc * TC:(qc + 1) * TC],
                    zu[r0:r0 + 64, 0:512], lb)),
                    ["ot0", "wstg0"], ["zT"]))
        _flush(epiq)

    for j in range(NT):
        b = xt[j % 2]
        bk = f"xt{j % 2}"
        ob = ot[0]
        obk = "ot0"
        em.dma("sync", lambda e, j=j, b=b: e.dma_start(
            out=b[:], in_=x2[j * 128:(j + 1) * 128, :]), reads=["g.x2"], writes=[bk])
        fns = [
            (lambda e, kc=kc, j=j, oc=oc: e.matmul(
                ps_pj[:, oc * 512:(oc + 1) * 512], zTv[:, kc, j * 128:(j + 1) * 128],
                wos[:, kc * 1024 + oc * 512: kc * 1024 + (oc + 1) * 512],
                start=(kc == 0), stop=False))
            for oc in range(2) for kc in range(8)]
        fns.append(lambda e: e.matmul(ps_pj[:, 0:512], oner[:, 0:128], bo[:, 0:512],
                                      start=False, stop=False))
        fns.append(lambda e: e.matmul(ps_pj[:, 512:1024], oner[:, 0:128], bo[:, 512:1024],
                                      start=False, stop=True))
        em.op("tensor", fns, reads=["zT", "wos", "oner", "bo"], writes=["ps_s2a"])
        em.op("vector", lambda e, ob=ob, b=b: e.tensor_add(ob[:], b[:], ps_pj[:]),
              reads=["ps_s2a", bk], writes=[obk])
        em.dma("sync", lambda e, j=j, ob=ob: e.dma_start(
            out=x3[j * 128:(j + 1) * 128, :], in_=ob[:]), reads=[obk],
            writes=["g.x3"], slot="st_" + obk)


def emit_phase_d(em, nc, es, x3, out, wrT, wkT, wvT, mu8, id16, id32, tag=""):
    sb = lambda n, s, d: es.enter_context(nc.sbuf_tensor(tag + n, s, d))
    ps = lambda n, s, d: es.enter_context(nc.psum_tensor(tag + n, s, d))

    yT = sb("yT", [128, NC * (T + 1)], BF16)
    w_r = sb("w_r", [128, 8192], BF16)
    w_k = sb("w_k", [128, 8192], BF16)
    w_v = sb("w_v", [128, 8192], BF16)
    sbid = sb("sbid", [128, 128], BF16)
    sbid32 = sb("sbid32", [128, 128], F32)
    mu = sb("mu", [128, 16], F32)
    negmu = sb("negmu", [128, 16], F32)
    wf = [sb(f"wf{i}", [128, 1024], F32) for i in range(2)]
    xt = [sb(f"xt{i}", [128, 1024], F32) for i in range(2)]
    ot = [sb(f"ot{i}", [128, 1024], F32) for i in range(2)]
    xr = xt
    sq = wf[0]
    s1 = sb("s1", [128, 1], F32)
    ssq = sb("ssq", [128, 1], F32)
    st = sb("st", [128, 8], F32)
    ybf = sb("ybf", [128, 1024], BF16)
    t1r = sb("t1r", [128, 512], F32)
    t1k = sb("t1k", [128, 512], F32)
    t2 = sb("t2", [128, 512], F32)
    xrT = sb("xrT", [128, 4096], BF16)
    xkT = sb("xkT", [128, 4096], BF16)
    rT = sb("rT", [128, 4096], F32)
    kaT = sb("kaT", [128, 4096], BF16)
    outT = rT

    ps_t = ps("ps_t", [128, 1024], BF16)
    ps_r2 = [ps("ps_r", [128, 512], F32), ps("ps_rb", [128, 512], F32)]
    ps_k = ps("ps_k", [128, 512], F32)
    ps_v2 = [ps("ps_v", [128, 512], F32), ps("ps_vb", [128, 512], F32)]
    ps_tb = ps("ps_tb", [128, 1024], F32)

    yTv = yT[:].rearrange("p (c n) -> p c n", n=T + 1)

    em.dma("sync", lambda e: e.dma_start(out=sbid[:], in_=id16), writes=["sbid"])
    em.dma("sync", lambda e: e.dma_start(out=sbid32[:], in_=id32), writes=["sbid32"])
    em.dma("sync", lambda e: e.dma_start(out=mu[:], in_=mu8), writes=["mu"])
    em.op("vector", lambda e: e.tensor_scalar(negmu[:], mu[:], -1.0, 1.0, ALU.mult, ALU.add),
          reads=["mu"], writes=["negmu"])
    for wi, (slab, wd) in enumerate([(w_r, wrT), (w_k, wkT), (w_v, wvT)]):
        sn = f"w{wi}"
        for kc in range(8):
            b = wf[kc % 2]
            bk = f"wf{kc % 2}"
            em.dma("sync", lambda e, wd=wd, kc=kc, b=b: e.dma_start(
                out=b[:], in_=wd[kc * 128:(kc + 1) * 128, :]), writes=[bk])
            em.op("vector", lambda e, slab=slab, kc=kc, b=b: e.tensor_copy(
                slab[:, kc * 1024:(kc + 1) * 1024], b[:]), reads=[bk], writes=[sn + "s"])
    em.op("vector", lambda e: e.memset(yTv[:, :, 0:1], 0.0), writes=["yT"])

    for j in range(NT):
        b = xt[j % 2]
        bk = f"xt{j % 2}"
        em.dma("sync", lambda e, j=j, b=b: e.dma_start(
            out=b[:], in_=x3[j * 128:(j + 1) * 128, :]), reads=["g.x3"], writes=[bk])
        em.op("vector", lambda e, b=b: e.tensor_reduce(s1[:], b[:], AX.X, ALU.add),
              reads=[bk], writes=["s1"])
        em.op("scalar", lambda e, b=b: e.activation(sq[:], b[:], AF.Square, accum_out=ssq[:]),
              reads=[bk], writes=["wf0", "ssq"])
        em.op("vector", lambda e: e.tensor_scalar_mul(st[:, 0:1], s1[:], 1.0 / D),
              reads=["s1"], writes=["st"])
        em.op("vector", lambda e: e.tensor_scalar_mul(st[:, 1:2], ssq[:], 1.0 / D),
              reads=["ssq"], writes=["st"])
        em.op("vector", lambda e: e.tensor_mul(st[:, 2:3], st[:, 0:1], st[:, 0:1]),
              reads=["st"], writes=["st"])
        em.op("vector", lambda e: e.tensor_sub(st[:, 2:3], st[:, 1:2], st[:, 2:3]),
              reads=["st"], writes=["st"])
        em.op("vector", lambda e: e.tensor_scalar_add(st[:, 2:3], st[:, 2:3], 1e-5),
              reads=["st"], writes=["st"])
        em.op("scalar", lambda e: e.activation(st[:, 3:4], st[:, 2:3], AF.Ln),
              reads=["st"], writes=["st"])
        em.op("scalar", lambda e: e.activation(st[:, 3:4], st[:, 3:4], AF.Exp, scale=-0.5),
              reads=["st"], writes=["st"])
        em.op("vector", lambda e: e.scalar_tensor_tensor(
            st[:, 4:5], st[:, 0:1], -1.0, st[:, 3:4], ALU.mult, ALU.mult),
            reads=["st"], writes=["st"])
        em.op("vector", lambda e, b=b: e.tensor_scalar(
            ybf[:], b[:], st[:, 3:4], st[:, 4:5], ALU.mult, ALU.add),
            reads=[bk, "st"], writes=["ybf"])
        em.op("tensor", [
            (lambda e, k=k: e.transpose(ps_t[:, k * 128:(k + 1) * 128],
                                        ybf[:, k * 128:(k + 1) * 128], sbid[:]))
            for k in range(8)],
            reads=["ybf", "sbid"], writes=["ps_t"])
        em.op("vector", lambda e, j=j: e.tensor_copy(
            yTv[:, :, 1 + j * 128: 1 + j * 128 + 128],
            ps_t[:].rearrange("p (c n) -> p c n", n=128)),
            reads=["ps_t"], writes=["yT"])

    def _emit_prep(tc):
        for kc in range(NC):
            ysh = yTv[:, kc, tc * TC: tc * TC + TC]
            yun = yTv[:, kc, 1 + tc * TC: 1 + tc * TC + TC]
            em.op("vector", lambda e, ysh=ysh, kc=kc: e.tensor_scalar_mul(
                t1r[:], ysh, negmu[:, kc:kc + 1]), reads=["yT", "negmu"], writes=["t1r"])
            em.op("vector", lambda e, yun=yun, kc=kc: e.scalar_tensor_tensor(
                xrT[:, kc * TC:(kc + 1) * TC], yun, mu[:, kc:kc + 1], t1r[:], ALU.mult, ALU.add),
                reads=["yT", "mu", "t1r"], writes=["xrT"])
            em.op("vector", lambda e, ysh=ysh, kc=kc: e.tensor_scalar_mul(
                t1k[:], ysh, negmu[:, 8 + kc:8 + kc + 1]), reads=["yT", "negmu"], writes=["t1k"])
            em.op("vector", lambda e, yun=yun, kc=kc: e.scalar_tensor_tensor(
                xkT[:, kc * TC:(kc + 1) * TC], yun, mu[:, 8 + kc:8 + kc + 1], t1k[:], ALU.mult, ALU.add),
                reads=["yT", "mu", "t1k"], writes=["xkT"])

    _emit_prep(0)
    for tc in range(NTC):
        for mc in range(8):
            pr, prk = ps_r2[mc % 2], ("ps_r" if mc % 2 == 0 else "ps_rb")
            em.op("tensor", [
                (lambda e, kc=kc, mc=mc, pr=pr: e.matmul(
                    pr[:], w_r[:, kc * 1024 + mc * 128: kc * 1024 + (mc + 1) * 128],
                    xrT[:, kc * TC:(kc + 1) * TC], start=(kc == 0), stop=(kc == 7)))
                for kc in range(8)],
                reads=["w0s", "xrT"], writes=[prk])
            em.op("scalar", lambda e, mc=mc, pr=pr: e.activation(
                rT[:, mc * TC:(mc + 1) * TC], pr[:], AF.Sigmoid),
                reads=[prk], writes=["rT"])
            em.op("tensor", [
                (lambda e, kc=kc, mc=mc: e.matmul(
                    ps_k[:], w_k[:, kc * 1024 + mc * 128: kc * 1024 + (mc + 1) * 128],
                    xkT[:, kc * TC:(kc + 1) * TC], start=(kc == 0), stop=(kc == 7)))
                for kc in range(8)],
                reads=["w1s", "xkT"], writes=["ps_k"])
            em.op("vector", lambda e: e.tensor_scalar_max(t2[:], ps_k[:], 0.0),
                  reads=["ps_k"], writes=["t2"])
            em.op("scalar", lambda e, mc=mc: e.activation(
                kaT[:, mc * TC:(mc + 1) * TC], t2[:], AF.Square),
                reads=["t2"], writes=["kaT"])
        for mc in range(8):
            pv, pvk = ps_v2[mc % 2], ("ps_v" if mc % 2 == 0 else "ps_vb")
            em.op("tensor", [
                (lambda e, kc=kc, mc=mc, pv=pv: e.matmul(
                    pv[:], w_v[:, kc * 1024 + mc * 128: kc * 1024 + (mc + 1) * 128],
                    kaT[:, kc * TC:(kc + 1) * TC], start=(kc == 0), stop=(kc == 7)))
                for kc in range(8)],
                reads=["w2s", "kaT"], writes=[pvk])
            em.op("vector", lambda e, mc=mc, pv=pv: e.tensor_mul(
                outT[:, mc * TC:(mc + 1) * TC], rT[:, mc * TC:(mc + 1) * TC], pv[:]),
                reads=["rT", pvk], writes=["rT"])
        if tc + 1 < NTC:
            _emit_prep(tc + 1)
        for tb in range(4):
            j = tc * 4 + tb
            b = xr[tb % 2]
            bk = f"xt{tb % 2}"
            ob = ot[tb % 2]
            obk = f"ot{tb % 2}"
            em.dma("sync", lambda e, j=j, b=b: e.dma_start(
                out=b[:], in_=x3[j * 128:(j + 1) * 128, :]), reads=["g.x3"], writes=[bk])
            em.op("tensor", [
                (lambda e, mc=mc, tb=tb: e.transpose(
                    ps_tb[:, mc * 128:(mc + 1) * 128],
                    outT[:, mc * TC + tb * 128: mc * TC + tb * 128 + 128], sbid32[:]))
                for mc in range(8)],
                reads=["rT", "sbid32"], writes=["ps_tb"])
            em.op("vector", lambda e, b=b, ob=ob: e.tensor_add(ob[:], b[:], ps_tb[:]),
                  reads=[bk, "ps_tb"], writes=[obk])
            em.dma("sync", lambda e, j=j, ob=ob: e.dma_start(
                out=out[j * 128:(j + 1) * 128, :], in_=ob[:]), reads=[obk], writes=["outd"],
                slot="st_" + obk)



def _build_fused():
    nc = bass.Bass("TRN2", target_bir_lowering=False)
    g = {}
    g["xall"] = nc.dram_tensor("xall", [B * T, D], F32, kind="ExternalInput").ap()
    g["xc"] = nc.dram_tensor("xc", [T, D], F32, kind="ExternalInput").ap()
    g["wqkvT"] = nc.dram_tensor("wqkvT", [3 * 128, 1024], F32, kind="ExternalInput").ap()
    g["qkvb"] = nc.dram_tensor("qkvb", [1, 384], F32, kind="ExternalInput").ap()
    g["projT"] = nc.dram_tensor("projT", [D, D], F32, kind="ExternalInput").ap()
    g["pbr"] = nc.dram_tensor("pbr", [1, D], F32, kind="ExternalInput").ap()
    g["cosd"] = nc.dram_tensor("cosd", [128, T], BF16, kind="ExternalInput").ap()
    g["sind"] = nc.dram_tensor("sind", [128, T], BF16, kind="ExternalInput").ap()
    g["r2d"] = nc.dram_tensor("r2d", [128, 128], F32, kind="ExternalInput").ap()
    g["id16"] = nc.dram_tensor("id16", [128, 128], BF16, kind="ExternalInput").ap()
    g["id32"] = nc.dram_tensor("id32", [128, 128], F32, kind="ExternalInput").ap()
    for nm in ["qwT", "kwT", "vwT", "owT"]:
        g[nm] = nc.dram_tensor(nm, [D, D], F32, kind="ExternalInput").ap()
    g["qkvb8"] = nc.dram_tensor("qkvb8", [128, 24], F32, kind="ExternalInput").ap()
    g["wbc"] = nc.dram_tensor("wbc", [128, D], F32, kind="ExternalInput").ap()
    g["bbc"] = nc.dram_tensor("bbc", [128, D], F32, kind="ExternalInput").ap()
    g["obr"] = nc.dram_tensor("obr", [1, D], F32, kind="ExternalInput").ap()
    for nm in ["wq", "wk", "wv", "wo"]:
        g[nm] = nc.dram_tensor(nm, [D, D], F32, kind="ExternalInput").ap()
    g["bob"] = nc.dram_tensor("bob", [128, D], F32, kind="ExternalInput").ap()
    for nm in ["wrT", "wkT2", "wvT2"]:
        g[nm] = nc.dram_tensor(nm, [D, D], F32, kind="ExternalInput").ap()
    g["mu8"] = nc.dram_tensor("mu8", [128, 16], F32, kind="ExternalInput").ap()
    g["y"] = nc.dram_tensor("y", [T, D], F32, kind="ExternalOutput").ap()
    x1i = nc.dram_tensor("x1i", [T, D], F32).ap()
    x2i = nc.dram_tensor("x2i", [T, D], F32).ap()
    x3i = nc.dram_tensor("x3i", [T, D], F32).ap()

    es = ExitStack()
    with es:
        em = Em(nc, es)
        em.tagk = "a."
        with ExitStack() as esp:
            emit_phase_a(em, nc, esp, g["xall"], g["xc"], x1i, g["wqkvT"], g["qkvb"],
                         g["projT"], g["pbr"], g["cosd"], g["sind"], g["r2d"],
                         g["id16"], tag="a_")
        em.phase_barrier()
        em.tagk = "b."
        with ExitStack() as esp:
            emit_phase_b(em, nc, esp, x1i, x2i, g["qwT"], g["kwT"], g["vwT"], g["owT"],
                         g["qkvb8"], g["wbc"], g["bbc"], g["obr"], g["id16"], tag="b_")
        em.phase_barrier()
        em.tagk = "c."
        with ExitStack() as esp:
            emit_phase_c(em, nc, esp, x2i, x3i, g["wq"], g["wk"], g["wv"], g["wo"],
                         g["bob"], g["id16"], tag="c_")
        em.phase_barrier()
        em.tagk = "d."
        with ExitStack() as esp:
            emit_phase_d(em, nc, esp, x3i, g["y"], g["wrT"], g["wkT2"], g["wvT2"],
                         g["mu8"], g["id16"], g["id32"], tag="d_")
        em.wait_all("sync")
        block = es.enter_context(nc.Block())
        em.finalize(block)
    return nc


_NC_CACHE = {}


def _get_nc():
    if "f" not in _NC_CACHE:
        _NC_CACHE["f"] = _build_fused()
    return _NC_CACHE["f"]


def _f32c(x):
    return np.ascontiguousarray(np.asarray(x, dtype=np.float32))


def kernel(L0_attn_x, L0_attn_qkv_w, L0_attn_qkv_b, L0_attn_proj_w, L0_attn_proj_b,
           L0_ffn_router_prob, L0_ffn_norm_w, L0_ffn_norm_b, L0_ffn_q_w, L0_ffn_q_b,
           L0_ffn_k_w, L0_ffn_k_b, L0_ffn_v_w, L0_ffn_v_b, L0_ffn_o_w, L0_ffn_o_b,
           L1_attn_W_Q, L1_attn_W_K, L1_attn_W_V, L1_attn_W_out, L1_attn_b_out,
           L1_ffn_W_r, L1_ffn_W_k, L1_ffn_W_v, L1_ffn_mu_r, L1_ffn_mu_k):
    cores = list(range(8))
    e_cnt = int(np.asarray(L0_ffn_router_prob).shape[2])
    assert e_cnt == 4, "kernel specialized for 4 experts"
    x = np.asarray(L0_attn_x, dtype=np.float32)
    xflat = np.ascontiguousarray(x.reshape(B * T, D))
    id16 = np.eye(128).astype(ml_dtypes.bfloat16)
    id32 = np.eye(128, dtype=np.float32)

    qkv_w = np.asarray(L0_attn_qkv_w, dtype=np.float32)
    qkv_b = np.asarray(L0_attn_qkv_b, dtype=np.float32)
    projT = _f32c(np.asarray(L0_attn_proj_w, dtype=np.float32).T)
    pbr = _f32c(np.asarray(L0_attn_proj_b, dtype=np.float32)[None, :])
    hd = 64
    inv_freq = 1.0 / 10000.0 ** (np.arange(0, hd, 2, dtype=np.float32) / hd)
    pos = np.arange(T, dtype=np.float32)
    freqs = pos[:, None] * inv_freq[None, :]
    cosd = np.ascontiguousarray(np.tile(np.cos(freqs).T, (4, 1))).astype(ml_dtypes.bfloat16)
    sind = np.ascontiguousarray(np.tile(np.sin(freqs).T, (4, 1))).astype(ml_dtypes.bfloat16)
    R2T = np.zeros((128, 128), np.float32)
    for r in range(128):
        blk = (r // 64) * 64
        dd = r % 64
        if dd < 32:
            R2T[r, blk + dd + 32] = -1.0
        else:
            R2T[r, blk + dd - 32] = 1.0
    r2d = np.ascontiguousarray(R2T.T)

    def pair_arr(w):
        wT = np.asarray(w, dtype=np.float32).T
        return np.ascontiguousarray(
            wT.reshape(8, 128, 8, 128).transpose(2, 1, 0, 3).reshape(1024, 1024))
    qwT = pair_arr(L0_ffn_q_w)
    kwT = pair_arr(L0_ffn_k_w)
    vwT = pair_arr(L0_ffn_v_w)
    owT = _f32c(np.asarray(L0_ffn_o_w, dtype=np.float32).T)
    qkvb8 = _f32c(np.concatenate([np.asarray(L0_ffn_q_b, np.float32).reshape(8, 128).T,
                                  np.asarray(L0_ffn_k_b, np.float32).reshape(8, 128).T,
                                  np.asarray(L0_ffn_v_b, np.float32).reshape(8, 128).T], axis=1))
    wbc = _f32c(np.broadcast_to(np.asarray(L0_ffn_norm_w, np.float32), (128, 1024)))
    bbc = _f32c(np.broadcast_to(np.asarray(L0_ffn_norm_b, np.float32), (128, 1024)))
    obr = _f32c(np.asarray(L0_ffn_o_b, np.float32)[None, :])

    def pair_arr2(w):
        return np.ascontiguousarray(
            np.asarray(w, np.float32).reshape(8, 128, 8, 128)
            .transpose(2, 1, 0, 3).reshape(1024, 1024))
    wq = pair_arr2(np.asarray(L1_attn_W_Q, np.float32).reshape(1024, 1024))
    wk = pair_arr2(np.asarray(L1_attn_W_K, np.float32).reshape(1024, 1024))
    wv = pair_arr2(np.asarray(L1_attn_W_V, np.float32).reshape(1024, 1024))
    wo = _f32c(np.asarray(L1_attn_W_out, np.float32).transpose(1, 0, 2).reshape(1024, 1024))
    bob = _f32c(np.broadcast_to(np.asarray(L1_attn_b_out, np.float32), (128, 1024)))

    wrT = _f32c(np.asarray(L1_ffn_W_r, np.float32).T)
    wkT2 = _f32c(np.asarray(L1_ffn_W_k, np.float32).T)
    wvT2 = _f32c(np.asarray(L1_ffn_W_v, np.float32).T)
    mu8 = _f32c(np.concatenate([np.asarray(L1_ffn_mu_r, np.float32).reshape(8, 128).T,
                                np.asarray(L1_ffn_mu_k, np.float32).reshape(8, 128).T], axis=1))

    shared = {"projT": projT, "pbr": pbr, "cosd": cosd, "sind": sind, "r2d": r2d,
              "id16": id16, "id32": id32, "qwT": qwT, "kwT": kwT, "vwT": vwT,
              "owT": owT, "qkvb8": qkvb8, "wbc": wbc, "bbc": bbc, "obr": obr,
              "wq": wq, "wk": wk, "wv": wv, "wo": wo, "bob": bob,
              "wrT": wrT, "wkT2": wkT2, "wvT2": wvT2, "mu8": mu8, "xall": xflat}
    in_maps = []
    for c in cores:
        wq_parts = []
        for t3 in range(3):
            W = qkv_w[1024 * t3 + 128 * c: 1024 * t3 + 128 * c + 128, :]
            wq_parts.append(np.ascontiguousarray(
                W.T.reshape(8, 128, 128).transpose(1, 0, 2).reshape(128, 1024)))
        wqkvT = _f32c(np.concatenate(wq_parts, axis=0))
        qkvbc = _f32c(np.concatenate(
            [qkv_b[1024 * t3 + 128 * c: 1024 * t3 + 128 * c + 128] for t3 in range(3)])[None, :])
        in_maps.append(dict(shared, wqkvT=wqkvT, qkvb=qkvbc,
                            xc=np.ascontiguousarray(x[c])))
    res = run_bass_kernel_spmd(_get_nc(), in_maps, core_ids=cores)
    return np.stack([np.asarray(r["y"]) for r in res.results], axis=0)


# version 16
# speedup vs baseline: 1.1918x; 1.1918x over previous
import numpy as np
import ml_dtypes
from contextlib import ExitStack

import concourse.bass as bass
import concourse.mybir as mybir
from concourse.bass_utils import run_bass_kernel_spmd

F32 = mybir.dt.float32
BF16 = mybir.dt.bfloat16
AF = mybir.ActivationFunctionType
ALU = mybir.AluOpType
AX = mybir.AxisListType

T = 4096
B = 8
D = 1024
TC = 512
NTC = 8
NT = 32
NC = 8
LND = float(np.log(1024.0) * 0.5)

ENGINES = ("sync", "tensor", "vector", "scalar", "gpsimd")


class Em:
    def __init__(self, nc, es: ExitStack, tag=""):
        self.nc = nc
        self.es = es
        self.tag = tag
        self.plan = {e: [] for e in ENGINES}
        self.csem = {}
        self.dsem = {}
        self.last_write = {}
        self.readers = {}
        self.waited = {e: {} for e in ENGINES}
        self.sems = {}
        self._semn = 0
        self.tagk = ""

    def _mk_sem(self, name):
        h = self.es.enter_context(self.nc.semaphore(f"{self.tag}{name}"))
        sid = self._semn
        self._semn += 1
        self.sems[sid] = h
        return sid

    def _csem(self, eng):
        if eng not in self.csem:
            self.csem[eng] = [self._mk_sem(f"c_{eng}"), 0]
        return self.csem[eng]

    def _dsem(self, slot):
        if slot not in self.dsem:
            self.dsem[slot] = [self._mk_sem(f"d_{slot}"), 0]
        return self.dsem[slot]

    def _need_waits(self, eng, reads, writes):
        toks = {}
        for r in reads:
            t = self.last_write.get(r)
            if t is not None:
                toks[t[0]] = max(toks.get(t[0], 0), t[1])
        for w in writes:
            t = self.last_write.get(w)
            if t is not None:
                toks[t[0]] = max(toks.get(t[0], 0), t[1])
            for t in self.readers.get(w, ()):
                toks[t[0]] = max(toks.get(t[0], 0), t[1])
        out = []
        wt = self.waited[eng]
        for sid, val in toks.items():
            if wt.get(sid, 0) < val:
                out.append((sid, val))
                wt[sid] = val
        return out

    def _post(self, eng, tok, reads, writes):
        for w in writes:
            self.last_write[w] = tok
            self.readers[w] = []
        for r in reads:
            self.readers.setdefault(r, []).append(tok)

    def _k(self, key):
        return key if key.startswith("g.") else self.tagk + key

    def op(self, eng, fns, reads=(), writes=()):
        if callable(fns):
            fns = [fns]
        reads = [self._k(r) for r in reads]
        writes = [self._k(w) for w in writes]
        waits = self._need_waits(eng, reads, writes)
        cs = self._csem(eng)
        cs[1] += 1
        self.plan[eng].append(("op", waits, fns, cs[0], 1))
        self._post(eng, (cs[0], cs[1]), reads, writes)

    def dma(self, eng, fn, reads=(), writes=(), slot=None):
        reads = [self._k(r) for r in reads]
        writes = [self._k(w) for w in writes]
        if slot is None:
            slot = writes[0] if writes else self._k("dma")
        else:
            slot = self._k(slot)
        waits = self._need_waits(eng, reads, writes)
        ds = self._dsem(slot)
        ds[1] += 16
        self.plan[eng].append(("op", waits, [fn], ds[0], 16))
        self._post(eng, (ds[0], ds[1]), reads, writes)

    def barrier_on(self, eng, keys):
        waits = self._need_waits(eng, keys, ())
        if waits:
            self.plan[eng].append(("waits", waits))

    def phase_barrier(self):
        for eng in ENGINES:
            waits = []
            wt = self.waited[eng]
            for sid in self.sems:
                final = 0
                for cs in self.csem.values():
                    if cs[0] == sid:
                        final = cs[1]
                for ds in self.dsem.values():
                    if ds[0] == sid:
                        final = ds[1]
                if final and wt.get(sid, 0) < final:
                    waits.append((sid, final))
                    wt[sid] = final
            if waits:
                self.plan[eng].append(("waits", waits))

    def wait_all(self, eng):
        waits = []
        wt = self.waited[eng]
        for sid, h in self.sems.items():
            final = None
            for e2, cs in self.csem.items():
                if cs[0] == sid:
                    final = cs[1]
            for e2, ds in self.dsem.items():
                if ds[0] == sid:
                    final = ds[1]
            if final and wt.get(sid, 0) < final:
                waits.append((sid, final))
                wt[sid] = final
        self.plan[eng].append(("waits", waits))

    def finalize(self, block):
        for eng in ENGINES:
            entries = self.plan[eng]
            if not entries:
                continue
            deco = getattr(block, eng)

            def body(e, entries=entries):
                for ent in entries:
                    if ent[0] == "waits":
                        for sid, val in ent[1]:
                            e.wait_ge(self.sems[sid], val)
                    else:
                        _, waits, fns, sid, inc = ent
                        for wsid, val in waits:
                            e.wait_ge(self.sems[wsid], val)
                        for i, fn in enumerate(fns):
                            instr = fn(e)
                            if i == len(fns) - 1:
                                instr.then_inc(self.sems[sid], inc)
            deco(body)


def emit_ln_tile(em, src, srck, ybf, ybfk, bufs):
    (s1, s1k), (ssq, ssqk), (sq, sqk), (st, stk), cb = bufs
    em.op("vector", lambda e: e.tensor_reduce(s1[:], src[:], AX.X, ALU.add),
          reads=[srck], writes=[s1k])
    em.op("scalar", lambda e: e.activation(sq[:], src[:], AF.Square, accum_out=ssq[:]),
          reads=[srck], writes=[sqk, ssqk])
    em.op("vector", lambda e: e.tensor_mul(st[:, 0:1], s1[:], s1[:]),
          reads=[s1k], writes=[stk])
    em.op("vector", lambda e: e.scalar_tensor_tensor(
        st[:, 1:2], st[:, 0:1], -1.0 / D, ssq[:], ALU.mult, ALU.add),
        reads=[stk, ssqk], writes=[stk])
    em.op("scalar", lambda e: e.activation(st[:, 2:3], st[:, 1:2], AF.Ln, bias=cb[:, 0:1]),
          reads=[stk, "lncb"], writes=[stk])
    em.op("scalar", lambda e: e.activation(st[:, 2:3], st[:, 2:3], AF.Exp, scale=-0.5, bias=cb[:, 1:2]),
          reads=[stk, "lncb"], writes=[stk])
    em.op("vector", lambda e: e.scalar_tensor_tensor(
        st[:, 3:4], s1[:], -1.0 / D, st[:, 2:3], ALU.mult, ALU.mult),
        reads=[s1k, stk], writes=[stk])
    em.op("vector", lambda e: e.tensor_scalar(
        ybf[:], src[:], st[:, 2:3], st[:, 3:4], ALU.mult, ALU.add),
        reads=[srck, stk], writes=[ybfk])


def emit_ln_stats(em, src, srck, st, c0, bufs):
    s1, ssq, sq, cb = bufs
    em.op("vector", lambda e: e.tensor_reduce(s1[:], src, AX.X, ALU.add),
          reads=[srck], writes=["s1"])
    em.op("scalar", lambda e: e.activation(sq[0][:], src, AF.Square, accum_out=ssq[:]),
          reads=[srck], writes=[sq[1], "ssq"])
    em.op("vector", lambda e: e.tensor_mul(st[:, 4:5], s1[:], s1[:]),
          reads=["s1"], writes=["st"])
    em.op("vector", lambda e: e.scalar_tensor_tensor(
        st[:, 5:6], st[:, 4:5], -1.0 / D, ssq[:], ALU.mult, ALU.add),
        reads=["st", "ssq"], writes=["st"])
    em.op("scalar", lambda e: e.activation(st[:, c0:c0 + 1], st[:, 5:6], AF.Ln, bias=cb[:, 0:1]),
          reads=["st", "lncb"], writes=["st"])
    em.op("scalar", lambda e: e.activation(st[:, c0:c0 + 1], st[:, c0:c0 + 1], AF.Exp,
                                           scale=-0.5, bias=cb[:, 1:2]),
          reads=["st", "lncb"], writes=["st"])
    em.op("vector", lambda e: e.scalar_tensor_tensor(
        st[:, c0 + 1:c0 + 2], s1[:], -1.0 / D, st[:, c0:c0 + 1], ALU.mult, ALU.mult),
        reads=["s1", "st"], writes=["st"])


def emit_phase_a(em, nc, es, xall, xc, x1, wqkvT, qkvb, projT, pbr, cosd, sind,
                 r2d, id16, tag=""):
    sb = lambda n, s, d: es.enter_context(nc.sbuf_tensor(tag + n, s, d))
    ps = lambda n, s, d: es.enter_context(nc.psum_tensor(tag + n, s, d))

    sscr = sb("sscr", [128, 8 * T], BF16)
    cosa = sb("cosa", [128, T], BF16)
    sina = sb("sina", [128, T], BF16)
    pjs = sb("pjs", [128, 8192], BF16)
    xlnT = [sb(f"xlnT{i}", [128, 8 * TC], BF16) for i in range(2)]
    wq3 = sb("wq3", [128, 3 * 1024], BF16)
    qTp = sb("qTp", [128, 4096], BF16)
    kTp = sb("kTp", [128, 4096], BF16)
    Vp = sb("Vp", [128, 2 * 32 * 65], BF16)
    r2 = sb("r2", [128, 128], BF16)
    sbid = sb("sbid", [128, 128], BF16)
    qkbs = sb("qkbs", [1, 384], F32)
    ones5 = sb("ones5", [1, 512], F32)
    wstg = [sb("wstg0", [128, 1024], F32)] * 2
    xt = [sb(f"xt{i}", [128, 1024], F32) for i in range(2)]
    xt4 = [xt[0], xt[1], sb("xt2", [128, 1024], F32), sb("xt3", [128, 1024], F32)]
    ot = sb("ot0", [128, 1024], F32)
    ybf2 = [sb(f"ybf{i}", [128, 1024], BF16) for i in range(2)]
    st4 = [sb(f"st4{i}", [128, 16], F32) for i in range(2)]
    lncb = sb("lncb", [128, 2], F32)
    qsb = sb("qsb", [128, 512], BF16)
    tcos = sb("tcos", [128, 512], F32)
    tsin = sb("tsin", [128, 512], F32)
    onec = sb("onec", [128, 1], BF16)
    oner = sb("oner", [1, 128], F32)
    nmx = sb("nmx", [1, 32], F32)
    cst = sb("cst", [1, 8], F32)
    bias_p = sb("bias_p", [128, 1], F32)
    pt = [sb(f"pt{i}", [128, 1024], BF16) for i in range(4)]
    linv = sb("linv", [1, 512], F32)
    lbc = sb("lbc", [128, 1024], F32)
    onorm = sb("onorm", [128, 512], F32)

    ps_s2 = [ps("ps_s2a", [128, 1024], F32), ps("ps_s2b", [128, 1024], F32)]
    ps_o2 = [ps("ps_oa", [128, 512], F32), ps("ps_ob", [128, 512], F32)]
    ps_bc = ps("ps_bc", [128, 512], F32)
    ps_t = ps("ps_t", [128, 1024], BF16)
    ps_a = ps_s2[0]
    ps_r = ps_s2[1]
    ps_pj = ps_s2[0]

    sq2 = [(wstg[0], "wstg0"), (ot, "ot0")]

    sscrv = sscr[:].rearrange("p (c n) -> p c n", n=T)
    pbs = None
    vstg = pt[2]
    q2 = pt[3]
    Vpv = Vp[:].rearrange("p (h s c) -> p h s c", h=2, s=32)

    em.dma("sync", lambda e: e.dma_start(out=sbid[:], in_=id16), writes=["sbid"])
    em.dma("sync", lambda e: e.dma_start(out=cosa[:], in_=cosd), writes=["cosa"])
    em.dma("sync", lambda e: e.dma_start(out=sina[:], in_=sind), writes=["sina"])
    em.op("vector", lambda e: e.memset(onec[:], 1.0), writes=["onec"])
    em.op("vector", lambda e: e.memset(oner[:], 1.0), writes=["oner"])
    em.op("vector", lambda e: e.memset(ones5[:], 1.0), writes=["ones5"])
    em.op("vector", lambda e: e.memset(lncb[:, 0:1], D * 1e-5), writes=["lncb"])
    em.op("vector", lambda e: e.memset(lncb[:, 1:2], LND), writes=["lncb"])
    em.op("vector", lambda e: e.memset(Vpv[:, :, :, 64:65], 1.0), writes=["Vp"])
    b, bk = wstg[0], "wstg0"
    em.dma("sync", lambda e: e.dma_start(out=b[:, 0:128], in_=r2d), writes=[bk])
    em.op("vector", lambda e: e.tensor_copy(r2[:], b[:, 0:128]), reads=[bk], writes=["r2"])
    em.dma("sync", lambda e: e.dma_start(out=b[0:1, 0:384], in_=qkvb),
           reads=["r2"], writes=[bk])
    em.op("vector", lambda e: e.tensor_copy(qkbs[:], b[0:1, 0:384]), reads=[bk], writes=["qkbs"])
    for t3 in range(3):
        em.dma("sync", lambda e, t3=t3: e.dma_start(
            out=b[:], in_=wqkvT[t3 * 128:(t3 + 1) * 128, :]), reads=["qkbs"], writes=[bk])
        em.op("vector", lambda e, t3=t3: e.tensor_copy(
            wq3[:, t3 * 1024:(t3 + 1) * 1024], b[:]), reads=[bk], writes=["wq3"])
    for kc in range(8):
        em.dma("sync", lambda e, kc=kc: e.dma_start(
            out=b[:], in_=projT[kc * 128:(kc + 1) * 128, :]), reads=["pjslast"], writes=[bk])
        em.op("vector", lambda e, kc=kc: e.tensor_copy(
            pjs[:, kc * 1024:(kc + 1) * 1024], b[:]), reads=[bk], writes=["pjslast"])

    for bb in range(B):
        for g in range(NTC):
            xl, xlk = xlnT[g % 2], f"xlnT{g % 2}"
            xlv = xl[:].rearrange("p (c n) -> p c n", n=TC)
            def _stageA(g, xlk2):
                st, stk = st4[g % 2], f"st4{g % 2}"
                for jj in range(4):
                    row0 = bb * T + g * 512 + jj * 128
                    xb, xbk = xt4[jj + 4 * (g % 2) if False else jj], f"xt{jj}"
                    em.dma("sync", lambda e, row0=row0, xb=xb: e.dma_start(
                        out=xb[:], in_=xall[row0:row0 + 128, :]), writes=[xbk])
                    em.op("vector", lambda e, xb=xb, jj=jj, st=st: e.tensor_reduce(
                        st[:, jj:jj + 1], xb[:], AX.X, ALU.add),
                        reads=[xbk], writes=[stk])
                    sq, sqk = sq2[g % 2]
                    em.op("scalar", lambda e, xb=xb, jj=jj, st=st, sq=sq: e.activation(
                        sq[:], xb[:], AF.Square, accum_out=st[:, 4 + jj:5 + jj]),
                        reads=[xbk], writes=[sqk, stk])

            def _stageB(g, xlv, xlk):
                st, stk = st4[g % 2], f"st4{g % 2}"
                em.op("vector", lambda e, st=st: e.tensor_mul(
                    st[:, 8:12], st[:, 0:4], st[:, 0:4]), reads=[stk], writes=[stk])
                em.op("vector", lambda e, st=st: e.scalar_tensor_tensor(
                    st[:, 8:12], st[:, 8:12], -1.0 / D, st[:, 4:8], ALU.mult, ALU.add),
                    reads=[stk], writes=[stk])
                em.op("scalar", lambda e, st=st: e.activation(
                    st[:, 8:12], st[:, 8:12], AF.Ln, bias=lncb[:, 0:1]),
                    reads=[stk, "lncb"], writes=[stk])
                em.op("scalar", lambda e, st=st: e.activation(
                    st[:, 8:12], st[:, 8:12], AF.Exp, scale=-0.5, bias=lncb[:, 1:2]),
                    reads=[stk, "lncb"], writes=[stk])
                em.op("vector", lambda e, st=st: e.scalar_tensor_tensor(
                    st[:, 12:16], st[:, 0:4], -1.0 / D, st[:, 8:12], ALU.mult, ALU.mult),
                    reads=[stk], writes=[stk])
                for jj in range(4):
                    xb, xbk = xt4[jj], f"xt{jj}"
                    ybf, ybfk = ybf2[jj % 2], f"ybf{jj % 2}"
                    em.op("vector", lambda e, xb=xb, jj=jj, st=st, ybf=ybf: e.tensor_scalar(
                        ybf[:], xb[:], st[:, 8 + jj:9 + jj], st[:, 12 + jj:13 + jj],
                        ALU.mult, ALU.add),
                        reads=[xbk, stk], writes=[ybfk])
                    em.op("tensor", [
                        (lambda e, k=k, ybf=ybf: e.transpose(ps_t[:, k * 128:(k + 1) * 128],
                                                    ybf[:, k * 128:(k + 1) * 128], sbid[:]))
                        for k in range(8)],
                        reads=[ybfk, "sbid"], writes=["ps_t"])
                    em.op("vector", lambda e, jj=jj, xlv=xlv: e.tensor_copy(
                        xlv[:, :, jj * 128: jj * 128 + 128],
                        ps_t[:].rearrange("p (c n) -> p c n", n=128)),
                        reads=["ps_t"], writes=[xlk])

            if g == 0:
                _stageA(g, None)
            _stageB(g, xlv, xlk)
            if g < NTC - 1:
                xl_n, xlk_n = xlnT[(g + 1) % 2], f"xlnT{(g + 1) % 2}"
                _stageA(g + 1, None)
            gc = slice(g * 512, (g + 1) * 512)
            for t3, dst, dstk in [(0, qTp, "qTp"), (1, kTp, "kTp")]:
                h0 = t3 * 512
                fns = [
                    (lambda e, kc=kc, t3=t3, xlv=xlv, h0=h0: e.matmul(
                        ps_a[:, h0:h0 + 512], wq3[:, t3 * 1024 + kc * 128: t3 * 1024 + (kc + 1) * 128],
                        xlv[:, kc, :], start=(kc == 0), stop=False))
                    for kc in range(8)]
                fns.append(lambda e, t3=t3, h0=h0: e.matmul(
                    ps_a[:, h0:h0 + 512], qkbs[:, t3 * 128:(t3 + 1) * 128], ones5[:],
                    start=False, stop=True))
                em.op("tensor", fns, reads=["wq3", xlk, "qkbs", "ones5"],
                      writes=["ps_s2a.lo" if t3 == 0 else "ps_s2a.hi"])
                half = 0 if t3 == 0 else 512
                hk = "ps_s2a.lo" if t3 == 0 else "ps_s2a.hi"
                rps = ps_o2[t3]
                rk = "ps_oa" if t3 == 0 else "ps_ob"
                qb, qbk = (qsb, "qsb") if (2 * g + t3) % 2 == 0 else (pt[0], "pt0")
                em.op("scalar", lambda e, half=half, qb=qb: e.copy(
                    qb[:, 0:512], ps_a[:, half:half + 512]),
                    reads=[hk], writes=[qbk])
                em.op("tensor", lambda e, rps=rps, qb=qb: e.matmul(
                    rps[:], r2[:], qb[:, 0:512], start=True, stop=True),
                      reads=["r2", qbk], writes=[rk])
                em.op("vector", lambda e, g=g, half=half: e.tensor_mul(
                    tcos[:], ps_a[:, half:half + 512], cosa[:, g * 512:(g + 1) * 512]),
                    reads=[hk, "cosa", rk], writes=["tcos"])
                em.op("vector", lambda e, g=g, rps=rps: e.tensor_mul(
                    tsin[:], rps[:], sina[:, g * 512:(g + 1) * 512]),
                    reads=[rk, "sina", hk], writes=["tsin"])
                em.op("vector", lambda e, dst=dst, gc=gc: e.tensor_add(
                    dst[:, gc], tcos[:], tsin[:]),
                    reads=["tcos", "tsin"], writes=[dstk])
            fns = [
                (lambda e, kc=kc, xlv=xlv: e.matmul(
                    ps_s2[1][:, 512:1024], wq3[:, 2 * 1024 + kc * 128: 2 * 1024 + (kc + 1) * 128],
                    xlv[:, kc, :], start=(kc == 0), stop=False))
                for kc in range(8)]
            fns.append(lambda e: e.matmul(
                ps_s2[1][:, 512:1024], qkbs[:, 256:384], ones5[:], start=False, stop=True))
            em.op("tensor", fns, reads=["wq3", xlk, "qkbs", "ones5"], writes=["ps_s2b.hi"])
            em.op("scalar", lambda e: e.copy(vstg[:, 0:512], ps_s2[1][:, 512:1024]),
                  reads=["ps_s2b.hi"], writes=["pt2"])
            em.op("tensor", [
                (lambda e, k=k: e.transpose(ps_t[:, k * 128:(k + 1) * 128],
                                            vstg[:, k * 128:(k + 1) * 128], sbid[:]))
                for k in range(4)],
                reads=["pt2", "sbid"], writes=["ps_t"])
            for k in range(4):
                sbk = g * 4 + k
                em.op("vector", lambda e, k=k, sbk=sbk: e.tensor_copy(
                    Vpv[:, 0, sbk, 0:64], ps_t[:, k * 128:k * 128 + 64]),
                    reads=["ps_t"], writes=["Vp"])
                em.op("vector", lambda e, k=k, sbk=sbk: e.tensor_copy(
                    Vpv[:, 1, sbk, 0:64], ps_t[:, k * 128 + 64:k * 128 + 128]),
                    reads=["ps_t"], writes=["Vp"])
        for src, srck, off in [(qTp, "qTp", 0), (kTp, "kTp", 8)]:
            for tc in range(NTC):
                qb, qbk = (q2, "pt3") if tc % 2 == 0 else (pt[1], "pt1")
                em.op("scalar", lambda e, src=src, tc=tc, qb=qb: e.activation(
                    qb[:, 0:512], src[:, tc * TC:(tc + 1) * TC], AF.Square),
                    reads=[srck], writes=[qbk])
                em.op("tensor", lambda e, qb=qb: e.matmul(
                    ps_bc[0:1, :], onec[:], qb[:, 0:512], start=True, stop=True),
                    reads=[qbk, "onec"], writes=["ps_bc"])
                em.op("vector", lambda e, tc=tc, off=off: e.tensor_reduce(
                    nmx[:, off + tc:off + tc + 1], ps_bc[0:1, :], AX.X, ALU.max),
                    reads=["ps_bc"], writes=["nmx"])
        em.op("vector", lambda e: e.tensor_reduce(cst[:, 0:1], nmx[:, 0:8], AX.X, ALU.max),
              reads=["nmx"], writes=["cst"])
        em.op("vector", lambda e: e.tensor_reduce(cst[:, 1:2], nmx[:, 8:16], AX.X, ALU.max),
              reads=["nmx"], writes=["cst"])
        em.op("vector", lambda e: e.tensor_mul(cst[:, 2:3], cst[:, 0:1], cst[:, 1:2]),
              reads=["cst"], writes=["cst"])
        em.op("scalar", lambda e: e.activation(cst[:, 3:4], cst[:, 2:3], AF.Ln),
              reads=["cst"], writes=["cst"])
        em.op("scalar", lambda e: e.activation(cst[:, 4:5], cst[:, 3:4], AF.Exp, scale=0.5),
              reads=["cst"], writes=["cst"])
        em.op("vector", lambda e: e.tensor_scalar(
            cst[:, 5:6], cst[:, 4:5], -1.02 / 8.0, -1e-3, ALU.mult, ALU.add),
            reads=["cst"], writes=["cst"])
        em.op("tensor", lambda e: e.matmul(ps_bc[:, 0:1], oner[:], cst[:, 5:6],
                                           start=True, stop=True),
              reads=["oner", "cst"], writes=["ps_bc"])
        em.op("scalar", lambda e: e.copy(bias_p[:], ps_bc[:, 0:1]),
              reads=["ps_bc"], writes=["bias_p"])

        pvq = []
        epiq = []
        def _flush(queue, n_keep=0):
            while len(queue) > n_keep:
                eng, fn, rd, wr = queue.pop(0)
                em.op(eng, fn, reads=rd, writes=wr)
        zu = tcos
        lrs = [(onorm[0:1, 0:512], "onorm"), (lbc[0:1, 512:1024], "lbc")]
        for qc in range(NTC):
            for sbi in range(32):
                if sbi >= 2:
                    _flush(epiq, max(0, len(epiq) - 2) if sbi < 31 else 0)
                par = sbi % 2
                pss = ps_s2[par]
                ks = [f"ps_s2{'ab'[par]}.lo", f"ps_s2{'ab'[par]}.hi"]
                pb, pbk = pt[sbi % 4], f"pt{sbi % 4}"
                em.op("tensor", [
                    (lambda e, sbi=sbi, hh=hh, qc=qc, pss=pss: e.matmul(
                        pss[:, hh * TC:(hh + 1) * TC],
                        kTp[hh * 64:hh * 64 + 64, sbi * 128:(sbi + 1) * 128],
                        qTp[hh * 64:hh * 64 + 64, qc * TC:(qc + 1) * TC],
                        start=True, stop=True))
                    for hh in range(2)],
                    reads=["kTp", "qTp"], writes=ks)
                em.op("scalar", lambda e, pb=pb, pss=pss: e.activation(
                    pb[:], pss[:], AF.Exp, bias=bias_p[:, 0:1], scale=0.125),
                    reads=ks + ["bias_p"], writes=[pbk])
                pvq.append(("tensor", [
                    (lambda e, sbi=sbi, hh=hh, pb=pb: e.matmul(
                        ps_o2[hh][0:65, :], Vpv[:, hh, sbi, :],
                        pb[:, hh * TC:(hh + 1) * TC],
                        start=(sbi == 0), stop=(sbi == 31)))
                    for hh in range(2)],
                    ["Vp", pbk], ["ps_oa", "ps_ob"]))
                _flush(pvq, 3)
            _flush(pvq)
            em.op("vector", lambda e: e.tensor_copy(zu[0:64, :], ps_o2[0][0:64, :]),
                  reads=["ps_oa"], writes=["tcos"])
            em.op("vector", lambda e: e.tensor_copy(zu[64:128, :], ps_o2[1][0:64, :]),
                  reads=["ps_ob"], writes=["tcos"])
            em.op("vector", lambda e: e.tensor_copy(lrs[0][0], ps_o2[0][64:65, :]),
                  reads=["ps_oa"], writes=["onorm"])
            em.op("vector", lambda e: e.tensor_copy(lrs[1][0], ps_o2[1][64:65, :]),
                  reads=["ps_ob"], writes=["onorm"])
            for hh in range(2):
                r0 = hh * 64
                lv, lvk = lrs[hh]
                lb = lbc[r0:r0 + 64, hh * 512:hh * 512 + 512]
                pbc = ps_bc[r0:r0 + 64, :]
                tpb = (hh % 2) * 2048 + bb * 256
                em.op("scalar", lambda e, lv=lv: e.activation(lv, lv, AF.Ln),
                      reads=[lvk], writes=[lvk])
                em.op("scalar", lambda e, lv=lv: e.activation(
                    lv, lv, AF.Exp, scale=-1.0), reads=[lvk], writes=[lvk])
                epiq.append(("tensor", (lambda e, lv=lv, pbc=pbc: e.matmul(
                    pbc, oner[:, 0:64], lv, start=True, stop=True)),
                    ["oner", lvk], ["ps_bc"]))
                epiq.append(("vector", (lambda e, lb=lb, pbc=pbc: e.tensor_copy(lb, pbc)),
                    ["ps_bc"], ["lbc"]))
                epiq.append(("vector", (lambda e, r0=r0, lb=lb: e.tensor_mul(
                    onorm[r0:r0 + 64, :], zu[r0:r0 + 64, :], lb)),
                    ["tcos", "lbc"], ["onorm"]))
                for ql in range(16):
                    p0 = (ql % 2) * 64
                    ch = ql // 2
                    tcol = tpb + qc * 32
                    epiq.append(("gpsimd", (lambda e, ql=ql, p0=p0, ch=ch, tcol=tcol, r0=r0: e.tensor_copy(
                        sscrv[p0:p0 + 64, ch, tcol:tcol + 32],
                        onorm[r0:r0 + 64, :].rearrange("p (a b) -> p a b", b=16)[:, :, ql])),
                        ["onorm"], ["sscr"]))
        _flush(epiq)
    pbs = lbc
    em.dma("sync", lambda e: e.dma_start(out=pbs[0:1, :], in_=pbr), writes=["lbc"])
    for j in range(32):
        xb, xbk = xt[j % 2], f"xt{j % 2}"
        em.dma("sync", lambda e, j=j, xb=xb: e.dma_start(
            out=xb[:], in_=xc[j * 128:(j + 1) * 128, :]), writes=[xbk])
        fns = []
        for oc in range(2):
            fns += [
                (lambda e, kc=kc, j=j, oc=oc: e.matmul(
                    ps_pj[:, oc * 512:(oc + 1) * 512], sscrv[:, kc, j * 128:(j + 1) * 128],
                    pjs[:, kc * 1024 + oc * 512: kc * 1024 + (oc + 1) * 512],
                    start=(kc == 0), stop=False))
                for kc in range(8)]
            fns.append(lambda e, oc=oc: e.matmul(
                ps_pj[:, oc * 512:(oc + 1) * 512], oner[:, 0:128], pbs[0:1, oc * 512:(oc + 1) * 512],
                start=False, stop=True))
        em.op("tensor", fns, reads=["sscr", "pjslast", "oner", "lbc"],
              writes=["ps_s2a.lo", "ps_s2a.hi"])
        em.op("vector", lambda e, xb=xb: e.tensor_add(ot[:], xb[:], ps_pj[:]),
              reads=["ps_s2a.lo", "ps_s2a.hi", xbk], writes=["ot0"])
        em.dma("sync", lambda e, j=j: e.dma_start(
            out=x1[j * 128:(j + 1) * 128, :], in_=ot[:]), reads=["ot0"],
            writes=["g.x1"], slot="st_ot0")


def emit_phase_b(em, nc, es, x1, x2, qwT, kwT, vwT, owT, qkvb8, wbc, bbc, obr, id16, tag=""):
    sb = lambda n, s, d: es.enter_context(nc.sbuf_tensor(tag + n, s, d))
    ps = lambda n, s, d: es.enter_context(nc.psum_tensor(tag + n, s, d))

    xiT = sb("xiT", [128, 8 * T], BF16)
    aoT = sb("aoT", [128, 8 * T], BF16)
    sbid = sb("sbid", [128, 128], BF16)
    wstg = [sb("wstg0", [128, 1024], F32)] * 2
    qwh = sb("qwh", [128, 1024], BF16)
    kwh = sb("kwh", [128, 1024], BF16)
    vwh = sb("vwh", [128, 1024], BF16)
    qkb = sb("qkb", [128, 24], F32)
    wbcs = sb("wbcs", [128, 1024], F32)
    bbcs = sb("bbcs", [128, 1024], F32)
    obs = sb("obs", [1, 1024], F32)
    xt = [sb(f"xt{i}", [128, 1024], F32) for i in range(2)]
    ot = sb("ot0", [128, 1024], F32)
    xmt = sb("xmt", [128, 1024], F32)
    ybf = sb("ybf", [128, 1024], BF16)
    s1 = sb("s1", [128, 1], F32)
    ssq = sb("ssq", [128, 1], F32)
    st = sb("st", [128, 8], F32)
    lncb = sb("lncb", [128, 2], F32)
    sstash = sb("sstash", [128, 64], F32)
    qTh = sb("qTh", [128, 4096], BF16)
    kTh = sb("kTh", [128, 4096], BF16)
    Vh = sb("Vh", [128, 4096], BF16)
    lg = [sb(f"lg{i}", [128, 512], BF16) for i in range(2)]
    onec = sb("onec", [128, 1], BF16)
    oner = sb("oner", [1, 128], F32)
    nmx = sb("nmx", [1, 32], F32)
    cst = sb("cst", [1, 8], F32)
    bias_h = sb("bias_h", [128, 1], F32)
    pt = [sb(f"pt{i}", [128, 1024], BF16) for i in range(4)]
    linv = sb("linv", [1, 512], F32)
    lbc = sb("lbc", [128, 512], F32)

    ps_s2 = [ps("ps_s2a", [128, 1024], F32), ps("ps_s2b", [128, 1024], F32)]
    ps_o = ps("ps_o", [128, 512], F32)
    ps_l = ps("ps_l", [128, 512], F32)
    ps_t = ps("ps_t", [128, 1024], BF16)
    ps_a = ps_s2[0]
    ps_bc = ps_l
    ps_pj = ps_s2[0]

    sq = (wstg[0], "wstg0")
    lnbufs = (s1, ssq, sq, lncb)

    owTs = xiT
    xiTv = xiT[:].rearrange("p (c n) -> p c n", n=T)
    aoTv = aoT[:].rearrange("p (c n) -> p c n", n=T)
    Vhv = Vh[:].rearrange("p (s c) -> p s c", s=32)
    vstg = pt[2]
    q2 = pt[3]
    zu = wstg[0]

    em.dma("sync", lambda e: e.dma_start(out=sbid[:], in_=id16), writes=["sbid"])
    em.dma("sync", lambda e: e.dma_start(out=qkb[:], in_=qkvb8), writes=["qkb"])
    em.dma("sync", lambda e: e.dma_start(out=wbcs[:], in_=wbc), writes=["wbcs"])
    em.dma("sync", lambda e: e.dma_start(out=bbcs[:], in_=bbc), writes=["bbcs"])
    em.dma("sync", lambda e: e.dma_start(out=obs[:], in_=obr), writes=["obs"])
    em.op("vector", lambda e: e.memset(onec[:], 1.0), writes=["onec"])
    em.op("vector", lambda e: e.memset(oner[:], 1.0), writes=["oner"])
    em.op("vector", lambda e: e.memset(lncb[:, 0:1], D * 1e-5), writes=["lncb"])
    em.op("vector", lambda e: e.memset(lncb[:, 1:2], LND), writes=["lncb"])
    for j in range(NT):
        b, bk = xt[j % 2], f"xt{j % 2}"
        em.dma("sync", lambda e, j=j, b=b: e.dma_start(
            out=b[:], in_=x1[j * 128:(j + 1) * 128, :]), reads=["g.x1"], writes=[bk])
        emit_ln_stats(em, b[:], bk, st, 0, lnbufs)
        em.op("vector", lambda e, j=j: e.tensor_scalar(
            sstash[:, 2 * j:2 * j + 1], st[:, 0:1], 1.0, 1.0, ALU.mult, ALU.add),
            reads=["st"], writes=["sstash"])
        em.op("vector", lambda e, j=j: e.tensor_copy(
            sstash[:, 2 * j + 1:2 * j + 2], st[:, 1:2]),
            reads=["st"], writes=["sstash"])
        em.op("vector", lambda e, b=b: e.tensor_scalar(
            xmt[:], b[:], st[:, 0:1], st[:, 1:2], ALU.mult, ALU.add),
            reads=[bk, "st"], writes=["xmt"])
        emit_ln_stats(em, xmt[:], "xmt", st, 2, lnbufs)
        em.op("vector", lambda e: e.tensor_scalar(
            xmt[:], xmt[:], st[:, 2:3], st[:, 3:4], ALU.mult, ALU.add),
            reads=["xmt", "st"], writes=["xmt"])
        em.op("vector", lambda e: e.tensor_mul(xmt[:], xmt[:], wbcs[:]),
              reads=["xmt", "wbcs"], writes=["xmt"])
        em.op("vector", lambda e: e.tensor_add(ybf[:], xmt[:], bbcs[:]),
              reads=["xmt", "bbcs"], writes=["ybf"])
        em.op("tensor", [
            (lambda e, k=k: e.transpose(ps_t[:, k * 128:(k + 1) * 128],
                                        ybf[:, k * 128:(k + 1) * 128], sbid[:]))
            for k in range(8)],
            reads=["ybf", "sbid"], writes=["ps_t"])
        em.op("vector", lambda e, j=j: e.tensor_copy(
            xiTv[:, :, j * 128: j * 128 + 128],
            ps_t[:].rearrange("p (c n) -> p c n", n=128)),
            reads=["ps_t"], writes=["xiT"])

    for h in range(8):
        for wdram, slab, sk in [(qwT, qwh, "qwh"), (kwT, kwh, "kwh"), (vwT, vwh, "vwh")]:
            b, bk = wstg[0], "wstg0"
            em.dma("sync", lambda e, wdram=wdram, h=h, b=b: e.dma_start(
                out=b[:], in_=wdram[h * 128:(h + 1) * 128, :]), writes=[bk])
            em.op("vector", lambda e, slab=slab, b=b: e.tensor_copy(slab[:], b[:]),
                  reads=[bk], writes=[sk])
        for tc in range(NTC):
            ex = tc // 2
            nkc = 8 >> ex
            em.op("tensor", [
                (lambda e, kc=kc, tc=tc, last=(kc == nkc - 1): e.matmul(
                    ps_a[:, 0:512], qwh[:, kc * 128:(kc + 1) * 128],
                    xiTv[:, kc, tc * TC:(tc + 1) * TC], start=(kc == 0), stop=last))
                for kc in range(nkc)],
                reads=["qwh", "xiT"], writes=["ps_s2a"])
            em.op("vector", lambda e, tc=tc, h=h: e.tensor_scalar_add(
                qTh[:, tc * TC:(tc + 1) * TC], ps_a[:, 0:512], qkb[:, h:h + 1]),
                reads=["ps_s2a", "qkb"], writes=["qTh"])
            em.op("tensor", [
                (lambda e, kc=kc, tc=tc, last=(kc == nkc - 1): e.matmul(
                    ps_a[:, 512:1024], kwh[:, kc * 128:(kc + 1) * 128],
                    xiTv[:, kc, tc * TC:(tc + 1) * TC], start=(kc == 0), stop=last))
                for kc in range(nkc)],
                reads=["kwh", "xiT"], writes=["ps_s2a"])
            em.op("vector", lambda e, tc=tc, h=h: e.tensor_scalar_add(
                kTh[:, tc * TC:(tc + 1) * TC], ps_a[:, 512:1024], qkb[:, 8 + h:8 + h + 1]),
                reads=["ps_s2a", "qkb"], writes=["kTh"])
            em.op("tensor", [
                (lambda e, kc=kc, tc=tc, last=(kc == nkc - 1): e.matmul(
                    ps_s2[1][:, 0:512], vwh[:, kc * 128:(kc + 1) * 128],
                    xiTv[:, kc, tc * TC:(tc + 1) * TC], start=(kc == 0), stop=last))
                for kc in range(nkc)],
                reads=["vwh", "xiT"], writes=["ps_s2b"])
            em.op("vector", lambda e, tc=tc, h=h: e.tensor_scalar_add(
                vstg[:, 0:512], ps_s2[1][:, 0:512], qkb[:, 16 + h:16 + h + 1]),
                reads=["ps_s2b", "qkb"], writes=["pt2"])
            em.op("tensor", [
                (lambda e, k=k: e.transpose(ps_t[:, k * 128:(k + 1) * 128],
                                            vstg[:, k * 128:(k + 1) * 128], sbid[:]))
                for k in range(4)],
                reads=["pt2", "sbid"], writes=["ps_t"])
            em.op("vector", lambda e, tc=tc: e.tensor_copy(
                Vhv[:, tc * 4:tc * 4 + 4, :],
                ps_t[:, 0:512].rearrange("p (s c) -> p s c", c=128)),
                reads=["ps_t"], writes=["Vh"])
        for src, srck, off in [(qTh, "qTh", 0), (kTh, "kTh", 8)]:
            for tc in range(NTC):
                qb, qbk = (q2, "pt3") if tc % 2 == 0 else (pt[1], "pt1")
                em.op("scalar", lambda e, src=src, tc=tc, qb=qb: e.activation(
                    qb[:, 0:512], src[:, tc * TC:(tc + 1) * TC], AF.Square),
                    reads=[srck], writes=[qbk])
                em.op("tensor", lambda e, qb=qb: e.matmul(
                    ps_bc[0:1, :], onec[:], qb[:, 0:512], start=True, stop=True),
                    reads=[qbk, "onec"], writes=["ps_l"])
                em.op("vector", lambda e, tc=tc, off=off: e.tensor_reduce(
                    nmx[:, off + tc:off + tc + 1], ps_bc[0:1, :], AX.X, ALU.max),
                    reads=["ps_l"], writes=["nmx"])
        em.op("vector", lambda e: e.tensor_reduce(cst[:, 0:1], nmx[:, 0:8], AX.X, ALU.max),
              reads=["nmx"], writes=["cst"])
        em.op("vector", lambda e: e.tensor_reduce(cst[:, 1:2], nmx[:, 8:16], AX.X, ALU.max),
              reads=["nmx"], writes=["cst"])
        em.op("vector", lambda e: e.tensor_mul(cst[:, 2:3], cst[:, 0:1], cst[:, 1:2]),
              reads=["cst"], writes=["cst"])
        em.op("scalar", lambda e: e.activation(cst[:, 3:4], cst[:, 2:3], AF.Ln),
              reads=["cst"], writes=["cst"])
        em.op("scalar", lambda e: e.activation(cst[:, 4:5], cst[:, 3:4], AF.Exp, scale=0.5),
              reads=["cst"], writes=["cst"])
        em.op("vector", lambda e: e.tensor_scalar(
            cst[:, 5:6], cst[:, 4:5], -1.02 / 32.0, -1e-3, ALU.mult, ALU.add),
            reads=["cst"], writes=["cst"])
        em.op("tensor", lambda e: e.matmul(ps_bc[:, 0:1], oner[:], cst[:, 5:6],
                                           start=True, stop=True),
              reads=["oner", "cst"], writes=["ps_l"])
        em.op("scalar", lambda e: e.copy(bias_h[:], ps_bc[:, 0:1]),
              reads=["ps_l"], writes=["bias_h"])

        pvq = []
        epiq = []
        def _flush(queue, n_keep=0):
            while len(queue) > n_keep:
                eng, fn, rd, wr = queue.pop(0)
                em.op(eng, fn, reads=rd, writes=wr)
        for qc in range(NTC):
            for sp in range(0, 32, 2):
                if sp >= 2:
                    _flush(epiq, max(0, len(epiq) - 2) if sp < 30 else 0)
                pi = (sp // 2) % 2
                pss, pssk = ps_s2[pi], f"ps_s2{'ab'[pi]}"
                pb, pbk = pt[(sp // 2) % 4], f"pt{(sp // 2) % 4}"
                em.op("tensor", [
                    (lambda e, sbi=sp + d, d=d, qc=qc, pss=pss: e.matmul(
                        pss[:, d * TC:(d + 1) * TC],
                        kTh[:, sbi * 128:(sbi + 1) * 128],
                        qTh[:, qc * TC:(qc + 1) * TC], start=True, stop=True))
                    for d in range(2)],
                    reads=["kTh", "qTh"], writes=[pssk])
                em.op("scalar", lambda e, pb=pb, pss=pss: e.activation(
                    pb[:], pss[:], AF.Exp, bias=bias_h[:, 0:1], scale=1.0 / 32.0),
                    reads=[pssk, "bias_h"], writes=[pbk])
                gi = sp // 16
                if sp % 16 == 0:
                    em.op("vector", lambda e, pb=pb, gi=gi: e.tensor_add(
                        lg[gi][:], pb[:, 0:512], pb[:, 512:1024]),
                        reads=[pbk], writes=[f"lg{gi}"])
                else:
                    em.op("vector", lambda e, pb=pb, gi=gi: e.tensor_add(
                        lg[gi][:], lg[gi][:], pb[:, 0:512]),
                        reads=[pbk, f"lg{gi}"], writes=[f"lg{gi}"])
                    em.op("vector", lambda e, pb=pb, gi=gi: e.tensor_add(
                        lg[gi][:], lg[gi][:], pb[:, 512:1024]),
                        reads=[pbk, f"lg{gi}"], writes=[f"lg{gi}"])
                fns = [
                    (lambda e, sbi=sp + d, d=d, pb=pb: e.matmul(
                        ps_o[:], Vhv[:, sbi, :], pb[:, d * TC:(d + 1) * TC],
                        start=(sbi == 0), stop=(sbi == 31)))
                    for d in range(2)]
                pvq.append(("tensor", fns, ["Vh", pbk], ["ps_o"]))
                while len(pvq) > 1:
                    eng, fn, rd, wr = pvq.pop(0)
                    em.op(eng, fn, reads=rd, writes=wr)
            _flush(pvq)
            em.op("vector", lambda e: e.tensor_add(lg[0][:], lg[0][:], lg[1][:]),
                  reads=["lg0", "lg1"], writes=["lg0"])
            em.op("tensor", lambda e: e.matmul(
                ps_l[0:1, :], onec[:], lg[0][:], start=True, stop=True),
                reads=["lg0", "onec"], writes=["ps_l"])
            em.op("vector", lambda e: e.tensor_copy(zu[:, 0:512], ps_o[:]),
                  reads=["ps_o"], writes=["wstg0"])
            em.op("scalar", lambda e: e.activation(linv[:], ps_l[0:1, :], AF.Ln),
                  reads=["ps_l"], writes=["linv"])
            em.op("scalar", lambda e: e.activation(linv[:], linv[:], AF.Exp, scale=-1.0),
                  reads=["linv"], writes=["linv"])
            em.op("tensor", lambda e: e.matmul(
                ps_bc[:, :], oner[:], linv[:], start=True, stop=True),
                reads=["oner", "linv"], writes=["ps_l"])
            em.op("vector", lambda e: e.tensor_copy(lbc[:], ps_bc[:, :]),
                  reads=["ps_l"], writes=["lbc"])
            epiq.append(("vector", (lambda e, h=h, qc=qc: e.tensor_mul(
                aoTv[:, h, qc * TC:(qc + 1) * TC], zu[:, 0:512], lbc[:])),
                ["wstg0", "lbc"], ["aoT"]))
        _flush(epiq)
    for kc in range(8):
        b, bk = wstg[0], "wstg0"
        em.dma("sync", lambda e, kc=kc, b=b: e.dma_start(
            out=b[:], in_=owT[kc * 128:(kc + 1) * 128, :]), writes=[bk])
        em.op("vector", lambda e, kc=kc, b=b: e.tensor_copy(
            owTs[:, kc * 1024:(kc + 1) * 1024], b[:]), reads=[bk], writes=["xiT"])

    for j in range(NT):
        ex = j // 8
        m = 1024 >> ex
        nkc = 8 >> ex
        b, bk = xt[j % 2], f"xt{j % 2}"
        em.dma("sync", lambda e, j=j, b=b: e.dma_start(
            out=b[:], in_=x1[j * 128:(j + 1) * 128, :]), reads=["g.x1"], writes=[bk])
        em.op("vector", lambda e, j=j, b=b: e.tensor_scalar(
            ot[:], b[:], sstash[:, 2 * j:2 * j + 1], sstash[:, 2 * j + 1:2 * j + 2],
            ALU.mult, ALU.add),
            reads=[bk, "sstash"], writes=["ot0"])
        fns = []
        for oc in range((m + 511) // 512):
            ow = min(512, m - oc * 512)
            fns += [
                (lambda e, kc=kc, j=j, oc=oc, ow=ow: e.matmul(
                    ps_pj[:, oc * 512:oc * 512 + ow], aoTv[:, kc, j * 128:(j + 1) * 128],
                    owTs[:, kc * 1024 + oc * 512: kc * 1024 + oc * 512 + ow],
                    start=(kc == 0), stop=False))
                for kc in range(nkc)]
            fns.append(lambda e, j=j, oc=oc, ow=ow: e.matmul(
                ps_pj[:, oc * 512:oc * 512 + ow], oner[:, 0:128], obs[:, oc * 512:oc * 512 + ow],
                start=False, stop=True))
        em.op("tensor", fns, reads=["aoT", "xiT", "oner", "obs"], writes=["ps_s2a"])
        em.op("vector", lambda e, m=m: e.tensor_add(
            ot[:, 0:m], ot[:, 0:m], ps_pj[:, 0:m]),
            reads=["ps_s2a", "ot0"], writes=["ot0"])
        em.dma("sync", lambda e, j=j: e.dma_start(
            out=x2[j * 128:(j + 1) * 128, :], in_=ot[:]), reads=["ot0"],
            writes=["g.x2"], slot="st_ot0")


def _mk_qkv_evac(qTp, kTp, vstg, Vpv, ps_t, sbid, TCc):
    def _emit(em, job):
        kind, tc, psx, half, pxk = job
        if kind == "q":
            em.op("scalar", lambda e: e.copy(qTp[:, tc * TCc:(tc + 1) * TCc], psx[:, half:half + 512]),
                  reads=[pxk], writes=["qTp"])
        elif kind == "k":
            em.op("scalar", lambda e: e.copy(kTp[:, tc * TCc:(tc + 1) * TCc], psx[:, half:half + 512]),
                  reads=[pxk], writes=["kTp"])
        else:
            em.op("scalar", lambda e: e.copy(vstg[:, 0:512], psx[:, half:half + 512]),
                  reads=[pxk], writes=["pt2"])
            em.op("tensor", [
                (lambda e, k=k: e.transpose(ps_t[:, k * 128:(k + 1) * 128],
                                            vstg[:, k * 128:(k + 1) * 128], sbid[:]))
                for k in range(4)],
                reads=["pt2", "sbid"], writes=["ps_t"])
            for k in range(4):
                sbk = tc * 4 + k
                em.op("vector", lambda e, k=k, sbk=sbk: e.tensor_copy(
                    Vpv[:, 0, sbk, 0:64], ps_t[:, k * 128:k * 128 + 64]),
                    reads=["ps_t"], writes=["Vp"])
                em.op("vector", lambda e, k=k, sbk=sbk: e.tensor_copy(
                    Vpv[:, 1, sbk, 0:64], ps_t[:, k * 128 + 64:k * 128 + 128]),
                    reads=["ps_t"], writes=["Vp"])
    return _emit


def emit_phase_c(em, nc, es, x2, x3, wq, wk, wv, wo, bob, id16, tag=""):
    sb = lambda n, s, d: es.enter_context(nc.sbuf_tensor(tag + n, s, d))
    ps = lambda n, s, d: es.enter_context(nc.psum_tensor(tag + n, s, d))

    xiT = sb("xiT", [128, 8 * T], BF16)
    zT = sb("zT", [128, 8 * T], BF16)
    wos = sb("wos", [128, 8192], BF16)
    bo = sb("bo", [1, 1024], F32)
    sbid = sb("sbid", [128, 128], BF16)
    wstg = [sb("wstg0", [128, 1024], F32)] * 2
    wqp = sb("wqp", [128, 1024], BF16)
    wkp = sb("wkp", [128, 1024], BF16)
    wvp = sb("wvp", [128, 1024], BF16)
    xt = [sb(f"xt{i}", [128, 1024], F32) for i in range(2)]
    ot = [sb("ot0", [128, 1024], F32)] * 2
    ybf2 = [sb(f"ybf{i}", [128, 1024], BF16) for i in range(2)]
    s12 = [sb(f"s1{i}", [128, 1], F32) for i in range(2)]
    ssq2 = [sb(f"ssq{i}", [128, 1], F32) for i in range(2)]
    st2 = [sb(f"st{i}", [128, 8], F32) for i in range(2)]
    qTp = sb("qTp", [128, 4096], BF16)
    kTp = sb("kTp", [128, 4096], BF16)
    Vp = sb("Vp", [128, 2 * 32 * 65], BF16)
    onec = sb("onec", [128, 1], BF16)
    oner = sb("oner", [1, 128], F32)
    nmx = sb("nmx", [1, 32], F32)
    cst = sb("cst", [1, 8], F32)
    bias_p = sb("bias_p", [128, 1], F32)
    pt = [sb(f"pt{i}", [128, 1024], BF16) for i in range(4)]

    ps_s2 = [ps("ps_s2a", [128, 1024], F32), ps("ps_s2b", [128, 1024], F32)]
    ps_o2 = [ps("ps_oa", [128, 512], F32), ps("ps_ob", [128, 512], F32)]
    ps_bc = ps("ps_bc", [128, 512], F32)
    ps_t = ps("ps_t", [128, 1024], BF16)
    ps_pj = ps_s2[0]

    lncb = sb("lncb", [128, 2], F32)
    lnsets = [
        ((s12[0], "s10"), (ssq2[0], "ssq0"), (wstg[0], "wstg0"), (st2[0], "st0"), lncb),
        ((s12[1], "s11"), (ssq2[1], "ssq1"), (ot[0], "ot0"), (st2[1], "st1"), lncb),
    ]

    xiTv = xiT[:].rearrange("p (c n) -> p c n", n=T)
    zTv = zT[:].rearrange("p (c n) -> p c n", n=T)
    Vpv = Vp[:].rearrange("p (h s c) -> p h s c", h=2, s=32)
    vstg = pt[2]
    q2 = pt[3]
    lbc = wstg[0]
    linv2 = [(xt[0], "xt0"), (xt[1], "xt1")]
    _emit_qkv_evac = _mk_qkv_evac(qTp, kTp, vstg, Vpv, ps_t, sbid, TC)

    em.dma("sync", lambda e: e.dma_start(out=sbid[:], in_=id16), writes=["sbid"])
    em.dma("sync", lambda e: e.dma_start(out=bo[:], in_=bob[0:1, :]), writes=["bo"])
    em.op("vector", lambda e: e.memset(onec[:], 1.0), writes=["onec"])
    em.op("vector", lambda e: e.memset(lncb[:, 0:1], D * 1e-5), writes=["lncb"])
    em.op("vector", lambda e: e.memset(lncb[:, 1:2], LND), writes=["lncb"])
    em.op("vector", lambda e: e.memset(oner[:], 1.0), writes=["oner"])
    em.op("vector", lambda e: e.memset(Vpv[:, :, :, 64:65], 1.0), writes=["Vp"])
    for kc in range(8):
        b = wstg[0]
        bk = "wstg0"
        em.dma("sync", lambda e, kc=kc, b=b: e.dma_start(
            out=b[:], in_=wo[kc * 128:(kc + 1) * 128, :]), writes=[bk])
        em.op("vector", lambda e, kc=kc, b=b: e.tensor_copy(
            wos[:, kc * 1024:(kc + 1) * 1024], b[:]), reads=[bk], writes=["wos"])

    for j in range(NT):
        b = xt[j % 2]
        bk = f"xt{j % 2}"
        em.dma("sync", lambda e, j=j, b=b: e.dma_start(
            out=b[:], in_=x2[j * 128:(j + 1) * 128, :]), reads=["g.x2"], writes=[bk])
        ybf, ybfk = ybf2[j % 2], f"ybf{j % 2}"
        emit_ln_tile(em, b, bk, ybf, ybfk, lnsets[j % 2])
        em.op("tensor", [
            (lambda e, k=k, ybf=ybf: e.transpose(ps_t[:, k * 128:(k + 1) * 128],
                                        ybf[:, k * 128:(k + 1) * 128], sbid[:]))
            for k in range(8)],
            reads=[ybfk, "sbid"], writes=["ps_t"])
        em.op("vector", lambda e, j=j: e.tensor_copy(
            xiTv[:, :, j * 128: j * 128 + 128],
            ps_t[:].rearrange("p (c n) -> p c n", n=128)),
            reads=["ps_t"], writes=["xiT"])

    for hp in range(8):
        for wdram, slab, sk in [(wq, wqp, "wqp"), (wk, wkp, "wkp"), (wv, wvp, "wvp")]:
            b = wstg[0]
            bk = "wstg0"
            em.dma("sync", lambda e, wdram=wdram, hp=hp, b=b: e.dma_start(
                out=b[:], in_=wdram[hp * 128:(hp + 1) * 128, :]), writes=[bk])
            em.op("vector", lambda e, slab=slab, b=b: e.tensor_copy(slab[:], b[:]),
                  reads=[bk], writes=[sk])
        qkv_jobs = []
        for tc in range(NTC):
            qkv_jobs.append(("q", tc, wqp, "wqp"))
            qkv_jobs.append(("k", tc, wkp, "wkp"))
            qkv_jobs.append(("v", tc, wvp, "wvp"))
        pending = None
        for gi, (kind, tc, wslab, wkey) in enumerate(qkv_jobs):
            psx = ps_s2[(gi // 2) % 2]
            half = (gi % 2) * 512
            pxk = f"ps_s2{'ab'[(gi // 2) % 2]}"
            em.op("tensor", [
                (lambda e, kc=kc, tc=tc, wslab=wslab, psx=psx, half=half: e.matmul(
                    psx[:, half:half + 512], wslab[:, kc * 128:(kc + 1) * 128],
                    xiTv[:, kc, tc * TC:(tc + 1) * TC], start=(kc == 0), stop=(kc == 7)))
                for kc in range(8)],
                reads=[wkey, "xiT"], writes=[pxk])
            if pending is not None:
                _emit_qkv_evac(em, pending)
            pending = (kind, tc, psx, half, pxk)
        _emit_qkv_evac(em, pending)
        for src, srck, off in [(qTp, "qTp", 0), (kTp, "kTp", 8)]:
            for tc in range(NTC):
                qb, qbk = (q2, "pt3") if tc % 2 == 0 else (pt[1], "pt1")
                em.op("scalar", lambda e, src=src, tc=tc, qb=qb: e.activation(
                    qb[:, 0:512], src[:, tc * TC:(tc + 1) * TC], AF.Square),
                    reads=[srck], writes=[qbk])
                em.op("tensor", lambda e, qb=qb: e.matmul(
                    ps_bc[0:1, :], onec[:], qb[:, 0:512],
                    start=True, stop=True), reads=[qbk, "onec"], writes=["ps_bc"])
                em.op("vector", lambda e, tc=tc, off=off: e.tensor_reduce(
                    nmx[:, off + tc:off + tc + 1], ps_bc[0:1, :], AX.X, ALU.max),
                    reads=["ps_bc"], writes=["nmx"])
        em.op("vector", lambda e: e.tensor_reduce(cst[:, 0:1], nmx[:, 0:8], AX.X, ALU.max),
              reads=["nmx"], writes=["cst"])
        em.op("vector", lambda e: e.tensor_reduce(cst[:, 1:2], nmx[:, 8:16], AX.X, ALU.max),
              reads=["nmx"], writes=["cst"])
        em.op("vector", lambda e: e.tensor_mul(cst[:, 2:3], cst[:, 0:1], cst[:, 1:2]),
              reads=["cst"], writes=["cst"])
        em.op("scalar", lambda e: e.activation(cst[:, 3:4], cst[:, 2:3], AF.Ln),
              reads=["cst"], writes=["cst"])
        em.op("scalar", lambda e: e.activation(cst[:, 4:5], cst[:, 3:4], AF.Exp, scale=0.5),
              reads=["cst"], writes=["cst"])
        em.op("vector", lambda e: e.tensor_scalar(
            cst[:, 5:6], cst[:, 4:5], -1.02 / 8.0, -1e-3, ALU.mult, ALU.add),
            reads=["cst"], writes=["cst"])
        em.op("tensor", lambda e: e.matmul(ps_bc[:, 0:1], oner[:], cst[:, 5:6],
                                           start=True, stop=True),
              reads=["oner", "cst"], writes=["ps_bc"])
        em.op("scalar", lambda e: e.copy(bias_p[:], ps_bc[:, 0:1]),
              reads=["ps_bc"], writes=["bias_p"])

        pvq = []
        epiq = []
        def _flush(queue, n_keep=0):
            while len(queue) > n_keep:
                eng, fn, rd, wr = queue.pop(0)
                em.op(eng, fn, reads=rd, writes=wr)
        for qc in range(NTC):
            nsb = 4 * qc + 4
            for sbi in range(nsb):
                if sbi >= 2:
                    _flush(epiq, max(0, len(epiq) - 1) if sbi < nsb - 1 else 0)

                par = sbi % 2
                pss, pssk = ps_s2[par], f"ps_s2{'ab'[par]}"
                pb, pbk = pt[sbi % 4], f"pt{sbi % 4}"
                em.op("tensor", [
                    (lambda e, sbi=sbi, hh=hh, qc=qc, pss=pss: e.matmul(
                        pss[:, hh * TC:(hh + 1) * TC],
                        kTp[hh * 64:hh * 64 + 64, sbi * 128:(sbi + 1) * 128],
                        qTp[hh * 64:hh * 64 + 64, qc * TC:(qc + 1) * TC],
                        start=True, stop=True))
                    for hh in range(2)],
                    reads=["kTp", "qTp"], writes=[pssk])
                em.op("scalar", lambda e, pb=pb, pss=pss: e.activation(
                    pb[:], pss[:], AF.Exp, bias=bias_p[:, 0:1], scale=0.125),
                    reads=[pssk, "bias_p"], writes=[pbk])
                if sbi >= 4 * qc:
                    em.op("gpsimd", lambda e, pb=pb, qc=qc, sbi=sbi: e.affine_select(
                        pb[:].rearrange("p (h n) -> p h n", h=2),
                        pb[:].rearrange("p (h n) -> p h n", h=2),
                        [[0, 2], [1, TC]], ALU.is_ge, 0.0,
                        base=qc * TC - sbi * 128, channel_multiplier=-1),
                        reads=[pbk], writes=[pbk])
                pvq.append(("tensor", [
                    (lambda e, sbi=sbi, hh=hh, nsb=nsb, pb=pb: e.matmul(
                        ps_o2[hh][0:65, :], Vpv[:, hh, sbi, :],
                        pb[:, hh * TC:(hh + 1) * TC],
                        start=(sbi == 0), stop=(sbi == nsb - 1)))
                    for hh in range(2)],
                    ["Vp", pbk], ["ps_oa", "ps_ob"]))
                _flush(pvq, 3)
            _flush(pvq)
            zu = ot[0]
            em.op("vector", lambda e: e.tensor_copy(zu[0:64, 0:512], ps_o2[0][0:64, :]),
                  reads=["ps_oa"], writes=["ot0"])
            em.op("vector", lambda e: e.tensor_copy(zu[64:128, 0:512], ps_o2[1][0:64, :]),
                  reads=["ps_ob"], writes=["ot0"])
            em.op("vector", lambda e: e.tensor_copy(xt[0][0:1, 0:512], ps_o2[0][64:65, :]),
                  reads=["ps_oa"], writes=["xt0"])
            em.op("vector", lambda e: e.tensor_copy(xt[1][0:1, 0:512], ps_o2[1][64:65, :]),
                  reads=["ps_ob"], writes=["xt1"])
            for hh in range(2):
                r0 = hh * 64
                lv, lvk = linv2[hh]
                lb = lbc[r0:r0 + 64, hh * 512:hh * 512 + 512]
                pbc = ps_bc[r0:r0 + 64, :]
                em.op("scalar", lambda e, lv=lv: e.activation(
                    lv[0:1, 0:512], lv[0:1, 0:512], AF.Ln),
                    reads=[lvk], writes=[lvk])
                em.op("scalar", lambda e, lv=lv: e.activation(
                    lv[0:1, 0:512], lv[0:1, 0:512], AF.Exp, scale=-1.0),
                    reads=[lvk], writes=[lvk])
                epiq.append(("tensor", (lambda e, lv=lv, pbc=pbc: e.matmul(
                    pbc, oner[:, 0:64], lv[0:1, 0:512],
                    start=True, stop=True)),
                    ["oner", lvk], ["ps_bc"]))
                epiq.append(("vector", (lambda e, lb=lb, pbc=pbc: e.tensor_copy(
                    lb, pbc)),
                    ["ps_bc"], ["wstg0"]))
                epiq.append(("vector", (lambda e, hp=hp, r0=r0, qc=qc, lb=lb: e.tensor_mul(
                    zTv[r0:r0 + 64, hp, qc * TC:(qc + 1) * TC],
                    zu[r0:r0 + 64, 0:512], lb)),
                    ["ot0", "wstg0"], ["zT"]))
        _flush(epiq)

    for j in range(NT):
        b = xt[j % 2]
        bk = f"xt{j % 2}"
        ob = ot[0]
        obk = "ot0"
        em.dma("sync", lambda e, j=j, b=b: e.dma_start(
            out=b[:], in_=x2[j * 128:(j + 1) * 128, :]), reads=["g.x2"], writes=[bk])
        fns = [
            (lambda e, kc=kc, j=j, oc=oc: e.matmul(
                ps_pj[:, oc * 512:(oc + 1) * 512], zTv[:, kc, j * 128:(j + 1) * 128],
                wos[:, kc * 1024 + oc * 512: kc * 1024 + (oc + 1) * 512],
                start=(kc == 0), stop=False))
            for oc in range(2) for kc in range(8)]
        fns.append(lambda e: e.matmul(ps_pj[:, 0:512], oner[:, 0:128], bo[:, 0:512],
                                      start=False, stop=False))
        fns.append(lambda e: e.matmul(ps_pj[:, 512:1024], oner[:, 0:128], bo[:, 512:1024],
                                      start=False, stop=True))
        em.op("tensor", fns, reads=["zT", "wos", "oner", "bo"], writes=["ps_s2a"])
        em.op("vector", lambda e, ob=ob, b=b: e.tensor_add(ob[:], b[:], ps_pj[:]),
              reads=["ps_s2a", bk], writes=[obk])
        em.dma("sync", lambda e, j=j, ob=ob: e.dma_start(
            out=x3[j * 128:(j + 1) * 128, :], in_=ob[:]), reads=[obk],
            writes=["g.x3"], slot="st_" + obk)


def emit_phase_d(em, nc, es, x3, out, wrT, wkT, wvT, mu8, id16, id32, tag=""):
    sb = lambda n, s, d: es.enter_context(nc.sbuf_tensor(tag + n, s, d))
    ps = lambda n, s, d: es.enter_context(nc.psum_tensor(tag + n, s, d))

    yT = sb("yT", [128, NC * (T + 1)], BF16)
    w_r = sb("w_r", [128, 8192], BF16)
    w_k = sb("w_k", [128, 8192], BF16)
    w_v = sb("w_v", [128, 8192], BF16)
    sbid = sb("sbid", [128, 128], BF16)
    sbid32 = sb("sbid32", [128, 128], F32)
    mu = sb("mu", [128, 16], F32)
    negmu = sb("negmu", [128, 16], F32)
    wf = [sb(f"wf{i}", [128, 1024], F32) for i in range(2)]
    xt = [sb(f"xt{i}", [128, 1024], F32) for i in range(2)]
    ot = [sb(f"ot{i}", [128, 1024], F32) for i in range(2)]
    xr = xt
    s1 = sb("s1", [128, 1], F32)
    ssq = sb("ssq", [128, 1], F32)
    st = sb("st", [128, 8], F32)
    ybf = sb("ybf", [128, 1024], BF16)
    t1r = sb("t1r", [128, 512], F32)
    t1k = sb("t1k", [128, 512], F32)
    t2 = sb("t2", [128, 512], F32)
    xrT = sb("xrT", [128, 4096], BF16)
    xkT = sb("xkT", [128, 4096], BF16)
    rT = sb("rT", [128, 4096], F32)
    kaT = sb("kaT", [128, 4096], BF16)
    outT = rT

    ps_t = ps("ps_t", [128, 1024], BF16)
    ps_r2 = [ps("ps_r", [128, 512], F32), ps("ps_rb", [128, 512], F32)]
    ps_k = ps("ps_k", [128, 512], F32)
    ps_v2 = [ps("ps_v", [128, 512], F32), ps("ps_vb", [128, 512], F32)]
    ps_tb = ps("ps_tb", [128, 1024], F32)

    yTv = yT[:].rearrange("p (c n) -> p c n", n=T + 1)

    em.dma("sync", lambda e: e.dma_start(out=sbid[:], in_=id16), writes=["sbid"])
    em.dma("sync", lambda e: e.dma_start(out=sbid32[:], in_=id32), writes=["sbid32"])
    em.dma("sync", lambda e: e.dma_start(out=mu[:], in_=mu8), writes=["mu"])
    em.op("vector", lambda e: e.tensor_scalar(negmu[:], mu[:], -1.0, 1.0, ALU.mult, ALU.add),
          reads=["mu"], writes=["negmu"])
    wjobs = []
    for wi, (slab, wd) in enumerate([(w_r, wrT), (w_k, wkT), (w_v, wvT)]):
        sn = f"w{wi}"
        for kc in range(8):
            b = wf[(wi * 8 + kc) % 2]
            bk = f"wf{(wi * 8 + kc) % 2}"
            wjobs.append((slab, sn, kc, b, bk, wd))
    em.op("vector", lambda e: e.memset(yTv[:, :, 0:1], 0.0), writes=["yT"])
    sq = ot[0]

    for j in range(NT):
        b = xt[j % 2]
        bk = f"xt{j % 2}"
        em.dma("sync", lambda e, j=j, b=b: e.dma_start(
            out=b[:], in_=x3[j * 128:(j + 1) * 128, :]), reads=["g.x3"], writes=[bk])
        em.op("vector", lambda e, b=b: e.tensor_reduce(s1[:], b[:], AX.X, ALU.add),
              reads=[bk], writes=["s1"])
        em.op("scalar", lambda e, b=b: e.activation(sq[:], b[:], AF.Square, accum_out=ssq[:]),
              reads=[bk], writes=["ot0", "ssq"])
        if 0 <= j - 2 < len(wjobs):
            slab, sn, kc, wb, wbk, wd = wjobs[j - 2]
            em.op("scalar", lambda e, slab=slab, kc=kc, wb=wb: e.copy(
                slab[:, kc * 1024:(kc + 1) * 1024], wb[:]),
                reads=[wbk], writes=[sn + "s"])
        if j < len(wjobs):
            slab, sn, kc, wb, wbk, wd = wjobs[j]
            em.dma("sync", lambda e, wd=wd, kc=kc, wb=wb: e.dma_start(
                out=wb[:], in_=wd[kc * 128:(kc + 1) * 128, :]), writes=[wbk])
        em.op("vector", lambda e: e.tensor_scalar_mul(st[:, 0:1], s1[:], 1.0 / D),
              reads=["s1"], writes=["st"])
        em.op("vector", lambda e: e.tensor_scalar_mul(st[:, 1:2], ssq[:], 1.0 / D),
              reads=["ssq"], writes=["st"])
        em.op("vector", lambda e: e.tensor_mul(st[:, 2:3], st[:, 0:1], st[:, 0:1]),
              reads=["st"], writes=["st"])
        em.op("vector", lambda e: e.tensor_sub(st[:, 2:3], st[:, 1:2], st[:, 2:3]),
              reads=["st"], writes=["st"])
        em.op("vector", lambda e: e.tensor_scalar_add(st[:, 2:3], st[:, 2:3], 1e-5),
              reads=["st"], writes=["st"])
        em.op("scalar", lambda e: e.activation(st[:, 3:4], st[:, 2:3], AF.Ln),
              reads=["st"], writes=["st"])
        em.op("scalar", lambda e: e.activation(st[:, 3:4], st[:, 3:4], AF.Exp, scale=-0.5),
              reads=["st"], writes=["st"])
        em.op("vector", lambda e: e.scalar_tensor_tensor(
            st[:, 4:5], st[:, 0:1], -1.0, st[:, 3:4], ALU.mult, ALU.mult),
            reads=["st"], writes=["st"])
        em.op("vector", lambda e, b=b: e.tensor_scalar(
            ybf[:], b[:], st[:, 3:4], st[:, 4:5], ALU.mult, ALU.add),
            reads=[bk, "st"], writes=["ybf"])
        em.op("tensor", [
            (lambda e, k=k: e.transpose(ps_t[:, k * 128:(k + 1) * 128],
                                        ybf[:, k * 128:(k + 1) * 128], sbid[:]))
            for k in range(8)],
            reads=["ybf", "sbid"], writes=["ps_t"])
        em.op("vector", lambda e, j=j: e.tensor_copy(
            yTv[:, :, 1 + j * 128: 1 + j * 128 + 128],
            ps_t[:].rearrange("p (c n) -> p c n", n=128)),
            reads=["ps_t"], writes=["yT"])

    def _emit_prep(tc):
        for kc in range(NC):
            ysh = yTv[:, kc, tc * TC: tc * TC + TC]
            yun = yTv[:, kc, 1 + tc * TC: 1 + tc * TC + TC]
            em.op("vector", lambda e, ysh=ysh, kc=kc: e.tensor_scalar_mul(
                t1r[:], ysh, negmu[:, kc:kc + 1]), reads=["yT", "negmu"], writes=["t1r"])
            em.op("vector", lambda e, yun=yun, kc=kc: e.scalar_tensor_tensor(
                xrT[:, kc * TC:(kc + 1) * TC], yun, mu[:, kc:kc + 1], t1r[:], ALU.mult, ALU.add),
                reads=["yT", "mu", "t1r"], writes=["xrT"])
            em.op("vector", lambda e, ysh=ysh, kc=kc: e.tensor_scalar_mul(
                t1k[:], ysh, negmu[:, 8 + kc:8 + kc + 1]), reads=["yT", "negmu"], writes=["t1k"])
            em.op("vector", lambda e, yun=yun, kc=kc: e.scalar_tensor_tensor(
                xkT[:, kc * TC:(kc + 1) * TC], yun, mu[:, 8 + kc:8 + kc + 1], t1k[:], ALU.mult, ALU.add),
                reads=["yT", "mu", "t1k"], writes=["xkT"])

    _emit_prep(0)
    for tc in range(NTC):
        for mc in range(8):
            pr, prk = ps_r2[mc % 2], ("ps_r" if mc % 2 == 0 else "ps_rb")
            em.op("tensor", [
                (lambda e, kc=kc, mc=mc, pr=pr: e.matmul(
                    pr[:], w_r[:, kc * 1024 + mc * 128: kc * 1024 + (mc + 1) * 128],
                    xrT[:, kc * TC:(kc + 1) * TC], start=(kc == 0), stop=(kc == 7)))
                for kc in range(8)],
                reads=["w0s", "xrT"], writes=[prk])
            em.op("scalar", lambda e, mc=mc, pr=pr: e.activation(
                rT[:, mc * TC:(mc + 1) * TC], pr[:], AF.Sigmoid),
                reads=[prk], writes=["rT"])
            em.op("tensor", [
                (lambda e, kc=kc, mc=mc: e.matmul(
                    ps_k[:], w_k[:, kc * 1024 + mc * 128: kc * 1024 + (mc + 1) * 128],
                    xkT[:, kc * TC:(kc + 1) * TC], start=(kc == 0), stop=(kc == 7)))
                for kc in range(8)],
                reads=["w1s", "xkT"], writes=["ps_k"])
            em.op("vector", lambda e: e.tensor_scalar_max(t2[:], ps_k[:], 0.0),
                  reads=["ps_k"], writes=["t2"])
            em.op("scalar", lambda e, mc=mc: e.activation(
                kaT[:, mc * TC:(mc + 1) * TC], t2[:], AF.Square),
                reads=["t2"], writes=["kaT"])
        for mc in range(8):
            pv, pvk = ps_v2[mc % 2], ("ps_v" if mc % 2 == 0 else "ps_vb")
            em.op("tensor", [
                (lambda e, kc=kc, mc=mc, pv=pv: e.matmul(
                    pv[:], w_v[:, kc * 1024 + mc * 128: kc * 1024 + (mc + 1) * 128],
                    kaT[:, kc * TC:(kc + 1) * TC], start=(kc == 0), stop=(kc == 7)))
                for kc in range(8)],
                reads=["w2s", "kaT"], writes=[pvk])
            em.op("vector", lambda e, mc=mc, pv=pv: e.tensor_mul(
                outT[:, mc * TC:(mc + 1) * TC], rT[:, mc * TC:(mc + 1) * TC], pv[:]),
                reads=["rT", pvk], writes=["rT"])
        if tc + 1 < NTC:
            _emit_prep(tc + 1)
        for tb in range(4):
            j = tc * 4 + tb
            b = xr[tb % 2]
            bk = f"xt{tb % 2}"
            ob = ot[tb % 2]
            obk = f"ot{tb % 2}"
            em.dma("sync", lambda e, j=j, b=b: e.dma_start(
                out=b[:], in_=x3[j * 128:(j + 1) * 128, :]), reads=["g.x3"], writes=[bk])
            em.op("tensor", [
                (lambda e, mc=mc, tb=tb: e.transpose(
                    ps_tb[:, mc * 128:(mc + 1) * 128],
                    outT[:, mc * TC + tb * 128: mc * TC + tb * 128 + 128], sbid32[:]))
                for mc in range(8)],
                reads=["rT", "sbid32"], writes=["ps_tb"])
            em.op("vector", lambda e, b=b, ob=ob: e.tensor_add(ob[:], b[:], ps_tb[:]),
                  reads=[bk, "ps_tb"], writes=[obk])
            em.dma("sync", lambda e, j=j, ob=ob: e.dma_start(
                out=out[j * 128:(j + 1) * 128, :], in_=ob[:]), reads=[obk], writes=["outd"],
                slot="st_" + obk)



def _build_fused():
    nc = bass.Bass("TRN2", target_bir_lowering=False)
    g = {}
    g["xall"] = nc.dram_tensor("xall", [B * T, D], F32, kind="ExternalInput").ap()
    g["xc"] = nc.dram_tensor("xc", [T, D], F32, kind="ExternalInput").ap()
    g["wqkvT"] = nc.dram_tensor("wqkvT", [3 * 128, 1024], F32, kind="ExternalInput").ap()
    g["qkvb"] = nc.dram_tensor("qkvb", [1, 384], F32, kind="ExternalInput").ap()
    g["projT"] = nc.dram_tensor("projT", [D, D], F32, kind="ExternalInput").ap()
    g["pbr"] = nc.dram_tensor("pbr", [1, D], F32, kind="ExternalInput").ap()
    g["cosd"] = nc.dram_tensor("cosd", [128, T], BF16, kind="ExternalInput").ap()
    g["sind"] = nc.dram_tensor("sind", [128, T], BF16, kind="ExternalInput").ap()
    g["r2d"] = nc.dram_tensor("r2d", [128, 128], F32, kind="ExternalInput").ap()
    g["id16"] = nc.dram_tensor("id16", [128, 128], BF16, kind="ExternalInput").ap()
    g["id32"] = nc.dram_tensor("id32", [128, 128], F32, kind="ExternalInput").ap()
    for nm in ["qwT", "kwT", "vwT", "owT"]:
        g[nm] = nc.dram_tensor(nm, [D, D], F32, kind="ExternalInput").ap()
    g["qkvb8"] = nc.dram_tensor("qkvb8", [128, 24], F32, kind="ExternalInput").ap()
    g["wbc"] = nc.dram_tensor("wbc", [128, D], F32, kind="ExternalInput").ap()
    g["bbc"] = nc.dram_tensor("bbc", [128, D], F32, kind="ExternalInput").ap()
    g["obr"] = nc.dram_tensor("obr", [1, D], F32, kind="ExternalInput").ap()
    for nm in ["wq", "wk", "wv", "wo"]:
        g[nm] = nc.dram_tensor(nm, [D, D], F32, kind="ExternalInput").ap()
    g["bob"] = nc.dram_tensor("bob", [128, D], F32, kind="ExternalInput").ap()
    for nm in ["wrT", "wkT2", "wvT2"]:
        g[nm] = nc.dram_tensor(nm, [D, D], F32, kind="ExternalInput").ap()
    g["mu8"] = nc.dram_tensor("mu8", [128, 16], F32, kind="ExternalInput").ap()
    g["y"] = nc.dram_tensor("y", [T, D], F32, kind="ExternalOutput").ap()
    x1i = nc.dram_tensor("x1i", [T, D], F32).ap()
    x2i = nc.dram_tensor("x2i", [T, D], F32).ap()
    x3i = nc.dram_tensor("x3i", [T, D], F32).ap()

    es = ExitStack()
    with es:
        em = Em(nc, es)
        em.tagk = "a."
        with ExitStack() as esp:
            emit_phase_a(em, nc, esp, g["xall"], g["xc"], x1i, g["wqkvT"], g["qkvb"],
                         g["projT"], g["pbr"], g["cosd"], g["sind"], g["r2d"],
                         g["id16"], tag="a_")
        em.phase_barrier()
        em.tagk = "b."
        with ExitStack() as esp:
            emit_phase_b(em, nc, esp, x1i, x2i, g["qwT"], g["kwT"], g["vwT"], g["owT"],
                         g["qkvb8"], g["wbc"], g["bbc"], g["obr"], g["id16"], tag="b_")
        em.phase_barrier()
        em.tagk = "c."
        with ExitStack() as esp:
            emit_phase_c(em, nc, esp, x2i, x3i, g["wq"], g["wk"], g["wv"], g["wo"],
                         g["bob"], g["id16"], tag="c_")
        em.phase_barrier()
        em.tagk = "d."
        with ExitStack() as esp:
            emit_phase_d(em, nc, esp, x3i, g["y"], g["wrT"], g["wkT2"], g["wvT2"],
                         g["mu8"], g["id16"], g["id32"], tag="d_")
        em.wait_all("sync")
        block = es.enter_context(nc.Block())
        em.finalize(block)
    return nc


_NC_CACHE = {}


def _get_nc():
    if "f" not in _NC_CACHE:
        _NC_CACHE["f"] = _build_fused()
    return _NC_CACHE["f"]


def _f32c(x):
    return np.ascontiguousarray(np.asarray(x, dtype=np.float32))


def kernel(L0_attn_x, L0_attn_qkv_w, L0_attn_qkv_b, L0_attn_proj_w, L0_attn_proj_b,
           L0_ffn_router_prob, L0_ffn_norm_w, L0_ffn_norm_b, L0_ffn_q_w, L0_ffn_q_b,
           L0_ffn_k_w, L0_ffn_k_b, L0_ffn_v_w, L0_ffn_v_b, L0_ffn_o_w, L0_ffn_o_b,
           L1_attn_W_Q, L1_attn_W_K, L1_attn_W_V, L1_attn_W_out, L1_attn_b_out,
           L1_ffn_W_r, L1_ffn_W_k, L1_ffn_W_v, L1_ffn_mu_r, L1_ffn_mu_k):
    cores = list(range(8))
    e_cnt = int(np.asarray(L0_ffn_router_prob).shape[2])
    assert e_cnt == 4, "kernel specialized for 4 experts"
    x = np.asarray(L0_attn_x, dtype=np.float32)
    xflat = np.ascontiguousarray(x.reshape(B * T, D))
    id16 = np.eye(128).astype(ml_dtypes.bfloat16)
    id32 = np.eye(128, dtype=np.float32)

    qkv_w = np.asarray(L0_attn_qkv_w, dtype=np.float32)
    qkv_b = np.asarray(L0_attn_qkv_b, dtype=np.float32)
    projT = _f32c(np.asarray(L0_attn_proj_w, dtype=np.float32).T)
    pbr = _f32c(np.asarray(L0_attn_proj_b, dtype=np.float32)[None, :])
    hd = 64
    inv_freq = 1.0 / 10000.0 ** (np.arange(0, hd, 2, dtype=np.float32) / hd)
    pos = np.arange(T, dtype=np.float32)
    freqs = pos[:, None] * inv_freq[None, :]
    cosd = np.ascontiguousarray(np.tile(np.cos(freqs).T, (4, 1))).astype(ml_dtypes.bfloat16)
    sind = np.ascontiguousarray(np.tile(np.sin(freqs).T, (4, 1))).astype(ml_dtypes.bfloat16)
    R2T = np.zeros((128, 128), np.float32)
    for r in range(128):
        blk = (r // 64) * 64
        dd = r % 64
        if dd < 32:
            R2T[r, blk + dd + 32] = -1.0
        else:
            R2T[r, blk + dd - 32] = 1.0
    r2d = np.ascontiguousarray(R2T.T)

    def pair_arr(w):
        wT = np.asarray(w, dtype=np.float32).T
        return np.ascontiguousarray(
            wT.reshape(8, 128, 8, 128).transpose(2, 1, 0, 3).reshape(1024, 1024))
    qwT = pair_arr(L0_ffn_q_w)
    kwT = pair_arr(L0_ffn_k_w)
    vwT = pair_arr(L0_ffn_v_w)
    owT = _f32c(np.asarray(L0_ffn_o_w, dtype=np.float32).T)
    qkvb8 = _f32c(np.concatenate([np.asarray(L0_ffn_q_b, np.float32).reshape(8, 128).T,
                                  np.asarray(L0_ffn_k_b, np.float32).reshape(8, 128).T,
                                  np.asarray(L0_ffn_v_b, np.float32).reshape(8, 128).T], axis=1))
    wbc = _f32c(np.broadcast_to(np.asarray(L0_ffn_norm_w, np.float32), (128, 1024)))
    bbc = _f32c(np.broadcast_to(np.asarray(L0_ffn_norm_b, np.float32), (128, 1024)))
    obr = _f32c(np.asarray(L0_ffn_o_b, np.float32)[None, :])

    def pair_arr2(w):
        return np.ascontiguousarray(
            np.asarray(w, np.float32).reshape(8, 128, 8, 128)
            .transpose(2, 1, 0, 3).reshape(1024, 1024))
    wq = pair_arr2(np.asarray(L1_attn_W_Q, np.float32).reshape(1024, 1024))
    wk = pair_arr2(np.asarray(L1_attn_W_K, np.float32).reshape(1024, 1024))
    wv = pair_arr2(np.asarray(L1_attn_W_V, np.float32).reshape(1024, 1024))
    wo = _f32c(np.asarray(L1_attn_W_out, np.float32).transpose(1, 0, 2).reshape(1024, 1024))
    bob = _f32c(np.broadcast_to(np.asarray(L1_attn_b_out, np.float32), (128, 1024)))

    wrT = _f32c(np.asarray(L1_ffn_W_r, np.float32).T)
    wkT2 = _f32c(np.asarray(L1_ffn_W_k, np.float32).T)
    wvT2 = _f32c(np.asarray(L1_ffn_W_v, np.float32).T)
    mu8 = _f32c(np.concatenate([np.asarray(L1_ffn_mu_r, np.float32).reshape(8, 128).T,
                                np.asarray(L1_ffn_mu_k, np.float32).reshape(8, 128).T], axis=1))

    shared = {"projT": projT, "pbr": pbr, "cosd": cosd, "sind": sind, "r2d": r2d,
              "id16": id16, "id32": id32, "qwT": qwT, "kwT": kwT, "vwT": vwT,
              "owT": owT, "qkvb8": qkvb8, "wbc": wbc, "bbc": bbc, "obr": obr,
              "wq": wq, "wk": wk, "wv": wv, "wo": wo, "bob": bob,
              "wrT": wrT, "wkT2": wkT2, "wvT2": wvT2, "mu8": mu8, "xall": xflat}
    in_maps = []
    for c in cores:
        wq_parts = []
        for t3 in range(3):
            W = qkv_w[1024 * t3 + 128 * c: 1024 * t3 + 128 * c + 128, :]
            wq_parts.append(np.ascontiguousarray(
                W.T.reshape(8, 128, 128).transpose(1, 0, 2).reshape(128, 1024)))
        wqkvT = _f32c(np.concatenate(wq_parts, axis=0))
        qkvbc = _f32c(np.concatenate(
            [qkv_b[1024 * t3 + 128 * c: 1024 * t3 + 128 * c + 128] for t3 in range(3)])[None, :])
        in_maps.append(dict(shared, wqkvT=wqkvT, qkvb=qkvbc,
                            xc=np.ascontiguousarray(x[c])))
    res = run_bass_kernel_spmd(_get_nc(), in_maps, core_ids=cores)
    return np.stack([np.asarray(r["y"]) for r in res.results], axis=0)


# version 17
# speedup vs baseline: 1.1968x; 1.0042x over previous
import numpy as np
import ml_dtypes
from contextlib import ExitStack

import concourse.bass as bass
import concourse.mybir as mybir
from concourse.bass_utils import run_bass_kernel_spmd

F32 = mybir.dt.float32
BF16 = mybir.dt.bfloat16
AF = mybir.ActivationFunctionType
ALU = mybir.AluOpType
AX = mybir.AxisListType

T = 4096
B = 8
D = 1024
TC = 512
NTC = 8
NT = 32
NC = 8
LND = float(np.log(1024.0) * 0.5)

ENGINES = ("sync", "tensor", "vector", "scalar", "gpsimd")


class Em:
    def __init__(self, nc, es: ExitStack, tag=""):
        self.nc = nc
        self.es = es
        self.tag = tag
        self.plan = {e: [] for e in ENGINES}
        self.csem = {}
        self.dsem = {}
        self.last_write = {}
        self.readers = {}
        self.waited = {e: {} for e in ENGINES}
        self.sems = {}
        self._semn = 0
        self.tagk = ""

    def _mk_sem(self, name):
        h = self.es.enter_context(self.nc.semaphore(f"{self.tag}{name}"))
        sid = self._semn
        self._semn += 1
        self.sems[sid] = h
        return sid

    def _csem(self, eng):
        if eng not in self.csem:
            self.csem[eng] = [self._mk_sem(f"c_{eng}"), 0]
        return self.csem[eng]

    def _dsem(self, slot):
        if slot not in self.dsem:
            self.dsem[slot] = [self._mk_sem(f"d_{slot}"), 0]
        return self.dsem[slot]

    def _need_waits(self, eng, reads, writes):
        toks = {}
        for r in reads:
            t = self.last_write.get(r)
            if t is not None:
                toks[t[0]] = max(toks.get(t[0], 0), t[1])
        for w in writes:
            t = self.last_write.get(w)
            if t is not None:
                toks[t[0]] = max(toks.get(t[0], 0), t[1])
            for t in self.readers.get(w, ()):
                toks[t[0]] = max(toks.get(t[0], 0), t[1])
        out = []
        wt = self.waited[eng]
        for sid, val in toks.items():
            if wt.get(sid, 0) < val:
                out.append((sid, val))
                wt[sid] = val
        return out

    def _post(self, eng, tok, reads, writes):
        for w in writes:
            self.last_write[w] = tok
            self.readers[w] = []
        for r in reads:
            self.readers.setdefault(r, []).append(tok)

    def _k(self, key):
        return key if key.startswith("g.") else self.tagk + key

    def op(self, eng, fns, reads=(), writes=()):
        if callable(fns):
            fns = [fns]
        reads = [self._k(r) for r in reads]
        writes = [self._k(w) for w in writes]
        waits = self._need_waits(eng, reads, writes)
        cs = self._csem(eng)
        cs[1] += 1
        self.plan[eng].append(("op", waits, fns, cs[0], 1))
        self._post(eng, (cs[0], cs[1]), reads, writes)

    def dma(self, eng, fn, reads=(), writes=(), slot=None):
        reads = [self._k(r) for r in reads]
        writes = [self._k(w) for w in writes]
        if slot is None:
            slot = writes[0] if writes else self._k("dma")
        else:
            slot = self._k(slot)
        waits = self._need_waits(eng, reads, writes)
        ds = self._dsem(slot)
        ds[1] += 16
        self.plan[eng].append(("op", waits, [fn], ds[0], 16))
        self._post(eng, (ds[0], ds[1]), reads, writes)

    def barrier_on(self, eng, keys):
        waits = self._need_waits(eng, keys, ())
        if waits:
            self.plan[eng].append(("waits", waits))

    def phase_barrier(self):
        for eng in ENGINES:
            waits = []
            wt = self.waited[eng]
            for sid in self.sems:
                final = 0
                for cs in self.csem.values():
                    if cs[0] == sid:
                        final = cs[1]
                for ds in self.dsem.values():
                    if ds[0] == sid:
                        final = ds[1]
                if final and wt.get(sid, 0) < final:
                    waits.append((sid, final))
                    wt[sid] = final
            if waits:
                self.plan[eng].append(("waits", waits))

    def wait_all(self, eng):
        waits = []
        wt = self.waited[eng]
        for sid, h in self.sems.items():
            final = None
            for e2, cs in self.csem.items():
                if cs[0] == sid:
                    final = cs[1]
            for e2, ds in self.dsem.items():
                if ds[0] == sid:
                    final = ds[1]
            if final and wt.get(sid, 0) < final:
                waits.append((sid, final))
                wt[sid] = final
        self.plan[eng].append(("waits", waits))

    def finalize(self, block):
        for eng in ENGINES:
            entries = self.plan[eng]
            if not entries:
                continue
            deco = getattr(block, eng)

            def body(e, entries=entries):
                for ent in entries:
                    if ent[0] == "waits":
                        for sid, val in ent[1]:
                            e.wait_ge(self.sems[sid], val)
                    else:
                        _, waits, fns, sid, inc = ent
                        for wsid, val in waits:
                            e.wait_ge(self.sems[wsid], val)
                        for i, fn in enumerate(fns):
                            instr = fn(e)
                            if i == len(fns) - 1:
                                instr.then_inc(self.sems[sid], inc)
            deco(body)


def emit_ln_tile(em, src, srck, ybf, ybfk, bufs):
    (s1, s1k), (ssq, ssqk), (sq, sqk), (st, stk), cb = bufs
    em.op("vector", lambda e: e.tensor_reduce(s1[:], src[:], AX.X, ALU.add),
          reads=[srck], writes=[s1k])
    em.op("scalar", lambda e: e.activation(sq[:], src[:], AF.Square, accum_out=ssq[:]),
          reads=[srck], writes=[sqk, ssqk])
    em.op("vector", lambda e: e.tensor_mul(st[:, 0:1], s1[:], s1[:]),
          reads=[s1k], writes=[stk])
    em.op("vector", lambda e: e.scalar_tensor_tensor(
        st[:, 1:2], st[:, 0:1], -1.0 / D, ssq[:], ALU.mult, ALU.add),
        reads=[stk, ssqk], writes=[stk])
    em.op("scalar", lambda e: e.activation(st[:, 2:3], st[:, 1:2], AF.Ln, bias=cb[:, 0:1]),
          reads=[stk, "lncb"], writes=[stk])
    em.op("scalar", lambda e: e.activation(st[:, 2:3], st[:, 2:3], AF.Exp, scale=-0.5, bias=cb[:, 1:2]),
          reads=[stk, "lncb"], writes=[stk])
    em.op("vector", lambda e: e.scalar_tensor_tensor(
        st[:, 3:4], s1[:], -1.0 / D, st[:, 2:3], ALU.mult, ALU.mult),
        reads=[s1k, stk], writes=[stk])
    em.op("vector", lambda e: e.tensor_scalar(
        ybf[:], src[:], st[:, 2:3], st[:, 3:4], ALU.mult, ALU.add),
        reads=[srck, stk], writes=[ybfk])


def emit_ln_stats(em, src, srck, st, c0, bufs):
    s1, ssq, sq, cb = bufs
    em.op("vector", lambda e: e.tensor_reduce(s1[:], src, AX.X, ALU.add),
          reads=[srck], writes=["s1"])
    em.op("scalar", lambda e: e.activation(sq[0][:], src, AF.Square, accum_out=ssq[:]),
          reads=[srck], writes=[sq[1], "ssq"])
    em.op("vector", lambda e: e.tensor_mul(st[:, 4:5], s1[:], s1[:]),
          reads=["s1"], writes=["st"])
    em.op("vector", lambda e: e.scalar_tensor_tensor(
        st[:, 5:6], st[:, 4:5], -1.0 / D, ssq[:], ALU.mult, ALU.add),
        reads=["st", "ssq"], writes=["st"])
    em.op("scalar", lambda e: e.activation(st[:, c0:c0 + 1], st[:, 5:6], AF.Ln, bias=cb[:, 0:1]),
          reads=["st", "lncb"], writes=["st"])
    em.op("scalar", lambda e: e.activation(st[:, c0:c0 + 1], st[:, c0:c0 + 1], AF.Exp,
                                           scale=-0.5, bias=cb[:, 1:2]),
          reads=["st", "lncb"], writes=["st"])
    em.op("vector", lambda e: e.scalar_tensor_tensor(
        st[:, c0 + 1:c0 + 2], s1[:], -1.0 / D, st[:, c0:c0 + 1], ALU.mult, ALU.mult),
        reads=["s1", "st"], writes=["st"])


def emit_phase_a(em, nc, es, xall, xc, x1, wqkvT, qkvb, projT, pbr, cosd, sind,
                 r2d, id16, tag=""):
    sb = lambda n, s, d: es.enter_context(nc.sbuf_tensor(tag + n, s, d))
    ps = lambda n, s, d: es.enter_context(nc.psum_tensor(tag + n, s, d))

    sscr = sb("sscr", [128, 8 * T], BF16)
    cosa = sb("cosa", [128, T], BF16)
    sina = sb("sina", [128, T], BF16)
    pjs = sb("pjs", [128, 8192], BF16)
    xlnT = [sb(f"xlnT{i}", [128, 8 * TC], BF16) for i in range(2)]
    wq3 = sb("wq3", [128, 3 * 1024], BF16)
    qTp = sb("qTp", [128, 4096], BF16)
    kTp = sb("kTp", [128, 4096], BF16)
    Vp = sb("Vp", [128, 2 * 32 * 65], BF16)
    r2 = sb("r2", [128, 128], BF16)
    sbid = sb("sbid", [128, 128], BF16)
    qkbs = sb("qkbs", [1, 384], F32)
    ones5 = sb("ones5", [1, 512], F32)
    wstg = [sb("wstg0", [128, 1024], F32)] * 2
    xt = [sb(f"xt{i}", [128, 1024], F32) for i in range(2)]
    xt4 = [xt[0], xt[1], sb("xt2", [128, 1024], F32), sb("xt3", [128, 1024], F32)]
    ot = sb("ot0", [128, 1024], F32)
    ybf2 = [sb(f"ybf{i}", [128, 1024], BF16) for i in range(2)]
    st4 = [sb(f"st4{i}", [128, 16], F32) for i in range(2)]
    lncb = sb("lncb", [128, 2], F32)
    qsb = sb("qsb", [128, 512], BF16)
    tcos = sb("tcos", [128, 512], F32)
    tsin = sb("tsin", [128, 512], F32)
    onec = sb("onec", [128, 1], BF16)
    oner = sb("oner", [1, 128], F32)
    nmx = sb("nmx", [1, 32], F32)
    cst = sb("cst", [1, 8], F32)
    bias_p = sb("bias_p", [128, 1], F32)
    pt = [sb(f"pt{i}", [128, 1024], BF16) for i in range(4)]
    linv = sb("linv", [1, 512], F32)
    lbc = sb("lbc", [128, 1024], F32)
    onorm = sb("onorm", [128, 512], F32)

    ps_s2 = [ps("ps_s2a", [128, 1024], F32), ps("ps_s2b", [128, 1024], F32)]
    ps_o2 = [ps("ps_oa", [128, 512], F32), ps("ps_ob", [128, 512], F32)]
    ps_bc = ps("ps_bc", [128, 512], F32)
    ps_t = ps("ps_t", [128, 1024], BF16)
    ps_a = ps_s2[0]
    ps_r = ps_s2[1]
    ps_pj = ps_s2[0]

    sq2 = [(wstg[0], "wstg0"), (ot, "ot0")]

    sscrv = sscr[:].rearrange("p (c n) -> p c n", n=T)
    pbs = None
    vstg = pt[2]
    q2 = pt[3]
    Vpv = Vp[:].rearrange("p (h s c) -> p h s c", h=2, s=32)

    em.dma("sync", lambda e: e.dma_start(out=sbid[:], in_=id16), writes=["sbid"])
    em.dma("sync", lambda e: e.dma_start(out=cosa[:], in_=cosd), writes=["cosa"])
    em.dma("sync", lambda e: e.dma_start(out=sina[:], in_=sind), writes=["sina"])
    em.op("vector", lambda e: e.memset(onec[:], 1.0), writes=["onec"])
    em.op("vector", lambda e: e.memset(oner[:], 1.0), writes=["oner"])
    em.op("vector", lambda e: e.memset(ones5[:], 1.0), writes=["ones5"])
    em.op("vector", lambda e: e.memset(lncb[:, 0:1], D * 1e-5), writes=["lncb"])
    em.op("vector", lambda e: e.memset(lncb[:, 1:2], LND), writes=["lncb"])
    em.op("vector", lambda e: e.memset(Vpv[:, :, :, 64:65], 1.0), writes=["Vp"])
    b, bk = wstg[0], "wstg0"
    em.dma("sync", lambda e: e.dma_start(out=b[:, 0:128], in_=r2d), writes=[bk])
    em.op("vector", lambda e: e.tensor_copy(r2[:], b[:, 0:128]), reads=[bk], writes=["r2"])
    em.dma("sync", lambda e: e.dma_start(out=b[0:1, 0:384], in_=qkvb),
           reads=["r2"], writes=[bk])
    em.op("vector", lambda e: e.tensor_copy(qkbs[:], b[0:1, 0:384]), reads=[bk], writes=["qkbs"])
    for t3 in range(3):
        em.dma("sync", lambda e, t3=t3: e.dma_start(
            out=b[:], in_=wqkvT[t3 * 128:(t3 + 1) * 128, :]), reads=["qkbs"], writes=[bk])
        em.op("vector", lambda e, t3=t3: e.tensor_copy(
            wq3[:, t3 * 1024:(t3 + 1) * 1024], b[:]), reads=[bk], writes=["wq3"])
    for kc in range(8):
        em.dma("sync", lambda e, kc=kc: e.dma_start(
            out=b[:], in_=projT[kc * 128:(kc + 1) * 128, :]), reads=["pjslast"], writes=[bk])
        em.op("vector", lambda e, kc=kc: e.tensor_copy(
            pjs[:, kc * 1024:(kc + 1) * 1024], b[:]), reads=[bk], writes=["pjslast"])

    def emit_stageA(bb2, g):
        st, stk = st4[g % 2], f"st4{g % 2}"
        for jj in range(4):
            row0 = bb2 * T + g * 512 + jj * 128
            xb, xbk = xt4[jj], f"xt{jj}"
            em.dma("sync", lambda e, row0=row0, xb=xb: e.dma_start(
                out=xb[:], in_=xall[row0:row0 + 128, :]), writes=[xbk])
            em.op("vector", lambda e, xb=xb, jj=jj, st=st: e.tensor_reduce(
                st[:, jj:jj + 1], xb[:], AX.X, ALU.add),
                reads=[xbk], writes=[stk])
            sq, sqk = sq2[g % 2]
            em.op("scalar", lambda e, xb=xb, jj=jj, st=st, sq=sq: e.activation(
                sq[:], xb[:], AF.Square, accum_out=st[:, 4 + jj:5 + jj]),
                reads=[xbk], writes=[sqk, stk])

    for bb in range(B):
        for g in range(NTC):
            xl, xlk = xlnT[g % 2], f"xlnT{g % 2}"
            xlv = xl[:].rearrange("p (c n) -> p c n", n=TC)
            def _stageB(g, xlv, xlk):
                st, stk = st4[g % 2], f"st4{g % 2}"
                em.op("vector", lambda e, st=st: e.tensor_mul(
                    st[:, 8:12], st[:, 0:4], st[:, 0:4]), reads=[stk], writes=[stk])
                em.op("vector", lambda e, st=st: e.scalar_tensor_tensor(
                    st[:, 8:12], st[:, 8:12], -1.0 / D, st[:, 4:8], ALU.mult, ALU.add),
                    reads=[stk], writes=[stk])
                em.op("scalar", lambda e, st=st: e.activation(
                    st[:, 8:12], st[:, 8:12], AF.Ln, bias=lncb[:, 0:1]),
                    reads=[stk, "lncb"], writes=[stk])
                em.op("scalar", lambda e, st=st: e.activation(
                    st[:, 8:12], st[:, 8:12], AF.Exp, scale=-0.5, bias=lncb[:, 1:2]),
                    reads=[stk, "lncb"], writes=[stk])
                em.op("vector", lambda e, st=st: e.scalar_tensor_tensor(
                    st[:, 12:16], st[:, 0:4], -1.0 / D, st[:, 8:12], ALU.mult, ALU.mult),
                    reads=[stk], writes=[stk])
                for jj in range(4):
                    xb, xbk = xt4[jj], f"xt{jj}"
                    ybf, ybfk = ybf2[jj % 2], f"ybf{jj % 2}"
                    em.op("vector", lambda e, xb=xb, jj=jj, st=st, ybf=ybf: e.tensor_scalar(
                        ybf[:], xb[:], st[:, 8 + jj:9 + jj], st[:, 12 + jj:13 + jj],
                        ALU.mult, ALU.add),
                        reads=[xbk, stk], writes=[ybfk])
                    em.op("tensor", [
                        (lambda e, k=k, ybf=ybf: e.transpose(ps_t[:, k * 128:(k + 1) * 128],
                                                    ybf[:, k * 128:(k + 1) * 128], sbid[:]))
                        for k in range(8)],
                        reads=[ybfk, "sbid"], writes=["ps_t"])
                    em.op("vector", lambda e, jj=jj, xlv=xlv: e.tensor_copy(
                        xlv[:, :, jj * 128: jj * 128 + 128],
                        ps_t[:].rearrange("p (c n) -> p c n", n=128)),
                        reads=["ps_t"], writes=[xlk])

            if g == 0 and bb == 0:
                emit_stageA(bb, 0)
            _stageB(g, xlv, xlk)
            if g < NTC - 1:
                emit_stageA(bb, g + 1)
            gc = slice(g * 512, (g + 1) * 512)
            for t3, dst, dstk in [(0, qTp, "qTp"), (1, kTp, "kTp")]:
                h0 = t3 * 512
                fns = [
                    (lambda e, kc=kc, t3=t3, xlv=xlv, h0=h0: e.matmul(
                        ps_a[:, h0:h0 + 512], wq3[:, t3 * 1024 + kc * 128: t3 * 1024 + (kc + 1) * 128],
                        xlv[:, kc, :], start=(kc == 0), stop=False))
                    for kc in range(8)]
                fns.append(lambda e, t3=t3, h0=h0: e.matmul(
                    ps_a[:, h0:h0 + 512], qkbs[:, t3 * 128:(t3 + 1) * 128], ones5[:],
                    start=False, stop=True))
                em.op("tensor", fns, reads=["wq3", xlk, "qkbs", "ones5"],
                      writes=["ps_s2a.lo" if t3 == 0 else "ps_s2a.hi"])
                half = 0 if t3 == 0 else 512
                hk = "ps_s2a.lo" if t3 == 0 else "ps_s2a.hi"
                rps = ps_o2[t3]
                rk = "ps_oa" if t3 == 0 else "ps_ob"
                qb, qbk = (qsb, "qsb") if (2 * g + t3) % 2 == 0 else (pt[0], "pt0")
                em.op("scalar", lambda e, half=half, qb=qb: e.copy(
                    qb[:, 0:512], ps_a[:, half:half + 512]),
                    reads=[hk], writes=[qbk])
                em.op("tensor", lambda e, rps=rps, qb=qb: e.matmul(
                    rps[:], r2[:], qb[:, 0:512], start=True, stop=True),
                      reads=["r2", qbk], writes=[rk])
                em.op("vector", lambda e, g=g, half=half: e.tensor_mul(
                    tcos[:], ps_a[:, half:half + 512], cosa[:, g * 512:(g + 1) * 512]),
                    reads=[hk, "cosa", rk], writes=["tcos"])
                em.op("vector", lambda e, g=g, rps=rps: e.tensor_mul(
                    tsin[:], rps[:], sina[:, g * 512:(g + 1) * 512]),
                    reads=[rk, "sina", hk], writes=["tsin"])
                em.op("vector", lambda e, dst=dst, gc=gc: e.tensor_add(
                    dst[:, gc], tcos[:], tsin[:]),
                    reads=["tcos", "tsin"], writes=[dstk])
            fns = [
                (lambda e, kc=kc, xlv=xlv: e.matmul(
                    ps_s2[1][:, 512:1024], wq3[:, 2 * 1024 + kc * 128: 2 * 1024 + (kc + 1) * 128],
                    xlv[:, kc, :], start=(kc == 0), stop=False))
                for kc in range(8)]
            fns.append(lambda e: e.matmul(
                ps_s2[1][:, 512:1024], qkbs[:, 256:384], ones5[:], start=False, stop=True))
            em.op("tensor", fns, reads=["wq3", xlk, "qkbs", "ones5"], writes=["ps_s2b.hi"])
            em.op("scalar", lambda e: e.copy(vstg[:, 0:512], ps_s2[1][:, 512:1024]),
                  reads=["ps_s2b.hi"], writes=["pt2"])
            em.op("tensor", [
                (lambda e, k=k: e.transpose(ps_t[:, k * 128:(k + 1) * 128],
                                            vstg[:, k * 128:(k + 1) * 128], sbid[:]))
                for k in range(4)],
                reads=["pt2", "sbid"], writes=["ps_t"])
            for k in range(4):
                sbk = g * 4 + k
                em.op("vector", lambda e, k=k, sbk=sbk: e.tensor_copy(
                    Vpv[:, 0, sbk, 0:64], ps_t[:, k * 128:k * 128 + 64]),
                    reads=["ps_t"], writes=["Vp"])
                em.op("vector", lambda e, k=k, sbk=sbk: e.tensor_copy(
                    Vpv[:, 1, sbk, 0:64], ps_t[:, k * 128 + 64:k * 128 + 128]),
                    reads=["ps_t"], writes=["Vp"])
        for src, srck, off in [(qTp, "qTp", 0), (kTp, "kTp", 8)]:
            for tc in range(NTC):
                qb, qbk = (q2, "pt3") if tc % 2 == 0 else (pt[1], "pt1")
                em.op("scalar", lambda e, src=src, tc=tc, qb=qb: e.activation(
                    qb[:, 0:512], src[:, tc * TC:(tc + 1) * TC], AF.Square),
                    reads=[srck], writes=[qbk])
                em.op("tensor", lambda e, qb=qb: e.matmul(
                    ps_bc[0:1, :], onec[:], qb[:, 0:512], start=True, stop=True),
                    reads=[qbk, "onec"], writes=["ps_bc"])
                em.op("vector", lambda e, tc=tc, off=off: e.tensor_reduce(
                    nmx[:, off + tc:off + tc + 1], ps_bc[0:1, :], AX.X, ALU.max),
                    reads=["ps_bc"], writes=["nmx"])
        em.op("vector", lambda e: e.tensor_reduce(cst[:, 0:1], nmx[:, 0:8], AX.X, ALU.max),
              reads=["nmx"], writes=["cst"])
        em.op("vector", lambda e: e.tensor_reduce(cst[:, 1:2], nmx[:, 8:16], AX.X, ALU.max),
              reads=["nmx"], writes=["cst"])
        em.op("vector", lambda e: e.tensor_mul(cst[:, 2:3], cst[:, 0:1], cst[:, 1:2]),
              reads=["cst"], writes=["cst"])
        em.op("scalar", lambda e: e.activation(cst[:, 3:4], cst[:, 2:3], AF.Ln),
              reads=["cst"], writes=["cst"])
        em.op("scalar", lambda e: e.activation(cst[:, 4:5], cst[:, 3:4], AF.Exp, scale=0.5),
              reads=["cst"], writes=["cst"])
        em.op("vector", lambda e: e.tensor_scalar(
            cst[:, 5:6], cst[:, 4:5], -1.02 / 8.0, -1e-3, ALU.mult, ALU.add),
            reads=["cst"], writes=["cst"])
        em.op("tensor", lambda e: e.matmul(ps_bc[:, 0:1], oner[:], cst[:, 5:6],
                                           start=True, stop=True),
              reads=["oner", "cst"], writes=["ps_bc"])
        em.op("scalar", lambda e: e.copy(bias_p[:], ps_bc[:, 0:1]),
              reads=["ps_bc"], writes=["bias_p"])

        pvq = []
        epiq = []
        def _flush(queue, n_keep=0):
            while len(queue) > n_keep:
                eng, fn, rd, wr = queue.pop(0)
                em.op(eng, fn, reads=rd, writes=wr)
        zu = tcos
        lrs = [(onorm[0:1, 0:512], "onorm"), (lbc[0:1, 512:1024], "lbc")]
        for qc in range(NTC):
            if qc == NTC - 1 and bb + 1 < B:
                emit_stageA(bb + 1, 0)
            for sbi in range(32):
                if sbi >= 2:
                    _flush(epiq, max(0, len(epiq) - 2) if sbi < 31 else 0)
                par = sbi % 2
                pss = ps_s2[par]
                ks = [f"ps_s2{'ab'[par]}.lo", f"ps_s2{'ab'[par]}.hi"]
                pb, pbk = pt[sbi % 4], f"pt{sbi % 4}"
                em.op("tensor", [
                    (lambda e, sbi=sbi, hh=hh, qc=qc, pss=pss: e.matmul(
                        pss[:, hh * TC:(hh + 1) * TC],
                        kTp[hh * 64:hh * 64 + 64, sbi * 128:(sbi + 1) * 128],
                        qTp[hh * 64:hh * 64 + 64, qc * TC:(qc + 1) * TC],
                        start=True, stop=True))
                    for hh in range(2)],
                    reads=["kTp", "qTp"], writes=ks)
                em.op("scalar", lambda e, pb=pb, pss=pss: e.activation(
                    pb[:], pss[:], AF.Exp, bias=bias_p[:, 0:1], scale=0.125),
                    reads=ks + ["bias_p"], writes=[pbk])
                pvq.append(("tensor", [
                    (lambda e, sbi=sbi, hh=hh, pb=pb: e.matmul(
                        ps_o2[hh][0:65, :], Vpv[:, hh, sbi, :],
                        pb[:, hh * TC:(hh + 1) * TC],
                        start=(sbi == 0), stop=(sbi == 31)))
                    for hh in range(2)],
                    ["Vp", pbk], ["ps_oa", "ps_ob"]))
                _flush(pvq, 3)
            _flush(pvq)
            em.op("vector", lambda e: e.tensor_copy(zu[0:64, :], ps_o2[0][0:64, :]),
                  reads=["ps_oa"], writes=["tcos"])
            em.op("vector", lambda e: e.tensor_copy(zu[64:128, :], ps_o2[1][0:64, :]),
                  reads=["ps_ob"], writes=["tcos"])
            em.op("vector", lambda e: e.tensor_copy(lrs[0][0], ps_o2[0][64:65, :]),
                  reads=["ps_oa"], writes=["onorm"])
            em.op("vector", lambda e: e.tensor_copy(lrs[1][0], ps_o2[1][64:65, :]),
                  reads=["ps_ob"], writes=["onorm"])
            for hh in range(2):
                r0 = hh * 64
                lv, lvk = lrs[hh]
                lb = lbc[r0:r0 + 64, hh * 512:hh * 512 + 512]
                pbc = ps_bc[r0:r0 + 64, :]
                tpb = (hh % 2) * 2048 + bb * 256
                em.op("scalar", lambda e, lv=lv: e.activation(lv, lv, AF.Ln),
                      reads=[lvk], writes=[lvk])
                em.op("scalar", lambda e, lv=lv: e.activation(
                    lv, lv, AF.Exp, scale=-1.0), reads=[lvk], writes=[lvk])
                epiq.append(("tensor", (lambda e, lv=lv, pbc=pbc: e.matmul(
                    pbc, oner[:, 0:64], lv, start=True, stop=True)),
                    ["oner", lvk], ["ps_bc"]))
                epiq.append(("vector", (lambda e, lb=lb, pbc=pbc: e.tensor_copy(lb, pbc)),
                    ["ps_bc"], ["lbc"]))
                epiq.append(("vector", (lambda e, r0=r0, lb=lb: e.tensor_mul(
                    onorm[r0:r0 + 64, :], zu[r0:r0 + 64, :], lb)),
                    ["tcos", "lbc"], ["onorm"]))
                for ql in range(16):
                    p0 = (ql % 2) * 64
                    ch = ql // 2
                    tcol = tpb + qc * 32
                    epiq.append(("gpsimd", (lambda e, ql=ql, p0=p0, ch=ch, tcol=tcol, r0=r0: e.tensor_copy(
                        sscrv[p0:p0 + 64, ch, tcol:tcol + 32],
                        onorm[r0:r0 + 64, :].rearrange("p (a b) -> p a b", b=16)[:, :, ql])),
                        ["onorm"], ["sscr"]))
        _flush(epiq)
    pbs = lbc
    em.dma("sync", lambda e: e.dma_start(out=pbs[0:1, :], in_=pbr), writes=["lbc"])
    for j in range(32):
        xb, xbk = xt[j % 2], f"xt{j % 2}"
        em.dma("sync", lambda e, j=j, xb=xb: e.dma_start(
            out=xb[:], in_=xc[j * 128:(j + 1) * 128, :]), writes=[xbk])
        fns = []
        for oc in range(2):
            fns += [
                (lambda e, kc=kc, j=j, oc=oc: e.matmul(
                    ps_pj[:, oc * 512:(oc + 1) * 512], sscrv[:, kc, j * 128:(j + 1) * 128],
                    pjs[:, kc * 1024 + oc * 512: kc * 1024 + (oc + 1) * 512],
                    start=(kc == 0), stop=False))
                for kc in range(8)]
            fns.append(lambda e, oc=oc: e.matmul(
                ps_pj[:, oc * 512:(oc + 1) * 512], oner[:, 0:128], pbs[0:1, oc * 512:(oc + 1) * 512],
                start=False, stop=True))
        em.op("tensor", fns, reads=["sscr", "pjslast", "oner", "lbc"],
              writes=["ps_s2a.lo", "ps_s2a.hi"])
        em.op("vector", lambda e, xb=xb: e.tensor_add(ot[:], xb[:], ps_pj[:]),
              reads=["ps_s2a.lo", "ps_s2a.hi", xbk], writes=["ot0"])
        em.dma("sync", lambda e, j=j: e.dma_start(
            out=x1[j * 128:(j + 1) * 128, :], in_=ot[:]), reads=["ot0"],
            writes=["g.x1"], slot="st_ot0")


def emit_phase_b(em, nc, es, x1, x2, qwT, kwT, vwT, owT, qkvb8, wbc, bbc, obr, id16, tag=""):
    sb = lambda n, s, d: es.enter_context(nc.sbuf_tensor(tag + n, s, d))
    ps = lambda n, s, d: es.enter_context(nc.psum_tensor(tag + n, s, d))

    xiT = sb("xiT", [128, 8 * T], BF16)
    aoT = sb("aoT", [128, 8 * T], BF16)
    sbid = sb("sbid", [128, 128], BF16)
    wstg = [sb("wstg0", [128, 1024], F32)] * 2
    qwh = sb("qwh", [128, 1024], BF16)
    kwh = sb("kwh", [128, 1024], BF16)
    vwh = sb("vwh", [128, 1024], BF16)
    qkb = sb("qkb", [128, 24], F32)
    wbcs = sb("wbcs", [128, 1024], F32)
    bbcs = sb("bbcs", [128, 1024], F32)
    obs = sb("obs", [1, 1024], F32)
    xt = [sb(f"xt{i}", [128, 1024], F32) for i in range(2)]
    ot = sb("ot0", [128, 1024], F32)
    xmt = sb("xmt", [128, 1024], F32)
    ybf = sb("ybf", [128, 1024], BF16)
    s1 = sb("s1", [128, 1], F32)
    ssq = sb("ssq", [128, 1], F32)
    st = sb("st", [128, 8], F32)
    lncb = sb("lncb", [128, 2], F32)
    sstash = sb("sstash", [128, 64], F32)
    qTh = sb("qTh", [128, 4096], BF16)
    kTh = sb("kTh", [128, 4096], BF16)
    Vh = sb("Vh", [128, 4096], BF16)
    lg = [sb(f"lg{i}", [128, 512], BF16) for i in range(2)]
    onec = sb("onec", [128, 1], BF16)
    oner = sb("oner", [1, 128], F32)
    nmx = sb("nmx", [1, 32], F32)
    cst = sb("cst", [1, 8], F32)
    bias_h = sb("bias_h", [128, 1], F32)
    pt = [sb(f"pt{i}", [128, 1024], BF16) for i in range(4)]
    linv = sb("linv", [1, 512], F32)
    lbc = sb("lbc", [128, 512], F32)

    ps_s2 = [ps("ps_s2a", [128, 1024], F32), ps("ps_s2b", [128, 1024], F32)]
    ps_o = ps("ps_o", [128, 512], F32)
    ps_l = ps("ps_l", [128, 512], F32)
    ps_t = ps("ps_t", [128, 1024], BF16)
    ps_a = ps_s2[0]
    ps_bc = ps_l
    ps_pj = ps_s2[0]

    sq = (wstg[0], "wstg0")
    lnbufs = (s1, ssq, sq, lncb)

    owTs = xiT
    xiTv = xiT[:].rearrange("p (c n) -> p c n", n=T)
    aoTv = aoT[:].rearrange("p (c n) -> p c n", n=T)
    Vhv = Vh[:].rearrange("p (s c) -> p s c", s=32)
    vstg = pt[2]
    q2 = pt[3]
    zu = wstg[0]

    em.dma("sync", lambda e: e.dma_start(out=sbid[:], in_=id16), writes=["sbid"])
    em.dma("sync", lambda e: e.dma_start(out=qkb[:], in_=qkvb8), writes=["qkb"])
    em.dma("sync", lambda e: e.dma_start(out=wbcs[:], in_=wbc), writes=["wbcs"])
    em.dma("sync", lambda e: e.dma_start(out=bbcs[:], in_=bbc), writes=["bbcs"])
    em.dma("sync", lambda e: e.dma_start(out=obs[:], in_=obr), writes=["obs"])
    em.op("vector", lambda e: e.memset(onec[:], 1.0), writes=["onec"])
    em.op("vector", lambda e: e.memset(oner[:], 1.0), writes=["oner"])
    em.op("vector", lambda e: e.memset(lncb[:, 0:1], D * 1e-5), writes=["lncb"])
    em.op("vector", lambda e: e.memset(lncb[:, 1:2], LND), writes=["lncb"])
    for j in range(NT):
        b, bk = xt[j % 2], f"xt{j % 2}"
        em.dma("sync", lambda e, j=j, b=b: e.dma_start(
            out=b[:], in_=x1[j * 128:(j + 1) * 128, :]), reads=["g.x1"], writes=[bk])
        emit_ln_stats(em, b[:], bk, st, 0, lnbufs)
        em.op("vector", lambda e, j=j: e.tensor_scalar(
            sstash[:, 2 * j:2 * j + 1], st[:, 0:1], 1.0, 1.0, ALU.mult, ALU.add),
            reads=["st"], writes=["sstash"])
        em.op("vector", lambda e, j=j: e.tensor_copy(
            sstash[:, 2 * j + 1:2 * j + 2], st[:, 1:2]),
            reads=["st"], writes=["sstash"])
        em.op("vector", lambda e, b=b: e.tensor_scalar(
            xmt[:], b[:], st[:, 0:1], st[:, 1:2], ALU.mult, ALU.add),
            reads=[bk, "st"], writes=["xmt"])
        emit_ln_stats(em, xmt[:], "xmt", st, 2, lnbufs)
        em.op("vector", lambda e: e.tensor_scalar(
            xmt[:], xmt[:], st[:, 2:3], st[:, 3:4], ALU.mult, ALU.add),
            reads=["xmt", "st"], writes=["xmt"])
        em.op("vector", lambda e: e.tensor_mul(xmt[:], xmt[:], wbcs[:]),
              reads=["xmt", "wbcs"], writes=["xmt"])
        em.op("vector", lambda e: e.tensor_add(ybf[:], xmt[:], bbcs[:]),
              reads=["xmt", "bbcs"], writes=["ybf"])
        em.op("tensor", [
            (lambda e, k=k: e.transpose(ps_t[:, k * 128:(k + 1) * 128],
                                        ybf[:, k * 128:(k + 1) * 128], sbid[:]))
            for k in range(8)],
            reads=["ybf", "sbid"], writes=["ps_t"])
        em.op("vector", lambda e, j=j: e.tensor_copy(
            xiTv[:, :, j * 128: j * 128 + 128],
            ps_t[:].rearrange("p (c n) -> p c n", n=128)),
            reads=["ps_t"], writes=["xiT"])

    for h in range(8):
        for wdram, slab, sk in [(qwT, qwh, "qwh"), (kwT, kwh, "kwh"), (vwT, vwh, "vwh")]:
            b, bk = wstg[0], "wstg0"
            em.dma("sync", lambda e, wdram=wdram, h=h, b=b: e.dma_start(
                out=b[:], in_=wdram[h * 128:(h + 1) * 128, :]), writes=[bk])
            em.op("vector", lambda e, slab=slab, b=b: e.tensor_copy(slab[:], b[:]),
                  reads=[bk], writes=[sk])
        for tc in range(NTC):
            ex = tc // 2
            nkc = 8 >> ex
            em.op("tensor", [
                (lambda e, kc=kc, tc=tc, last=(kc == nkc - 1): e.matmul(
                    ps_a[:, 0:512], qwh[:, kc * 128:(kc + 1) * 128],
                    xiTv[:, kc, tc * TC:(tc + 1) * TC], start=(kc == 0), stop=last))
                for kc in range(nkc)],
                reads=["qwh", "xiT"], writes=["ps_s2a"])
            em.op("vector", lambda e, tc=tc, h=h: e.tensor_scalar_add(
                qTh[:, tc * TC:(tc + 1) * TC], ps_a[:, 0:512], qkb[:, h:h + 1]),
                reads=["ps_s2a", "qkb"], writes=["qTh"])
            em.op("tensor", [
                (lambda e, kc=kc, tc=tc, last=(kc == nkc - 1): e.matmul(
                    ps_a[:, 512:1024], kwh[:, kc * 128:(kc + 1) * 128],
                    xiTv[:, kc, tc * TC:(tc + 1) * TC], start=(kc == 0), stop=last))
                for kc in range(nkc)],
                reads=["kwh", "xiT"], writes=["ps_s2a"])
            em.op("vector", lambda e, tc=tc, h=h: e.tensor_scalar_add(
                kTh[:, tc * TC:(tc + 1) * TC], ps_a[:, 512:1024], qkb[:, 8 + h:8 + h + 1]),
                reads=["ps_s2a", "qkb"], writes=["kTh"])
            em.op("tensor", [
                (lambda e, kc=kc, tc=tc, last=(kc == nkc - 1): e.matmul(
                    ps_s2[1][:, 0:512], vwh[:, kc * 128:(kc + 1) * 128],
                    xiTv[:, kc, tc * TC:(tc + 1) * TC], start=(kc == 0), stop=last))
                for kc in range(nkc)],
                reads=["vwh", "xiT"], writes=["ps_s2b"])
            em.op("vector", lambda e, tc=tc, h=h: e.tensor_scalar_add(
                vstg[:, 0:512], ps_s2[1][:, 0:512], qkb[:, 16 + h:16 + h + 1]),
                reads=["ps_s2b", "qkb"], writes=["pt2"])
            em.op("tensor", [
                (lambda e, k=k: e.transpose(ps_t[:, k * 128:(k + 1) * 128],
                                            vstg[:, k * 128:(k + 1) * 128], sbid[:]))
                for k in range(4)],
                reads=["pt2", "sbid"], writes=["ps_t"])
            em.op("vector", lambda e, tc=tc: e.tensor_copy(
                Vhv[:, tc * 4:tc * 4 + 4, :],
                ps_t[:, 0:512].rearrange("p (s c) -> p s c", c=128)),
                reads=["ps_t"], writes=["Vh"])
        for src, srck, off in [(qTh, "qTh", 0), (kTh, "kTh", 8)]:
            for tc in range(NTC):
                qb, qbk = (q2, "pt3") if tc % 2 == 0 else (pt[1], "pt1")
                em.op("scalar", lambda e, src=src, tc=tc, qb=qb: e.activation(
                    qb[:, 0:512], src[:, tc * TC:(tc + 1) * TC], AF.Square),
                    reads=[srck], writes=[qbk])
                em.op("tensor", lambda e, qb=qb: e.matmul(
                    ps_bc[0:1, :], onec[:], qb[:, 0:512], start=True, stop=True),
                    reads=[qbk, "onec"], writes=["ps_l"])
                em.op("vector", lambda e, tc=tc, off=off: e.tensor_reduce(
                    nmx[:, off + tc:off + tc + 1], ps_bc[0:1, :], AX.X, ALU.max),
                    reads=["ps_l"], writes=["nmx"])
        em.op("vector", lambda e: e.tensor_reduce(cst[:, 0:1], nmx[:, 0:8], AX.X, ALU.max),
              reads=["nmx"], writes=["cst"])
        em.op("vector", lambda e: e.tensor_reduce(cst[:, 1:2], nmx[:, 8:16], AX.X, ALU.max),
              reads=["nmx"], writes=["cst"])
        em.op("vector", lambda e: e.tensor_mul(cst[:, 2:3], cst[:, 0:1], cst[:, 1:2]),
              reads=["cst"], writes=["cst"])
        em.op("scalar", lambda e: e.activation(cst[:, 3:4], cst[:, 2:3], AF.Ln),
              reads=["cst"], writes=["cst"])
        em.op("scalar", lambda e: e.activation(cst[:, 4:5], cst[:, 3:4], AF.Exp, scale=0.5),
              reads=["cst"], writes=["cst"])
        em.op("vector", lambda e: e.tensor_scalar(
            cst[:, 5:6], cst[:, 4:5], -1.02 / 32.0, -1e-3, ALU.mult, ALU.add),
            reads=["cst"], writes=["cst"])
        em.op("tensor", lambda e: e.matmul(ps_bc[:, 0:1], oner[:], cst[:, 5:6],
                                           start=True, stop=True),
              reads=["oner", "cst"], writes=["ps_l"])
        em.op("scalar", lambda e: e.copy(bias_h[:], ps_bc[:, 0:1]),
              reads=["ps_l"], writes=["bias_h"])

        pvq = []
        epiq = []
        def _flush(queue, n_keep=0):
            while len(queue) > n_keep:
                eng, fn, rd, wr = queue.pop(0)
                em.op(eng, fn, reads=rd, writes=wr)
        for qc in range(NTC):
            for sp in range(0, 32, 2):
                if sp >= 2:
                    _flush(epiq, max(0, len(epiq) - 2) if sp < 30 else 0)
                pi = (sp // 2) % 2
                pss, pssk = ps_s2[pi], f"ps_s2{'ab'[pi]}"
                pb, pbk = pt[(sp // 2) % 4], f"pt{(sp // 2) % 4}"
                em.op("tensor", [
                    (lambda e, sbi=sp + d, d=d, qc=qc, pss=pss: e.matmul(
                        pss[:, d * TC:(d + 1) * TC],
                        kTh[:, sbi * 128:(sbi + 1) * 128],
                        qTh[:, qc * TC:(qc + 1) * TC], start=True, stop=True))
                    for d in range(2)],
                    reads=["kTh", "qTh"], writes=[pssk])
                em.op("scalar", lambda e, pb=pb, pss=pss: e.activation(
                    pb[:], pss[:], AF.Exp, bias=bias_h[:, 0:1], scale=1.0 / 32.0),
                    reads=[pssk, "bias_h"], writes=[pbk])
                gi = sp // 16
                if sp % 16 == 0:
                    em.op("vector", lambda e, pb=pb, gi=gi: e.tensor_add(
                        lg[gi][:], pb[:, 0:512], pb[:, 512:1024]),
                        reads=[pbk], writes=[f"lg{gi}"])
                else:
                    em.op("vector", lambda e, pb=pb, gi=gi: e.tensor_add(
                        lg[gi][:], lg[gi][:], pb[:, 0:512]),
                        reads=[pbk, f"lg{gi}"], writes=[f"lg{gi}"])
                    em.op("vector", lambda e, pb=pb, gi=gi: e.tensor_add(
                        lg[gi][:], lg[gi][:], pb[:, 512:1024]),
                        reads=[pbk, f"lg{gi}"], writes=[f"lg{gi}"])
                fns = [
                    (lambda e, sbi=sp + d, d=d, pb=pb: e.matmul(
                        ps_o[:], Vhv[:, sbi, :], pb[:, d * TC:(d + 1) * TC],
                        start=(sbi == 0), stop=(sbi == 31)))
                    for d in range(2)]
                pvq.append(("tensor", fns, ["Vh", pbk], ["ps_o"]))
                while len(pvq) > 1:
                    eng, fn, rd, wr = pvq.pop(0)
                    em.op(eng, fn, reads=rd, writes=wr)
            _flush(pvq)
            em.op("vector", lambda e: e.tensor_add(lg[0][:], lg[0][:], lg[1][:]),
                  reads=["lg0", "lg1"], writes=["lg0"])
            em.op("tensor", lambda e: e.matmul(
                ps_l[0:1, :], onec[:], lg[0][:], start=True, stop=True),
                reads=["lg0", "onec"], writes=["ps_l"])
            em.op("vector", lambda e: e.tensor_copy(zu[:, 0:512], ps_o[:]),
                  reads=["ps_o"], writes=["wstg0"])
            em.op("scalar", lambda e: e.activation(linv[:], ps_l[0:1, :], AF.Ln),
                  reads=["ps_l"], writes=["linv"])
            em.op("scalar", lambda e: e.activation(linv[:], linv[:], AF.Exp, scale=-1.0),
                  reads=["linv"], writes=["linv"])
            em.op("tensor", lambda e: e.matmul(
                ps_bc[:, :], oner[:], linv[:], start=True, stop=True),
                reads=["oner", "linv"], writes=["ps_l"])
            em.op("vector", lambda e: e.tensor_copy(lbc[:], ps_bc[:, :]),
                  reads=["ps_l"], writes=["lbc"])
            epiq.append(("vector", (lambda e, h=h, qc=qc: e.tensor_mul(
                aoTv[:, h, qc * TC:(qc + 1) * TC], zu[:, 0:512], lbc[:])),
                ["wstg0", "lbc"], ["aoT"]))
        _flush(epiq)
    for kc in range(8):
        b, bk = wstg[0], "wstg0"
        em.dma("sync", lambda e, kc=kc, b=b: e.dma_start(
            out=b[:], in_=owT[kc * 128:(kc + 1) * 128, :]), writes=[bk])
        em.op("vector", lambda e, kc=kc, b=b: e.tensor_copy(
            owTs[:, kc * 1024:(kc + 1) * 1024], b[:]), reads=[bk], writes=["xiT"])

    for j in range(NT):
        ex = j // 8
        m = 1024 >> ex
        nkc = 8 >> ex
        b, bk = xt[j % 2], f"xt{j % 2}"
        em.dma("sync", lambda e, j=j, b=b: e.dma_start(
            out=b[:], in_=x1[j * 128:(j + 1) * 128, :]), reads=["g.x1"], writes=[bk])
        em.op("vector", lambda e, j=j, b=b: e.tensor_scalar(
            ot[:], b[:], sstash[:, 2 * j:2 * j + 1], sstash[:, 2 * j + 1:2 * j + 2],
            ALU.mult, ALU.add),
            reads=[bk, "sstash"], writes=["ot0"])
        fns = []
        for oc in range((m + 511) // 512):
            ow = min(512, m - oc * 512)
            fns += [
                (lambda e, kc=kc, j=j, oc=oc, ow=ow: e.matmul(
                    ps_pj[:, oc * 512:oc * 512 + ow], aoTv[:, kc, j * 128:(j + 1) * 128],
                    owTs[:, kc * 1024 + oc * 512: kc * 1024 + oc * 512 + ow],
                    start=(kc == 0), stop=False))
                for kc in range(nkc)]
            fns.append(lambda e, j=j, oc=oc, ow=ow: e.matmul(
                ps_pj[:, oc * 512:oc * 512 + ow], oner[:, 0:128], obs[:, oc * 512:oc * 512 + ow],
                start=False, stop=True))
        em.op("tensor", fns, reads=["aoT", "xiT", "oner", "obs"], writes=["ps_s2a"])
        em.op("vector", lambda e, m=m: e.tensor_add(
            ot[:, 0:m], ot[:, 0:m], ps_pj[:, 0:m]),
            reads=["ps_s2a", "ot0"], writes=["ot0"])
        em.dma("sync", lambda e, j=j: e.dma_start(
            out=x2[j * 128:(j + 1) * 128, :], in_=ot[:]), reads=["ot0"],
            writes=["g.x2"], slot="st_ot0")


def _mk_qkv_evac(qTp, kTp, vstg, Vpv, ps_t, sbid, TCc):
    def _emit(em, job):
        kind, tc, psx, half, pxk = job
        if kind == "q":
            em.op("scalar", lambda e: e.copy(qTp[:, tc * TCc:(tc + 1) * TCc], psx[:, half:half + 512]),
                  reads=[pxk], writes=["qTp"])
        elif kind == "k":
            em.op("scalar", lambda e: e.copy(kTp[:, tc * TCc:(tc + 1) * TCc], psx[:, half:half + 512]),
                  reads=[pxk], writes=["kTp"])
        else:
            em.op("scalar", lambda e: e.copy(vstg[:, 0:512], psx[:, half:half + 512]),
                  reads=[pxk], writes=["pt2"])
            em.op("tensor", [
                (lambda e, k=k: e.transpose(ps_t[:, k * 128:(k + 1) * 128],
                                            vstg[:, k * 128:(k + 1) * 128], sbid[:]))
                for k in range(4)],
                reads=["pt2", "sbid"], writes=["ps_t"])
            for k in range(4):
                sbk = tc * 4 + k
                em.op("vector", lambda e, k=k, sbk=sbk: e.tensor_copy(
                    Vpv[:, 0, sbk, 0:64], ps_t[:, k * 128:k * 128 + 64]),
                    reads=["ps_t"], writes=["Vp"])
                em.op("vector", lambda e, k=k, sbk=sbk: e.tensor_copy(
                    Vpv[:, 1, sbk, 0:64], ps_t[:, k * 128 + 64:k * 128 + 128]),
                    reads=["ps_t"], writes=["Vp"])
    return _emit


def emit_phase_c(em, nc, es, x2, x3, wq, wk, wv, wo, bob, id16, tag=""):
    sb = lambda n, s, d: es.enter_context(nc.sbuf_tensor(tag + n, s, d))
    ps = lambda n, s, d: es.enter_context(nc.psum_tensor(tag + n, s, d))

    xiT = sb("xiT", [128, 8 * T], BF16)
    zT = sb("zT", [128, 8 * T], BF16)
    wos = sb("wos", [128, 8192], BF16)
    bo = sb("bo", [1, 1024], F32)
    sbid = sb("sbid", [128, 128], BF16)
    wstg = [sb("wstg0", [128, 1024], F32)] * 2
    wqp = sb("wqp", [128, 1024], BF16)
    wkp = sb("wkp", [128, 1024], BF16)
    wvp = sb("wvp", [128, 1024], BF16)
    xt = [sb(f"xt{i}", [128, 1024], F32) for i in range(2)]
    ot = [sb("ot0", [128, 1024], F32)] * 2
    ybf2 = [sb(f"ybf{i}", [128, 1024], BF16) for i in range(2)]
    s12 = [sb(f"s1{i}", [128, 1], F32) for i in range(2)]
    ssq2 = [sb(f"ssq{i}", [128, 1], F32) for i in range(2)]
    st2 = [sb(f"st{i}", [128, 8], F32) for i in range(2)]
    qTp = sb("qTp", [128, 4096], BF16)
    kTp = sb("kTp", [128, 4096], BF16)
    Vp = sb("Vp", [128, 2 * 32 * 65], BF16)
    onec = sb("onec", [128, 1], BF16)
    oner = sb("oner", [1, 128], F32)
    nmx = sb("nmx", [1, 32], F32)
    cst = sb("cst", [1, 8], F32)
    bias_p = sb("bias_p", [128, 1], F32)
    pt = [sb(f"pt{i}", [128, 1024], BF16) for i in range(4)]

    ps_s2 = [ps("ps_s2a", [128, 1024], F32), ps("ps_s2b", [128, 1024], F32)]
    ps_o2 = [ps("ps_oa", [128, 512], F32), ps("ps_ob", [128, 512], F32)]
    ps_bc = ps("ps_bc", [128, 512], F32)
    ps_t = ps("ps_t", [128, 1024], BF16)
    ps_pj = ps_s2[0]

    lncb = sb("lncb", [128, 2], F32)
    lnsets = [
        ((s12[0], "s10"), (ssq2[0], "ssq0"), (wstg[0], "wstg0"), (st2[0], "st0"), lncb),
        ((s12[1], "s11"), (ssq2[1], "ssq1"), (ot[0], "ot0"), (st2[1], "st1"), lncb),
    ]

    xiTv = xiT[:].rearrange("p (c n) -> p c n", n=T)
    zTv = zT[:].rearrange("p (c n) -> p c n", n=T)
    Vpv = Vp[:].rearrange("p (h s c) -> p h s c", h=2, s=32)
    vstg = pt[2]
    q2 = pt[3]
    lbc = wstg[0]
    linv2 = [(xt[0], "xt0"), (xt[1], "xt1")]
    _emit_qkv_evac = _mk_qkv_evac(qTp, kTp, vstg, Vpv, ps_t, sbid, TC)

    em.dma("sync", lambda e: e.dma_start(out=sbid[:], in_=id16), writes=["sbid"])
    em.dma("sync", lambda e: e.dma_start(out=bo[:], in_=bob[0:1, :]), writes=["bo"])
    em.op("vector", lambda e: e.memset(onec[:], 1.0), writes=["onec"])
    em.op("vector", lambda e: e.memset(lncb[:, 0:1], D * 1e-5), writes=["lncb"])
    em.op("vector", lambda e: e.memset(lncb[:, 1:2], LND), writes=["lncb"])
    em.op("vector", lambda e: e.memset(oner[:], 1.0), writes=["oner"])
    em.op("vector", lambda e: e.memset(Vpv[:, :, :, 64:65], 1.0), writes=["Vp"])
    for kc in range(8):
        b = wstg[0]
        bk = "wstg0"
        em.dma("sync", lambda e, kc=kc, b=b: e.dma_start(
            out=b[:], in_=wo[kc * 128:(kc + 1) * 128, :]), writes=[bk])
        em.op("vector", lambda e, kc=kc, b=b: e.tensor_copy(
            wos[:, kc * 1024:(kc + 1) * 1024], b[:]), reads=[bk], writes=["wos"])

    for j in range(NT):
        b = xt[j % 2]
        bk = f"xt{j % 2}"
        em.dma("sync", lambda e, j=j, b=b: e.dma_start(
            out=b[:], in_=x2[j * 128:(j + 1) * 128, :]), reads=["g.x2"], writes=[bk])
        ybf, ybfk = ybf2[j % 2], f"ybf{j % 2}"
        emit_ln_tile(em, b, bk, ybf, ybfk, lnsets[j % 2])
        em.op("tensor", [
            (lambda e, k=k, ybf=ybf: e.transpose(ps_t[:, k * 128:(k + 1) * 128],
                                        ybf[:, k * 128:(k + 1) * 128], sbid[:]))
            for k in range(8)],
            reads=[ybfk, "sbid"], writes=["ps_t"])
        em.op("vector", lambda e, j=j: e.tensor_copy(
            xiTv[:, :, j * 128: j * 128 + 128],
            ps_t[:].rearrange("p (c n) -> p c n", n=128)),
            reads=["ps_t"], writes=["xiT"])

    for hp in range(8):
        for wdram, slab, sk in [(wq, wqp, "wqp"), (wk, wkp, "wkp"), (wv, wvp, "wvp")]:
            b = wstg[0]
            bk = "wstg0"
            em.dma("sync", lambda e, wdram=wdram, hp=hp, b=b: e.dma_start(
                out=b[:], in_=wdram[hp * 128:(hp + 1) * 128, :]), writes=[bk])
            em.op("vector", lambda e, slab=slab, b=b: e.tensor_copy(slab[:], b[:]),
                  reads=[bk], writes=[sk])
        qkv_jobs = []
        for tc in range(NTC):
            qkv_jobs.append(("q", tc, wqp, "wqp"))
            qkv_jobs.append(("k", tc, wkp, "wkp"))
            qkv_jobs.append(("v", tc, wvp, "wvp"))
        pending = None
        for gi, (kind, tc, wslab, wkey) in enumerate(qkv_jobs):
            psx = ps_s2[(gi // 2) % 2]
            half = (gi % 2) * 512
            pxk = f"ps_s2{'ab'[(gi // 2) % 2]}"
            em.op("tensor", [
                (lambda e, kc=kc, tc=tc, wslab=wslab, psx=psx, half=half: e.matmul(
                    psx[:, half:half + 512], wslab[:, kc * 128:(kc + 1) * 128],
                    xiTv[:, kc, tc * TC:(tc + 1) * TC], start=(kc == 0), stop=(kc == 7)))
                for kc in range(8)],
                reads=[wkey, "xiT"], writes=[pxk])
            if pending is not None:
                _emit_qkv_evac(em, pending)
            pending = (kind, tc, psx, half, pxk)
        _emit_qkv_evac(em, pending)
        for src, srck, off in [(qTp, "qTp", 0), (kTp, "kTp", 8)]:
            for tc in range(NTC):
                qb, qbk = (q2, "pt3") if tc % 2 == 0 else (pt[1], "pt1")
                em.op("scalar", lambda e, src=src, tc=tc, qb=qb: e.activation(
                    qb[:, 0:512], src[:, tc * TC:(tc + 1) * TC], AF.Square),
                    reads=[srck], writes=[qbk])
                em.op("tensor", lambda e, qb=qb: e.matmul(
                    ps_bc[0:1, :], onec[:], qb[:, 0:512],
                    start=True, stop=True), reads=[qbk, "onec"], writes=["ps_bc"])
                em.op("vector", lambda e, tc=tc, off=off: e.tensor_reduce(
                    nmx[:, off + tc:off + tc + 1], ps_bc[0:1, :], AX.X, ALU.max),
                    reads=["ps_bc"], writes=["nmx"])
        em.op("vector", lambda e: e.tensor_reduce(cst[:, 0:1], nmx[:, 0:8], AX.X, ALU.max),
              reads=["nmx"], writes=["cst"])
        em.op("vector", lambda e: e.tensor_reduce(cst[:, 1:2], nmx[:, 8:16], AX.X, ALU.max),
              reads=["nmx"], writes=["cst"])
        em.op("vector", lambda e: e.tensor_mul(cst[:, 2:3], cst[:, 0:1], cst[:, 1:2]),
              reads=["cst"], writes=["cst"])
        em.op("scalar", lambda e: e.activation(cst[:, 3:4], cst[:, 2:3], AF.Ln),
              reads=["cst"], writes=["cst"])
        em.op("scalar", lambda e: e.activation(cst[:, 4:5], cst[:, 3:4], AF.Exp, scale=0.5),
              reads=["cst"], writes=["cst"])
        em.op("vector", lambda e: e.tensor_scalar(
            cst[:, 5:6], cst[:, 4:5], -1.02 / 8.0, -1e-3, ALU.mult, ALU.add),
            reads=["cst"], writes=["cst"])
        em.op("tensor", lambda e: e.matmul(ps_bc[:, 0:1], oner[:], cst[:, 5:6],
                                           start=True, stop=True),
              reads=["oner", "cst"], writes=["ps_bc"])
        em.op("scalar", lambda e: e.copy(bias_p[:], ps_bc[:, 0:1]),
              reads=["ps_bc"], writes=["bias_p"])

        pvq = []
        epiq = []
        def _flush(queue, n_keep=0):
            while len(queue) > n_keep:
                eng, fn, rd, wr = queue.pop(0)
                em.op(eng, fn, reads=rd, writes=wr)
        for qc in range(NTC):
            nsb = 4 * qc + 4
            for sbi in range(nsb):
                if sbi >= 2:
                    _flush(epiq, max(0, len(epiq) - 1) if sbi < nsb - 1 else 0)

                par = sbi % 2
                pss, pssk = ps_s2[par], f"ps_s2{'ab'[par]}"
                pb, pbk = pt[sbi % 4], f"pt{sbi % 4}"
                em.op("tensor", [
                    (lambda e, sbi=sbi, hh=hh, qc=qc, pss=pss: e.matmul(
                        pss[:, hh * TC:(hh + 1) * TC],
                        kTp[hh * 64:hh * 64 + 64, sbi * 128:(sbi + 1) * 128],
                        qTp[hh * 64:hh * 64 + 64, qc * TC:(qc + 1) * TC],
                        start=True, stop=True))
                    for hh in range(2)],
                    reads=["kTp", "qTp"], writes=[pssk])
                em.op("scalar", lambda e, pb=pb, pss=pss: e.activation(
                    pb[:], pss[:], AF.Exp, bias=bias_p[:, 0:1], scale=0.125),
                    reads=[pssk, "bias_p"], writes=[pbk])
                if sbi >= 4 * qc:
                    em.op("gpsimd", lambda e, pb=pb, qc=qc, sbi=sbi: e.affine_select(
                        pb[:].rearrange("p (h n) -> p h n", h=2),
                        pb[:].rearrange("p (h n) -> p h n", h=2),
                        [[0, 2], [1, TC]], ALU.is_ge, 0.0,
                        base=qc * TC - sbi * 128, channel_multiplier=-1),
                        reads=[pbk], writes=[pbk])
                pvq.append(("tensor", [
                    (lambda e, sbi=sbi, hh=hh, nsb=nsb, pb=pb: e.matmul(
                        ps_o2[hh][0:65, :], Vpv[:, hh, sbi, :],
                        pb[:, hh * TC:(hh + 1) * TC],
                        start=(sbi == 0), stop=(sbi == nsb - 1)))
                    for hh in range(2)],
                    ["Vp", pbk], ["ps_oa", "ps_ob"]))
                _flush(pvq, 3)
            _flush(pvq)
            zu = ot[0]
            em.op("vector", lambda e: e.tensor_copy(zu[0:64, 0:512], ps_o2[0][0:64, :]),
                  reads=["ps_oa"], writes=["ot0"])
            em.op("vector", lambda e: e.tensor_copy(zu[64:128, 0:512], ps_o2[1][0:64, :]),
                  reads=["ps_ob"], writes=["ot0"])
            em.op("vector", lambda e: e.tensor_copy(xt[0][0:1, 0:512], ps_o2[0][64:65, :]),
                  reads=["ps_oa"], writes=["xt0"])
            em.op("vector", lambda e: e.tensor_copy(xt[1][0:1, 0:512], ps_o2[1][64:65, :]),
                  reads=["ps_ob"], writes=["xt1"])
            for hh in range(2):
                r0 = hh * 64
                lv, lvk = linv2[hh]
                lb = lbc[r0:r0 + 64, hh * 512:hh * 512 + 512]
                pbc = ps_bc[r0:r0 + 64, :]
                em.op("scalar", lambda e, lv=lv: e.activation(
                    lv[0:1, 0:512], lv[0:1, 0:512], AF.Ln),
                    reads=[lvk], writes=[lvk])
                em.op("scalar", lambda e, lv=lv: e.activation(
                    lv[0:1, 0:512], lv[0:1, 0:512], AF.Exp, scale=-1.0),
                    reads=[lvk], writes=[lvk])
                epiq.append(("tensor", (lambda e, lv=lv, pbc=pbc: e.matmul(
                    pbc, oner[:, 0:64], lv[0:1, 0:512],
                    start=True, stop=True)),
                    ["oner", lvk], ["ps_bc"]))
                epiq.append(("vector", (lambda e, lb=lb, pbc=pbc: e.tensor_copy(
                    lb, pbc)),
                    ["ps_bc"], ["wstg0"]))
                epiq.append(("vector", (lambda e, hp=hp, r0=r0, qc=qc, lb=lb: e.tensor_mul(
                    zTv[r0:r0 + 64, hp, qc * TC:(qc + 1) * TC],
                    zu[r0:r0 + 64, 0:512], lb)),
                    ["ot0", "wstg0"], ["zT"]))
        _flush(epiq)

    for j in range(NT):
        b = xt[j % 2]
        bk = f"xt{j % 2}"
        ob = ot[0]
        obk = "ot0"
        em.dma("sync", lambda e, j=j, b=b: e.dma_start(
            out=b[:], in_=x2[j * 128:(j + 1) * 128, :]), reads=["g.x2"], writes=[bk])
        fns = [
            (lambda e, kc=kc, j=j, oc=oc: e.matmul(
                ps_pj[:, oc * 512:(oc + 1) * 512], zTv[:, kc, j * 128:(j + 1) * 128],
                wos[:, kc * 1024 + oc * 512: kc * 1024 + (oc + 1) * 512],
                start=(kc == 0), stop=False))
            for oc in range(2) for kc in range(8)]
        fns.append(lambda e: e.matmul(ps_pj[:, 0:512], oner[:, 0:128], bo[:, 0:512],
                                      start=False, stop=False))
        fns.append(lambda e: e.matmul(ps_pj[:, 512:1024], oner[:, 0:128], bo[:, 512:1024],
                                      start=False, stop=True))
        em.op("tensor", fns, reads=["zT", "wos", "oner", "bo"], writes=["ps_s2a"])
        em.op("vector", lambda e, ob=ob, b=b: e.tensor_add(ob[:], b[:], ps_pj[:]),
              reads=["ps_s2a", bk], writes=[obk])
        em.dma("sync", lambda e, j=j, ob=ob: e.dma_start(
            out=x3[j * 128:(j + 1) * 128, :], in_=ob[:]), reads=[obk],
            writes=["g.x3"], slot="st_" + obk)


def emit_phase_d(em, nc, es, x3, out, wrT, wkT, wvT, mu8, id16, id32, tag=""):
    sb = lambda n, s, d: es.enter_context(nc.sbuf_tensor(tag + n, s, d))
    ps = lambda n, s, d: es.enter_context(nc.psum_tensor(tag + n, s, d))

    yT = sb("yT", [128, NC * (T + 1)], BF16)
    w_r = sb("w_r", [128, 8192], BF16)
    w_k = sb("w_k", [128, 8192], BF16)
    w_v = sb("w_v", [128, 8192], BF16)
    sbid = sb("sbid", [128, 128], BF16)
    sbid32 = sb("sbid32", [128, 128], F32)
    mu = sb("mu", [128, 16], F32)
    negmu = sb("negmu", [128, 16], F32)
    wf = [sb(f"wf{i}", [128, 1024], F32) for i in range(2)]
    xt = [sb(f"xt{i}", [128, 1024], F32) for i in range(2)]
    ot = [sb(f"ot{i}", [128, 1024], F32) for i in range(2)]
    xr = xt
    s1 = sb("s1", [128, 1], F32)
    ssq = sb("ssq", [128, 1], F32)
    st = sb("st", [128, 8], F32)
    ybf = sb("ybf", [128, 1024], BF16)
    t1r = sb("t1r", [128, 512], F32)
    t1k = sb("t1k", [128, 512], F32)
    t2 = sb("t2", [128, 512], F32)
    xrT = sb("xrT", [128, 4096], BF16)
    xkT = sb("xkT", [128, 4096], BF16)
    rT = sb("rT", [128, 4096], F32)
    kaT = sb("kaT", [128, 4096], BF16)
    outT = rT

    ps_t = ps("ps_t", [128, 1024], BF16)
    ps_r2 = [ps("ps_r", [128, 512], F32), ps("ps_rb", [128, 512], F32)]
    ps_k = ps("ps_k", [128, 512], F32)
    ps_v2 = [ps("ps_v", [128, 512], F32), ps("ps_vb", [128, 512], F32)]
    ps_tb = ps("ps_tb", [128, 1024], F32)

    yTv = yT[:].rearrange("p (c n) -> p c n", n=T + 1)

    em.dma("sync", lambda e: e.dma_start(out=sbid[:], in_=id16), writes=["sbid"])
    em.dma("sync", lambda e: e.dma_start(out=sbid32[:], in_=id32), writes=["sbid32"])
    em.dma("sync", lambda e: e.dma_start(out=mu[:], in_=mu8), writes=["mu"])
    em.op("vector", lambda e: e.tensor_scalar(negmu[:], mu[:], -1.0, 1.0, ALU.mult, ALU.add),
          reads=["mu"], writes=["negmu"])
    wjobs = []
    for wi, (slab, wd) in enumerate([(w_r, wrT), (w_k, wkT), (w_v, wvT)]):
        sn = f"w{wi}"
        for kc in range(8):
            b = wf[(wi * 8 + kc) % 2]
            bk = f"wf{(wi * 8 + kc) % 2}"
            wjobs.append((slab, sn, kc, b, bk, wd))
    em.op("vector", lambda e: e.memset(yTv[:, :, 0:1], 0.0), writes=["yT"])
    sq = ot[0]

    for j in range(NT):
        b = xt[j % 2]
        bk = f"xt{j % 2}"
        em.dma("sync", lambda e, j=j, b=b: e.dma_start(
            out=b[:], in_=x3[j * 128:(j + 1) * 128, :]), reads=["g.x3"], writes=[bk])
        em.op("vector", lambda e, b=b: e.tensor_reduce(s1[:], b[:], AX.X, ALU.add),
              reads=[bk], writes=["s1"])
        em.op("scalar", lambda e, b=b: e.activation(sq[:], b[:], AF.Square, accum_out=ssq[:]),
              reads=[bk], writes=["ot0", "ssq"])
        if 0 <= j - 2 < len(wjobs):
            slab, sn, kc, wb, wbk, wd = wjobs[j - 2]
            em.op("scalar", lambda e, slab=slab, kc=kc, wb=wb: e.copy(
                slab[:, kc * 1024:(kc + 1) * 1024], wb[:]),
                reads=[wbk], writes=[sn + "s"])
        if j < len(wjobs):
            slab, sn, kc, wb, wbk, wd = wjobs[j]
            em.dma("sync", lambda e, wd=wd, kc=kc, wb=wb: e.dma_start(
                out=wb[:], in_=wd[kc * 128:(kc + 1) * 128, :]), writes=[wbk])
        em.op("vector", lambda e: e.tensor_scalar_mul(st[:, 0:1], s1[:], 1.0 / D),
              reads=["s1"], writes=["st"])
        em.op("vector", lambda e: e.tensor_scalar_mul(st[:, 1:2], ssq[:], 1.0 / D),
              reads=["ssq"], writes=["st"])
        em.op("vector", lambda e: e.tensor_mul(st[:, 2:3], st[:, 0:1], st[:, 0:1]),
              reads=["st"], writes=["st"])
        em.op("vector", lambda e: e.tensor_sub(st[:, 2:3], st[:, 1:2], st[:, 2:3]),
              reads=["st"], writes=["st"])
        em.op("vector", lambda e: e.tensor_scalar_add(st[:, 2:3], st[:, 2:3], 1e-5),
              reads=["st"], writes=["st"])
        em.op("scalar", lambda e: e.activation(st[:, 3:4], st[:, 2:3], AF.Ln),
              reads=["st"], writes=["st"])
        em.op("scalar", lambda e: e.activation(st[:, 3:4], st[:, 3:4], AF.Exp, scale=-0.5),
              reads=["st"], writes=["st"])
        em.op("vector", lambda e: e.scalar_tensor_tensor(
            st[:, 4:5], st[:, 0:1], -1.0, st[:, 3:4], ALU.mult, ALU.mult),
            reads=["st"], writes=["st"])
        em.op("vector", lambda e, b=b: e.tensor_scalar(
            ybf[:], b[:], st[:, 3:4], st[:, 4:5], ALU.mult, ALU.add),
            reads=[bk, "st"], writes=["ybf"])
        em.op("tensor", [
            (lambda e, k=k: e.transpose(ps_t[:, k * 128:(k + 1) * 128],
                                        ybf[:, k * 128:(k + 1) * 128], sbid[:]))
            for k in range(8)],
            reads=["ybf", "sbid"], writes=["ps_t"])
        em.op("vector", lambda e, j=j: e.tensor_copy(
            yTv[:, :, 1 + j * 128: 1 + j * 128 + 128],
            ps_t[:].rearrange("p (c n) -> p c n", n=128)),
            reads=["ps_t"], writes=["yT"])

    def _emit_prep(tc):
        for kc in range(NC):
            ysh = yTv[:, kc, tc * TC: tc * TC + TC]
            yun = yTv[:, kc, 1 + tc * TC: 1 + tc * TC + TC]
            em.op("vector", lambda e, ysh=ysh, kc=kc: e.tensor_scalar_mul(
                t1r[:], ysh, negmu[:, kc:kc + 1]), reads=["yT", "negmu"], writes=["t1r"])
            em.op("vector", lambda e, yun=yun, kc=kc: e.scalar_tensor_tensor(
                xrT[:, kc * TC:(kc + 1) * TC], yun, mu[:, kc:kc + 1], t1r[:], ALU.mult, ALU.add),
                reads=["yT", "mu", "t1r"], writes=["xrT"])
            em.op("vector", lambda e, ysh=ysh, kc=kc: e.tensor_scalar_mul(
                t1k[:], ysh, negmu[:, 8 + kc:8 + kc + 1]), reads=["yT", "negmu"], writes=["t1k"])
            em.op("vector", lambda e, yun=yun, kc=kc: e.scalar_tensor_tensor(
                xkT[:, kc * TC:(kc + 1) * TC], yun, mu[:, 8 + kc:8 + kc + 1], t1k[:], ALU.mult, ALU.add),
                reads=["yT", "mu", "t1k"], writes=["xkT"])

    _emit_prep(0)
    for tc in range(NTC):
        for mc in range(8):
            pr, prk = ps_r2[mc % 2], ("ps_r" if mc % 2 == 0 else "ps_rb")
            em.op("tensor", [
                (lambda e, kc=kc, mc=mc, pr=pr: e.matmul(
                    pr[:], w_r[:, kc * 1024 + mc * 128: kc * 1024 + (mc + 1) * 128],
                    xrT[:, kc * TC:(kc + 1) * TC], start=(kc == 0), stop=(kc == 7)))
                for kc in range(8)],
                reads=["w0s", "xrT"], writes=[prk])
            em.op("scalar", lambda e, mc=mc, pr=pr: e.activation(
                rT[:, mc * TC:(mc + 1) * TC], pr[:], AF.Sigmoid),
                reads=[prk], writes=["rT"])
            em.op("tensor", [
                (lambda e, kc=kc, mc=mc: e.matmul(
                    ps_k[:], w_k[:, kc * 1024 + mc * 128: kc * 1024 + (mc + 1) * 128],
                    xkT[:, kc * TC:(kc + 1) * TC], start=(kc == 0), stop=(kc == 7)))
                for kc in range(8)],
                reads=["w1s", "xkT"], writes=["ps_k"])
            em.op("vector", lambda e: e.tensor_scalar_max(t2[:], ps_k[:], 0.0),
                  reads=["ps_k"], writes=["t2"])
            em.op("scalar", lambda e, mc=mc: e.activation(
                kaT[:, mc * TC:(mc + 1) * TC], t2[:], AF.Square),
                reads=["t2"], writes=["kaT"])
        for mc in range(8):
            pv, pvk = ps_v2[mc % 2], ("ps_v" if mc % 2 == 0 else "ps_vb")
            em.op("tensor", [
                (lambda e, kc=kc, mc=mc, pv=pv: e.matmul(
                    pv[:], w_v[:, kc * 1024 + mc * 128: kc * 1024 + (mc + 1) * 128],
                    kaT[:, kc * TC:(kc + 1) * TC], start=(kc == 0), stop=(kc == 7)))
                for kc in range(8)],
                reads=["w2s", "kaT"], writes=[pvk])
            em.op("vector", lambda e, mc=mc, pv=pv: e.tensor_mul(
                outT[:, mc * TC:(mc + 1) * TC], rT[:, mc * TC:(mc + 1) * TC], pv[:]),
                reads=["rT", pvk], writes=["rT"])
        if tc + 1 < NTC:
            _emit_prep(tc + 1)
        for tb in range(4):
            j = tc * 4 + tb
            b = xr[tb % 2]
            bk = f"xt{tb % 2}"
            ob = ot[tb % 2]
            obk = f"ot{tb % 2}"
            em.dma("sync", lambda e, j=j, b=b: e.dma_start(
                out=b[:], in_=x3[j * 128:(j + 1) * 128, :]), reads=["g.x3"], writes=[bk])
            em.op("tensor", [
                (lambda e, mc=mc, tb=tb: e.transpose(
                    ps_tb[:, mc * 128:(mc + 1) * 128],
                    outT[:, mc * TC + tb * 128: mc * TC + tb * 128 + 128], sbid32[:]))
                for mc in range(8)],
                reads=["rT", "sbid32"], writes=["ps_tb"])
            em.op("vector", lambda e, b=b, ob=ob: e.tensor_add(ob[:], b[:], ps_tb[:]),
                  reads=[bk, "ps_tb"], writes=[obk])
            em.dma("sync", lambda e, j=j, ob=ob: e.dma_start(
                out=out[j * 128:(j + 1) * 128, :], in_=ob[:]), reads=[obk], writes=["outd"],
                slot="st_" + obk)



def _build_fused():
    nc = bass.Bass("TRN2", target_bir_lowering=False)
    g = {}
    g["xall"] = nc.dram_tensor("xall", [B * T, D], F32, kind="ExternalInput").ap()
    g["xc"] = nc.dram_tensor("xc", [T, D], F32, kind="ExternalInput").ap()
    g["wqkvT"] = nc.dram_tensor("wqkvT", [3 * 128, 1024], F32, kind="ExternalInput").ap()
    g["qkvb"] = nc.dram_tensor("qkvb", [1, 384], F32, kind="ExternalInput").ap()
    g["projT"] = nc.dram_tensor("projT", [D, D], F32, kind="ExternalInput").ap()
    g["pbr"] = nc.dram_tensor("pbr", [1, D], F32, kind="ExternalInput").ap()
    g["cosd"] = nc.dram_tensor("cosd", [128, T], BF16, kind="ExternalInput").ap()
    g["sind"] = nc.dram_tensor("sind", [128, T], BF16, kind="ExternalInput").ap()
    g["r2d"] = nc.dram_tensor("r2d", [128, 128], F32, kind="ExternalInput").ap()
    g["id16"] = nc.dram_tensor("id16", [128, 128], BF16, kind="ExternalInput").ap()
    g["id32"] = nc.dram_tensor("id32", [128, 128], F32, kind="ExternalInput").ap()
    for nm in ["qwT", "kwT", "vwT", "owT"]:
        g[nm] = nc.dram_tensor(nm, [D, D], F32, kind="ExternalInput").ap()
    g["qkvb8"] = nc.dram_tensor("qkvb8", [128, 24], F32, kind="ExternalInput").ap()
    g["wbc"] = nc.dram_tensor("wbc", [128, D], F32, kind="ExternalInput").ap()
    g["bbc"] = nc.dram_tensor("bbc", [128, D], F32, kind="ExternalInput").ap()
    g["obr"] = nc.dram_tensor("obr", [1, D], F32, kind="ExternalInput").ap()
    for nm in ["wq", "wk", "wv", "wo"]:
        g[nm] = nc.dram_tensor(nm, [D, D], F32, kind="ExternalInput").ap()
    g["bob"] = nc.dram_tensor("bob", [128, D], F32, kind="ExternalInput").ap()
    for nm in ["wrT", "wkT2", "wvT2"]:
        g[nm] = nc.dram_tensor(nm, [D, D], F32, kind="ExternalInput").ap()
    g["mu8"] = nc.dram_tensor("mu8", [128, 16], F32, kind="ExternalInput").ap()
    g["y"] = nc.dram_tensor("y", [T, D], F32, kind="ExternalOutput").ap()
    x1i = nc.dram_tensor("x1i", [T, D], F32).ap()
    x2i = nc.dram_tensor("x2i", [T, D], F32).ap()
    x3i = nc.dram_tensor("x3i", [T, D], F32).ap()

    es = ExitStack()
    with es:
        em = Em(nc, es)
        em.tagk = "a."
        with ExitStack() as esp:
            emit_phase_a(em, nc, esp, g["xall"], g["xc"], x1i, g["wqkvT"], g["qkvb"],
                         g["projT"], g["pbr"], g["cosd"], g["sind"], g["r2d"],
                         g["id16"], tag="a_")
        em.phase_barrier()
        em.tagk = "b."
        with ExitStack() as esp:
            emit_phase_b(em, nc, esp, x1i, x2i, g["qwT"], g["kwT"], g["vwT"], g["owT"],
                         g["qkvb8"], g["wbc"], g["bbc"], g["obr"], g["id16"], tag="b_")
        em.phase_barrier()
        em.tagk = "c."
        with ExitStack() as esp:
            emit_phase_c(em, nc, esp, x2i, x3i, g["wq"], g["wk"], g["wv"], g["wo"],
                         g["bob"], g["id16"], tag="c_")
        em.phase_barrier()
        em.tagk = "d."
        with ExitStack() as esp:
            emit_phase_d(em, nc, esp, x3i, g["y"], g["wrT"], g["wkT2"], g["wvT2"],
                         g["mu8"], g["id16"], g["id32"], tag="d_")
        em.wait_all("sync")
        block = es.enter_context(nc.Block())
        em.finalize(block)
    return nc


_NC_CACHE = {}


def _get_nc():
    if "f" not in _NC_CACHE:
        _NC_CACHE["f"] = _build_fused()
    return _NC_CACHE["f"]


def _f32c(x):
    return np.ascontiguousarray(np.asarray(x, dtype=np.float32))


def kernel(L0_attn_x, L0_attn_qkv_w, L0_attn_qkv_b, L0_attn_proj_w, L0_attn_proj_b,
           L0_ffn_router_prob, L0_ffn_norm_w, L0_ffn_norm_b, L0_ffn_q_w, L0_ffn_q_b,
           L0_ffn_k_w, L0_ffn_k_b, L0_ffn_v_w, L0_ffn_v_b, L0_ffn_o_w, L0_ffn_o_b,
           L1_attn_W_Q, L1_attn_W_K, L1_attn_W_V, L1_attn_W_out, L1_attn_b_out,
           L1_ffn_W_r, L1_ffn_W_k, L1_ffn_W_v, L1_ffn_mu_r, L1_ffn_mu_k):
    cores = list(range(8))
    e_cnt = int(np.asarray(L0_ffn_router_prob).shape[2])
    assert e_cnt == 4, "kernel specialized for 4 experts"
    x = np.asarray(L0_attn_x, dtype=np.float32)
    xflat = np.ascontiguousarray(x.reshape(B * T, D))
    id16 = np.eye(128).astype(ml_dtypes.bfloat16)
    id32 = np.eye(128, dtype=np.float32)

    qkv_w = np.asarray(L0_attn_qkv_w, dtype=np.float32)
    qkv_b = np.asarray(L0_attn_qkv_b, dtype=np.float32)
    projT = _f32c(np.asarray(L0_attn_proj_w, dtype=np.float32).T)
    pbr = _f32c(np.asarray(L0_attn_proj_b, dtype=np.float32)[None, :])
    hd = 64
    inv_freq = 1.0 / 10000.0 ** (np.arange(0, hd, 2, dtype=np.float32) / hd)
    pos = np.arange(T, dtype=np.float32)
    freqs = pos[:, None] * inv_freq[None, :]
    cosd = np.ascontiguousarray(np.tile(np.cos(freqs).T, (4, 1))).astype(ml_dtypes.bfloat16)
    sind = np.ascontiguousarray(np.tile(np.sin(freqs).T, (4, 1))).astype(ml_dtypes.bfloat16)
    R2T = np.zeros((128, 128), np.float32)
    for r in range(128):
        blk = (r // 64) * 64
        dd = r % 64
        if dd < 32:
            R2T[r, blk + dd + 32] = -1.0
        else:
            R2T[r, blk + dd - 32] = 1.0
    r2d = np.ascontiguousarray(R2T.T)

    def pair_arr(w):
        wT = np.asarray(w, dtype=np.float32).T
        return np.ascontiguousarray(
            wT.reshape(8, 128, 8, 128).transpose(2, 1, 0, 3).reshape(1024, 1024))
    qwT = pair_arr(L0_ffn_q_w)
    kwT = pair_arr(L0_ffn_k_w)
    vwT = pair_arr(L0_ffn_v_w)
    owT = _f32c(np.asarray(L0_ffn_o_w, dtype=np.float32).T)
    qkvb8 = _f32c(np.concatenate([np.asarray(L0_ffn_q_b, np.float32).reshape(8, 128).T,
                                  np.asarray(L0_ffn_k_b, np.float32).reshape(8, 128).T,
                                  np.asarray(L0_ffn_v_b, np.float32).reshape(8, 128).T], axis=1))
    wbc = _f32c(np.broadcast_to(np.asarray(L0_ffn_norm_w, np.float32), (128, 1024)))
    bbc = _f32c(np.broadcast_to(np.asarray(L0_ffn_norm_b, np.float32), (128, 1024)))
    obr = _f32c(np.asarray(L0_ffn_o_b, np.float32)[None, :])

    def pair_arr2(w):
        return np.ascontiguousarray(
            np.asarray(w, np.float32).reshape(8, 128, 8, 128)
            .transpose(2, 1, 0, 3).reshape(1024, 1024))
    wq = pair_arr2(np.asarray(L1_attn_W_Q, np.float32).reshape(1024, 1024))
    wk = pair_arr2(np.asarray(L1_attn_W_K, np.float32).reshape(1024, 1024))
    wv = pair_arr2(np.asarray(L1_attn_W_V, np.float32).reshape(1024, 1024))
    wo = _f32c(np.asarray(L1_attn_W_out, np.float32).transpose(1, 0, 2).reshape(1024, 1024))
    bob = _f32c(np.broadcast_to(np.asarray(L1_attn_b_out, np.float32), (128, 1024)))

    wrT = _f32c(np.asarray(L1_ffn_W_r, np.float32).T)
    wkT2 = _f32c(np.asarray(L1_ffn_W_k, np.float32).T)
    wvT2 = _f32c(np.asarray(L1_ffn_W_v, np.float32).T)
    mu8 = _f32c(np.concatenate([np.asarray(L1_ffn_mu_r, np.float32).reshape(8, 128).T,
                                np.asarray(L1_ffn_mu_k, np.float32).reshape(8, 128).T], axis=1))

    shared = {"projT": projT, "pbr": pbr, "cosd": cosd, "sind": sind, "r2d": r2d,
              "id16": id16, "id32": id32, "qwT": qwT, "kwT": kwT, "vwT": vwT,
              "owT": owT, "qkvb8": qkvb8, "wbc": wbc, "bbc": bbc, "obr": obr,
              "wq": wq, "wk": wk, "wv": wv, "wo": wo, "bob": bob,
              "wrT": wrT, "wkT2": wkT2, "wvT2": wvT2, "mu8": mu8, "xall": xflat}
    in_maps = []
    for c in cores:
        wq_parts = []
        for t3 in range(3):
            W = qkv_w[1024 * t3 + 128 * c: 1024 * t3 + 128 * c + 128, :]
            wq_parts.append(np.ascontiguousarray(
                W.T.reshape(8, 128, 128).transpose(1, 0, 2).reshape(128, 1024)))
        wqkvT = _f32c(np.concatenate(wq_parts, axis=0))
        qkvbc = _f32c(np.concatenate(
            [qkv_b[1024 * t3 + 128 * c: 1024 * t3 + 128 * c + 128] for t3 in range(3)])[None, :])
        in_maps.append(dict(shared, wqkvT=wqkvT, qkvb=qkvbc,
                            xc=np.ascontiguousarray(x[c])))
    res = run_bass_kernel_spmd(_get_nc(), in_maps, core_ids=cores)
    return np.stack([np.asarray(r["y"]) for r in res.results], axis=0)


# version 18 (final)
# speedup vs baseline: 1.2181x; 1.0178x over previous
import numpy as np
import ml_dtypes
from contextlib import ExitStack

import concourse.bass as bass
import concourse.mybir as mybir
from concourse.bass_utils import run_bass_kernel_spmd

F32 = mybir.dt.float32
BF16 = mybir.dt.bfloat16
AF = mybir.ActivationFunctionType
ALU = mybir.AluOpType
AX = mybir.AxisListType

T = 4096
B = 8
D = 1024
TC = 512
NTC = 8
NT = 32
NC = 8
LND = float(np.log(1024.0) * 0.5)

ENGINES = ("sync", "tensor", "vector", "scalar", "gpsimd")


class Em:
    def __init__(self, nc, es: ExitStack, tag=""):
        self.nc = nc
        self.es = es
        self.tag = tag
        self.plan = {e: [] for e in ENGINES}
        self.csem = {}
        self.dsem = {}
        self.last_write = {}
        self.readers = {}
        self.waited = {e: {} for e in ENGINES}
        self.sems = {}
        self._semn = 0
        self.tagk = ""

    def _mk_sem(self, name):
        h = self.es.enter_context(self.nc.semaphore(f"{self.tag}{name}"))
        sid = self._semn
        self._semn += 1
        self.sems[sid] = h
        return sid

    def _csem(self, eng):
        if eng not in self.csem:
            self.csem[eng] = [self._mk_sem(f"c_{eng}"), 0]
        return self.csem[eng]

    def _dsem(self, slot):
        if slot not in self.dsem:
            self.dsem[slot] = [self._mk_sem(f"d_{slot}"), 0]
        return self.dsem[slot]

    def _need_waits(self, eng, reads, writes):
        toks = {}
        for r in reads:
            t = self.last_write.get(r)
            if t is not None:
                toks[t[0]] = max(toks.get(t[0], 0), t[1])
        for w in writes:
            t = self.last_write.get(w)
            if t is not None:
                toks[t[0]] = max(toks.get(t[0], 0), t[1])
            for t in self.readers.get(w, ()):
                toks[t[0]] = max(toks.get(t[0], 0), t[1])
        out = []
        wt = self.waited[eng]
        for sid, val in toks.items():
            if wt.get(sid, 0) < val:
                out.append((sid, val))
                wt[sid] = val
        return out

    def _post(self, eng, tok, reads, writes):
        for w in writes:
            self.last_write[w] = tok
            self.readers[w] = []
        for r in reads:
            self.readers.setdefault(r, []).append(tok)

    def _k(self, key):
        return key if key.startswith("g.") else self.tagk + key

    def op(self, eng, fns, reads=(), writes=()):
        if callable(fns):
            fns = [fns]
        reads = [self._k(r) for r in reads]
        writes = [self._k(w) for w in writes]
        waits = self._need_waits(eng, reads, writes)
        cs = self._csem(eng)
        cs[1] += 1
        self.plan[eng].append(("op", waits, fns, cs[0], 1))
        self._post(eng, (cs[0], cs[1]), reads, writes)

    def dma(self, eng, fn, reads=(), writes=(), slot=None):
        reads = [self._k(r) for r in reads]
        writes = [self._k(w) for w in writes]
        if slot is None:
            slot = writes[0] if writes else self._k("dma")
        else:
            slot = self._k(slot)
        waits = self._need_waits(eng, reads, writes)
        ds = self._dsem(slot)
        ds[1] += 16
        self.plan[eng].append(("op", waits, [fn], ds[0], 16))
        self._post(eng, (ds[0], ds[1]), reads, writes)

    def barrier_on(self, eng, keys):
        waits = self._need_waits(eng, keys, ())
        if waits:
            self.plan[eng].append(("waits", waits))

    def phase_barrier(self):
        for eng in ENGINES:
            waits = []
            wt = self.waited[eng]
            for sid in self.sems:
                final = 0
                for cs in self.csem.values():
                    if cs[0] == sid:
                        final = cs[1]
                for ds in self.dsem.values():
                    if ds[0] == sid:
                        final = ds[1]
                if final and wt.get(sid, 0) < final:
                    waits.append((sid, final))
                    wt[sid] = final
            if waits:
                self.plan[eng].append(("waits", waits))

    def wait_all(self, eng):
        waits = []
        wt = self.waited[eng]
        for sid, h in self.sems.items():
            final = None
            for e2, cs in self.csem.items():
                if cs[0] == sid:
                    final = cs[1]
            for e2, ds in self.dsem.items():
                if ds[0] == sid:
                    final = ds[1]
            if final and wt.get(sid, 0) < final:
                waits.append((sid, final))
                wt[sid] = final
        self.plan[eng].append(("waits", waits))

    def finalize(self, block):
        for eng in ENGINES:
            entries = self.plan[eng]
            if not entries:
                continue
            deco = getattr(block, eng)

            def body(e, entries=entries):
                for ent in entries:
                    if ent[0] == "waits":
                        for sid, val in ent[1]:
                            e.wait_ge(self.sems[sid], val)
                    else:
                        _, waits, fns, sid, inc = ent
                        for wsid, val in waits:
                            e.wait_ge(self.sems[wsid], val)
                        for i, fn in enumerate(fns):
                            instr = fn(e)
                            if i == len(fns) - 1:
                                instr.then_inc(self.sems[sid], inc)
            deco(body)


def emit_ln_tile(em, src, srck, ybf, ybfk, bufs):
    (s1, s1k), (ssq, ssqk), (sq, sqk), (st, stk), cb = bufs
    em.op("vector", lambda e: e.tensor_reduce(s1[:], src[:], AX.X, ALU.add),
          reads=[srck], writes=[s1k])
    em.op("scalar", lambda e: e.activation(sq[:], src[:], AF.Square, accum_out=ssq[:]),
          reads=[srck], writes=[sqk, ssqk])
    em.op("vector", lambda e: e.tensor_mul(st[:, 0:1], s1[:], s1[:]),
          reads=[s1k], writes=[stk])
    em.op("vector", lambda e: e.scalar_tensor_tensor(
        st[:, 1:2], st[:, 0:1], -1.0 / D, ssq[:], ALU.mult, ALU.add),
        reads=[stk, ssqk], writes=[stk])
    em.op("scalar", lambda e: e.activation(st[:, 2:3], st[:, 1:2], AF.Ln, bias=cb[:, 0:1]),
          reads=[stk, "lncb"], writes=[stk])
    em.op("scalar", lambda e: e.activation(st[:, 2:3], st[:, 2:3], AF.Exp, scale=-0.5, bias=cb[:, 1:2]),
          reads=[stk, "lncb"], writes=[stk])
    em.op("vector", lambda e: e.scalar_tensor_tensor(
        st[:, 3:4], s1[:], -1.0 / D, st[:, 2:3], ALU.mult, ALU.mult),
        reads=[s1k, stk], writes=[stk])
    em.op("vector", lambda e: e.tensor_scalar(
        ybf[:], src[:], st[:, 2:3], st[:, 3:4], ALU.mult, ALU.add),
        reads=[srck, stk], writes=[ybfk])


def emit_ln_stats(em, src, srck, st, c0, bufs):
    s1, ssq, sq, cb = bufs
    em.op("vector", lambda e: e.tensor_reduce(s1[:], src, AX.X, ALU.add),
          reads=[srck], writes=["s1"])
    em.op("scalar", lambda e: e.activation(sq[0][:], src, AF.Square, accum_out=ssq[:]),
          reads=[srck], writes=[sq[1], "ssq"])
    em.op("vector", lambda e: e.tensor_mul(st[:, 4:5], s1[:], s1[:]),
          reads=["s1"], writes=["st"])
    em.op("vector", lambda e: e.scalar_tensor_tensor(
        st[:, 5:6], st[:, 4:5], -1.0 / D, ssq[:], ALU.mult, ALU.add),
        reads=["st", "ssq"], writes=["st"])
    em.op("scalar", lambda e: e.activation(st[:, c0:c0 + 1], st[:, 5:6], AF.Ln, bias=cb[:, 0:1]),
          reads=["st", "lncb"], writes=["st"])
    em.op("scalar", lambda e: e.activation(st[:, c0:c0 + 1], st[:, c0:c0 + 1], AF.Exp,
                                           scale=-0.5, bias=cb[:, 1:2]),
          reads=["st", "lncb"], writes=["st"])
    em.op("vector", lambda e: e.scalar_tensor_tensor(
        st[:, c0 + 1:c0 + 2], s1[:], -1.0 / D, st[:, c0:c0 + 1], ALU.mult, ALU.mult),
        reads=["s1", "st"], writes=["st"])


def emit_phase_a(em, nc, es, xall, xc, x1, wqkvT, qkvb, projT, pbr, cosd, sind,
                 r2d, id16, tag=""):
    sb = lambda n, s, d: es.enter_context(nc.sbuf_tensor(tag + n, s, d))
    ps = lambda n, s, d: es.enter_context(nc.psum_tensor(tag + n, s, d))

    sscr = sb("sscr", [128, 8 * T], BF16)
    cosa = sb("cosa", [128, T], BF16)
    sina = sb("sina", [128, T], BF16)
    pjs = sb("pjs", [128, 8192], BF16)
    xlnT = [sb(f"xlnT{i}", [128, 8 * TC], BF16) for i in range(2)]
    wq3 = sb("wq3", [128, 3 * 1024], BF16)
    qTp = sb("qTp", [128, 4096], BF16)
    kTp = sb("kTp", [128, 4096], BF16)
    Vp = sb("Vp", [128, 2 * 32 * 65], BF16)
    r2 = sb("r2", [128, 128], BF16)
    sbid = sb("sbid", [128, 128], BF16)
    qkbs = sb("qkbs", [1, 384], F32)
    ones5 = sb("ones5", [1, 512], F32)
    wstg = [sb("wstg0", [128, 1024], F32)] * 2
    xt = [sb(f"xt{i}", [128, 1024], F32) for i in range(2)]
    xt4 = [xt[0], xt[1], sb("xt2", [128, 1024], F32), sb("xt3", [128, 1024], F32)]
    ot = sb("ot0", [128, 1024], F32)
    ybf2 = [sb(f"ybf{i}", [128, 1024], BF16) for i in range(2)]
    st4 = [sb(f"st4{i}", [128, 16], F32) for i in range(2)]
    lncb = sb("lncb", [128, 2], F32)
    qsb = sb("qsb", [128, 512], BF16)
    tcos = sb("tcos", [128, 512], F32)
    tsin = sb("tsin", [128, 512], F32)
    onec = sb("onec", [128, 1], BF16)
    oner = sb("oner", [1, 128], F32)
    nmx = sb("nmx", [1, 32], F32)
    cst = sb("cst", [1, 8], F32)
    bias_p = sb("bias_p", [128, 1], F32)
    pt = [sb(f"pt{i}", [128, 1024], BF16) for i in range(4)]
    linv = sb("linv", [1, 512], F32)
    lbc = sb("lbc", [128, 1024], F32)
    onorm = sb("onorm", [128, 512], F32)

    ps_s2 = [ps("ps_s2a", [128, 1024], F32), ps("ps_s2b", [128, 1024], F32)]
    ps_o2 = [ps("ps_oa", [128, 512], F32), ps("ps_ob", [128, 512], F32)]
    ps_bc = ps("ps_bc", [128, 512], F32)
    ps_t = ps("ps_t", [128, 1024], BF16)
    ps_a = ps_s2[0]
    ps_r = ps_s2[1]
    ps_pj = ps_s2[0]

    sq2 = [(wstg[0], "wstg0"), (ot, "ot0")]

    sscrv = sscr[:].rearrange("p (c n) -> p c n", n=T)
    pbs = None
    vstg = pt[2]
    q2 = pt[3]
    Vpv = Vp[:].rearrange("p (h s c) -> p h s c", h=2, s=32)

    em.dma("sync", lambda e: e.dma_start(out=sbid[:], in_=id16), writes=["sbid"])
    em.dma("sync", lambda e: e.dma_start(out=cosa[:], in_=cosd), writes=["cosa"])
    em.dma("sync", lambda e: e.dma_start(out=sina[:], in_=sind), writes=["sina"])
    em.op("vector", lambda e: e.memset(onec[:], 1.0), writes=["onec"])
    em.op("vector", lambda e: e.memset(oner[:], 1.0), writes=["oner"])
    em.op("vector", lambda e: e.memset(ones5[:], 1.0), writes=["ones5"])
    em.op("vector", lambda e: e.memset(lncb[:, 0:1], D * 1e-5), writes=["lncb"])
    em.op("vector", lambda e: e.memset(lncb[:, 1:2], LND), writes=["lncb"])
    em.op("vector", lambda e: e.memset(Vpv[:, :, :, 64:65], 1.0), writes=["Vp"])
    b, bk = wstg[0], "wstg0"
    em.dma("sync", lambda e: e.dma_start(out=b[:, 0:128], in_=r2d), writes=[bk])
    em.op("vector", lambda e: e.tensor_copy(r2[:], b[:, 0:128]), reads=[bk], writes=["r2"])
    em.dma("sync", lambda e: e.dma_start(out=b[0:1, 0:384], in_=qkvb),
           reads=["r2"], writes=[bk])
    em.op("vector", lambda e: e.tensor_copy(qkbs[:], b[0:1, 0:384]), reads=[bk], writes=["qkbs"])
    for t3 in range(3):
        em.dma("sync", lambda e, t3=t3: e.dma_start(
            out=b[:], in_=wqkvT[t3 * 128:(t3 + 1) * 128, :]), reads=["qkbs"], writes=[bk])
        em.op("vector", lambda e, t3=t3: e.tensor_copy(
            wq3[:, t3 * 1024:(t3 + 1) * 1024], b[:]), reads=[bk], writes=["wq3"])
    for kc in range(8):
        em.dma("sync", lambda e, kc=kc: e.dma_start(
            out=b[:], in_=projT[kc * 128:(kc + 1) * 128, :]), reads=["pjslast"], writes=[bk])
        em.op("vector", lambda e, kc=kc: e.tensor_copy(
            pjs[:, kc * 1024:(kc + 1) * 1024], b[:]), reads=[bk], writes=["pjslast"])

    def emit_stageA(bb2, g):
        st, stk = st4[g % 2], f"st4{g % 2}"
        for jj in range(4):
            row0 = bb2 * T + g * 512 + jj * 128
            xb, xbk = xt4[jj], f"xt{jj}"
            em.dma("sync", lambda e, row0=row0, xb=xb: e.dma_start(
                out=xb[:], in_=xall[row0:row0 + 128, :]), writes=[xbk])
            em.op("vector", lambda e, xb=xb, jj=jj, st=st: e.tensor_reduce(
                st[:, jj:jj + 1], xb[:], AX.X, ALU.add),
                reads=[xbk], writes=[stk])
            sq, sqk = sq2[g % 2]
            em.op("scalar", lambda e, xb=xb, jj=jj, st=st, sq=sq: e.activation(
                sq[:], xb[:], AF.Square, accum_out=st[:, 4 + jj:5 + jj]),
                reads=[xbk], writes=[sqk, stk])

    for bb in range(B):
        for g in range(NTC):
            xl, xlk = xlnT[g % 2], f"xlnT{g % 2}"
            xlv = xl[:].rearrange("p (c n) -> p c n", n=TC)
            def _stageB(g, xlv, xlk):
                st, stk = st4[g % 2], f"st4{g % 2}"
                em.op("vector", lambda e, st=st: e.tensor_mul(
                    st[:, 8:12], st[:, 0:4], st[:, 0:4]), reads=[stk], writes=[stk])
                em.op("vector", lambda e, st=st: e.scalar_tensor_tensor(
                    st[:, 8:12], st[:, 8:12], -1.0 / D, st[:, 4:8], ALU.mult, ALU.add),
                    reads=[stk], writes=[stk])
                em.op("scalar", lambda e, st=st: e.activation(
                    st[:, 8:12], st[:, 8:12], AF.Ln, bias=lncb[:, 0:1]),
                    reads=[stk, "lncb"], writes=[stk])
                em.op("scalar", lambda e, st=st: e.activation(
                    st[:, 8:12], st[:, 8:12], AF.Exp, scale=-0.5, bias=lncb[:, 1:2]),
                    reads=[stk, "lncb"], writes=[stk])
                em.op("vector", lambda e, st=st: e.scalar_tensor_tensor(
                    st[:, 12:16], st[:, 0:4], -1.0 / D, st[:, 8:12], ALU.mult, ALU.mult),
                    reads=[stk], writes=[stk])
                for jj in range(4):
                    xb, xbk = xt4[jj], f"xt{jj}"
                    ybf, ybfk = ybf2[jj % 2], f"ybf{jj % 2}"
                    em.op("vector", lambda e, xb=xb, jj=jj, st=st, ybf=ybf: e.tensor_scalar(
                        ybf[:], xb[:], st[:, 8 + jj:9 + jj], st[:, 12 + jj:13 + jj],
                        ALU.mult, ALU.add),
                        reads=[xbk, stk], writes=[ybfk])
                    em.op("tensor", [
                        (lambda e, k=k, ybf=ybf: e.transpose(ps_t[:, k * 128:(k + 1) * 128],
                                                    ybf[:, k * 128:(k + 1) * 128], sbid[:]))
                        for k in range(8)],
                        reads=[ybfk, "sbid"], writes=["ps_t"])
                    em.op("vector", lambda e, jj=jj, xlv=xlv: e.tensor_copy(
                        xlv[:, :, jj * 128: jj * 128 + 128],
                        ps_t[:].rearrange("p (c n) -> p c n", n=128)),
                        reads=["ps_t"], writes=[xlk])

            if g == 0 and bb == 0:
                emit_stageA(bb, 0)
            _stageB(g, xlv, xlk)
            if g < NTC - 1:
                emit_stageA(bb, g + 1)
            gc = slice(g * 512, (g + 1) * 512)
            for t3, dst, dstk in [(0, qTp, "qTp"), (1, kTp, "kTp")]:
                h0 = t3 * 512
                fns = [
                    (lambda e, kc=kc, t3=t3, xlv=xlv, h0=h0: e.matmul(
                        ps_a[:, h0:h0 + 512], wq3[:, t3 * 1024 + kc * 128: t3 * 1024 + (kc + 1) * 128],
                        xlv[:, kc, :], start=(kc == 0), stop=False))
                    for kc in range(8)]
                fns.append(lambda e, t3=t3, h0=h0: e.matmul(
                    ps_a[:, h0:h0 + 512], qkbs[:, t3 * 128:(t3 + 1) * 128], ones5[:],
                    start=False, stop=True))
                em.op("tensor", fns, reads=["wq3", xlk, "qkbs", "ones5"],
                      writes=["ps_s2a.lo" if t3 == 0 else "ps_s2a.hi"])
                half = 0 if t3 == 0 else 512
                hk = "ps_s2a.lo" if t3 == 0 else "ps_s2a.hi"
                rps = ps_o2[t3]
                rk = "ps_oa" if t3 == 0 else "ps_ob"
                qb, qbk = (qsb, "qsb") if (2 * g + t3) % 2 == 0 else (pt[0], "pt0")
                em.op("scalar", lambda e, half=half, qb=qb: e.copy(
                    qb[:, 0:512], ps_a[:, half:half + 512]),
                    reads=[hk], writes=[qbk])
                em.op("tensor", lambda e, rps=rps, qb=qb: e.matmul(
                    rps[:], r2[:], qb[:, 0:512], start=True, stop=True),
                      reads=["r2", qbk], writes=[rk])
                em.op("vector", lambda e, g=g, half=half: e.tensor_mul(
                    tcos[:], ps_a[:, half:half + 512], cosa[:, g * 512:(g + 1) * 512]),
                    reads=[hk, "cosa", rk], writes=["tcos"])
                em.op("vector", lambda e, g=g, rps=rps: e.tensor_mul(
                    tsin[:], rps[:], sina[:, g * 512:(g + 1) * 512]),
                    reads=[rk, "sina", hk], writes=["tsin"])
                em.op("vector", lambda e, dst=dst, gc=gc: e.tensor_add(
                    dst[:, gc], tcos[:], tsin[:]),
                    reads=["tcos", "tsin"], writes=[dstk])
            fns = [
                (lambda e, kc=kc, xlv=xlv: e.matmul(
                    ps_s2[1][:, 512:1024], wq3[:, 2 * 1024 + kc * 128: 2 * 1024 + (kc + 1) * 128],
                    xlv[:, kc, :], start=(kc == 0), stop=False))
                for kc in range(8)]
            fns.append(lambda e: e.matmul(
                ps_s2[1][:, 512:1024], qkbs[:, 256:384], ones5[:], start=False, stop=True))
            em.op("tensor", fns, reads=["wq3", xlk, "qkbs", "ones5"], writes=["ps_s2b.hi"])
            em.op("scalar", lambda e: e.copy(vstg[:, 0:512], ps_s2[1][:, 512:1024]),
                  reads=["ps_s2b.hi"], writes=["pt2"])
            em.op("tensor", [
                (lambda e, k=k: e.transpose(ps_t[:, k * 128:(k + 1) * 128],
                                            vstg[:, k * 128:(k + 1) * 128], sbid[:]))
                for k in range(4)],
                reads=["pt2", "sbid"], writes=["ps_t"])
            for k in range(4):
                sbk = g * 4 + k
                em.op("vector", lambda e, k=k, sbk=sbk: e.tensor_copy(
                    Vpv[:, 0, sbk, 0:64], ps_t[:, k * 128:k * 128 + 64]),
                    reads=["ps_t"], writes=["Vp"])
                em.op("vector", lambda e, k=k, sbk=sbk: e.tensor_copy(
                    Vpv[:, 1, sbk, 0:64], ps_t[:, k * 128 + 64:k * 128 + 128]),
                    reads=["ps_t"], writes=["Vp"])
        for src, srck, off in [(qTp, "qTp", 0), (kTp, "kTp", 8)]:
            for tc in range(NTC):
                qb, qbk = (q2, "pt3") if tc % 2 == 0 else (pt[1], "pt1")
                em.op("scalar", lambda e, src=src, tc=tc, qb=qb: e.activation(
                    qb[:, 0:512], src[:, tc * TC:(tc + 1) * TC], AF.Square),
                    reads=[srck], writes=[qbk])
                em.op("tensor", lambda e, qb=qb: e.matmul(
                    ps_bc[0:1, :], onec[:], qb[:, 0:512], start=True, stop=True),
                    reads=[qbk, "onec"], writes=["ps_bc"])
                em.op("vector", lambda e, tc=tc, off=off: e.tensor_reduce(
                    nmx[:, off + tc:off + tc + 1], ps_bc[0:1, :], AX.X, ALU.max),
                    reads=["ps_bc"], writes=["nmx"])
        em.op("vector", lambda e: e.tensor_reduce(cst[:, 0:1], nmx[:, 0:8], AX.X, ALU.max),
              reads=["nmx"], writes=["cst"])
        em.op("vector", lambda e: e.tensor_reduce(cst[:, 1:2], nmx[:, 8:16], AX.X, ALU.max),
              reads=["nmx"], writes=["cst"])
        em.op("vector", lambda e: e.tensor_mul(cst[:, 2:3], cst[:, 0:1], cst[:, 1:2]),
              reads=["cst"], writes=["cst"])
        em.op("scalar", lambda e: e.activation(cst[:, 3:4], cst[:, 2:3], AF.Ln),
              reads=["cst"], writes=["cst"])
        em.op("scalar", lambda e: e.activation(cst[:, 4:5], cst[:, 3:4], AF.Exp, scale=0.5),
              reads=["cst"], writes=["cst"])
        em.op("vector", lambda e: e.tensor_scalar(
            cst[:, 5:6], cst[:, 4:5], -1.02 / 8.0, -1e-3, ALU.mult, ALU.add),
            reads=["cst"], writes=["cst"])
        em.op("tensor", lambda e: e.matmul(ps_bc[:, 0:1], oner[:], cst[:, 5:6],
                                           start=True, stop=True),
              reads=["oner", "cst"], writes=["ps_bc"])
        em.op("scalar", lambda e: e.copy(bias_p[:], ps_bc[:, 0:1]),
              reads=["ps_bc"], writes=["bias_p"])

        pvq = []
        epiq = []
        def _flush(queue, n_keep=0):
            while len(queue) > n_keep:
                eng, fn, rd, wr = queue.pop(0)
                em.op(eng, fn, reads=rd, writes=wr)
        zu = tcos
        lrs = [(onorm[0:1, 0:512], "onorm"), (lbc[0:1, 512:1024], "lbc")]
        for qc in range(NTC):
            if qc == NTC - 1 and bb + 1 < B:
                emit_stageA(bb + 1, 0)
            for sbi in range(32):
                if sbi >= 2:
                    _flush(epiq, max(0, len(epiq) - 2) if sbi < 31 else 0)
                par = sbi % 2
                pss = ps_s2[par]
                ks = [f"ps_s2{'ab'[par]}.lo", f"ps_s2{'ab'[par]}.hi"]
                pb, pbk = pt[sbi % 4], f"pt{sbi % 4}"
                em.op("tensor", [
                    (lambda e, sbi=sbi, hh=hh, qc=qc, pss=pss: e.matmul(
                        pss[:, hh * TC:(hh + 1) * TC],
                        kTp[hh * 64:hh * 64 + 64, sbi * 128:(sbi + 1) * 128],
                        qTp[hh * 64:hh * 64 + 64, qc * TC:(qc + 1) * TC],
                        start=True, stop=True))
                    for hh in range(2)],
                    reads=["kTp", "qTp"], writes=ks)
                em.op("scalar", lambda e, pb=pb, pss=pss: e.activation(
                    pb[:], pss[:], AF.Exp, bias=bias_p[:, 0:1], scale=0.125),
                    reads=ks + ["bias_p"], writes=[pbk])
                pvq.append(("tensor", [
                    (lambda e, sbi=sbi, hh=hh, pb=pb: e.matmul(
                        ps_o2[hh][0:65, :], Vpv[:, hh, sbi, :],
                        pb[:, hh * TC:(hh + 1) * TC],
                        start=(sbi == 0), stop=(sbi == 31)))
                    for hh in range(2)],
                    ["Vp", pbk], ["ps_oa", "ps_ob"]))
                _flush(pvq, 3)
            _flush(pvq)
            em.op("vector", lambda e: e.tensor_copy(zu[0:64, :], ps_o2[0][0:64, :]),
                  reads=["ps_oa"], writes=["tcos"])
            em.op("vector", lambda e: e.tensor_copy(zu[64:128, :], ps_o2[1][0:64, :]),
                  reads=["ps_ob"], writes=["tcos"])
            em.op("vector", lambda e: e.tensor_copy(lrs[0][0], ps_o2[0][64:65, :]),
                  reads=["ps_oa"], writes=["onorm"])
            em.op("vector", lambda e: e.tensor_copy(lrs[1][0], ps_o2[1][64:65, :]),
                  reads=["ps_ob"], writes=["onorm"])
            for hh in range(2):
                r0 = hh * 64
                lv, lvk = lrs[hh]
                lb = lbc[r0:r0 + 64, hh * 512:hh * 512 + 512]
                pbc = ps_bc[r0:r0 + 64, :]
                tpb = (hh % 2) * 2048 + bb * 256
                em.op("scalar", lambda e, lv=lv: e.activation(lv, lv, AF.Ln),
                      reads=[lvk], writes=[lvk])
                em.op("scalar", lambda e, lv=lv: e.activation(
                    lv, lv, AF.Exp, scale=-1.0), reads=[lvk], writes=[lvk])
                epiq.append(("tensor", (lambda e, lv=lv, pbc=pbc: e.matmul(
                    pbc, oner[:, 0:64], lv, start=True, stop=True)),
                    ["oner", lvk], ["ps_bc"]))
                epiq.append(("vector", (lambda e, lb=lb, pbc=pbc: e.tensor_copy(lb, pbc)),
                    ["ps_bc"], ["lbc"]))
                epiq.append(("vector", (lambda e, r0=r0, lb=lb: e.tensor_mul(
                    onorm[r0:r0 + 64, :], zu[r0:r0 + 64, :], lb)),
                    ["tcos", "lbc"], ["onorm"]))
                for ql in range(16):
                    p0 = (ql % 2) * 64
                    ch = ql // 2
                    tcol = tpb + qc * 32
                    epiq.append(("gpsimd", (lambda e, ql=ql, p0=p0, ch=ch, tcol=tcol, r0=r0: e.tensor_copy(
                        sscrv[p0:p0 + 64, ch, tcol:tcol + 32],
                        onorm[r0:r0 + 64, :].rearrange("p (a b) -> p a b", b=16)[:, :, ql])),
                        ["onorm"], ["sscr"]))
        _flush(epiq)
    pbs = lbc
    em.dma("sync", lambda e: e.dma_start(out=pbs[0:1, :], in_=pbr), writes=["lbc"])
    for j in range(32):
        xb, xbk = xt[j % 2], f"xt{j % 2}"
        em.dma("sync", lambda e, j=j, xb=xb: e.dma_start(
            out=xb[:], in_=xc[j * 128:(j + 1) * 128, :]), writes=[xbk])
        fns = []
        for oc in range(2):
            fns += [
                (lambda e, kc=kc, j=j, oc=oc: e.matmul(
                    ps_pj[:, oc * 512:(oc + 1) * 512], sscrv[:, kc, j * 128:(j + 1) * 128],
                    pjs[:, kc * 1024 + oc * 512: kc * 1024 + (oc + 1) * 512],
                    start=(kc == 0), stop=False))
                for kc in range(8)]
            fns.append(lambda e, oc=oc: e.matmul(
                ps_pj[:, oc * 512:(oc + 1) * 512], oner[:, 0:128], pbs[0:1, oc * 512:(oc + 1) * 512],
                start=False, stop=True))
        em.op("tensor", fns, reads=["sscr", "pjslast", "oner", "lbc"],
              writes=["ps_s2a.lo", "ps_s2a.hi"])
        em.op("vector", lambda e, xb=xb: e.tensor_add(ot[:], xb[:], ps_pj[:]),
              reads=["ps_s2a.lo", "ps_s2a.hi", xbk], writes=["ot0"])
        em.dma("sync", lambda e, j=j: e.dma_start(
            out=x1[j * 128:(j + 1) * 128, :], in_=ot[:]), reads=["ot0"],
            writes=["g.x1"], slot="st_ot0")


def emit_phase_b(em, nc, es, x1, x2, qwT, kwT, vwT, owT, qkvb8, wbc, bbc, obr, id16, tag=""):
    sb = lambda n, s, d: es.enter_context(nc.sbuf_tensor(tag + n, s, d))
    ps = lambda n, s, d: es.enter_context(nc.psum_tensor(tag + n, s, d))

    xiT = sb("xiT", [128, 8 * T], BF16)
    aoT = sb("aoT", [128, 8 * T], BF16)
    sbid = sb("sbid", [128, 128], BF16)
    wstg = [sb("wstg0", [128, 1024], F32)] * 2
    qwh = sb("qwh", [128, 1024], BF16)
    kwh = sb("kwh", [128, 1024], BF16)
    vwh = sb("vwh", [128, 1024], BF16)
    qkb = sb("qkb", [128, 24], F32)
    wbcs = sb("wbcs", [128, 1024], F32)
    bbcs = sb("bbcs", [128, 1024], F32)
    obs = sb("obs", [1, 1024], F32)
    xt = [sb(f"xt{i}", [128, 1024], F32) for i in range(2)]
    ot = sb("ot0", [128, 1024], F32)
    xmt = sb("xmt", [128, 1024], F32)
    ybf = sb("ybf", [128, 1024], BF16)
    s1 = sb("s1", [128, 1], F32)
    ssq = sb("ssq", [128, 1], F32)
    st = sb("st", [128, 8], F32)
    lncb = sb("lncb", [128, 2], F32)
    sstash = sb("sstash", [128, 64], F32)
    qTh = sb("qTh", [128, 4096], BF16)
    kTh = sb("kTh", [128, 4096], BF16)
    Vh = sb("Vh", [128, 4096], BF16)
    lg = [sb(f"lg{i}", [128, 512], BF16) for i in range(2)]
    onec = sb("onec", [128, 1], BF16)
    oner = sb("oner", [1, 128], F32)
    nmx = sb("nmx", [1, 32], F32)
    cst = sb("cst", [1, 8], F32)
    bias_h = sb("bias_h", [128, 1], F32)
    pt = [sb(f"pt{i}", [128, 1024], BF16) for i in range(4)]
    linv = sb("linv", [1, 512], F32)
    lbc = sb("lbc", [128, 512], F32)

    ps_s2 = [ps("ps_s2a", [128, 1024], F32), ps("ps_s2b", [128, 1024], F32)]
    ps_o = ps("ps_o", [128, 512], F32)
    ps_l = ps("ps_l", [128, 512], F32)
    ps_t = ps("ps_t", [128, 1024], BF16)
    ps_a = ps_s2[0]
    ps_bc = ps_l
    ps_pj = ps_s2[0]

    sq = (wstg[0], "wstg0")
    lnbufs = (s1, ssq, sq, lncb)

    owTs = xiT
    xiTv = xiT[:].rearrange("p (c n) -> p c n", n=T)
    aoTv = aoT[:].rearrange("p (c n) -> p c n", n=T)
    Vhv = Vh[:].rearrange("p (s c) -> p s c", s=32)
    vstg = pt[2]
    q2 = pt[3]
    zu = wstg[0]

    em.dma("sync", lambda e: e.dma_start(out=sbid[:], in_=id16), writes=["sbid"])
    em.dma("sync", lambda e: e.dma_start(out=qkb[:], in_=qkvb8), writes=["qkb"])
    em.dma("sync", lambda e: e.dma_start(out=wbcs[:], in_=wbc), writes=["wbcs"])
    em.dma("sync", lambda e: e.dma_start(out=bbcs[:], in_=bbc), writes=["bbcs"])
    em.dma("sync", lambda e: e.dma_start(out=obs[:], in_=obr), writes=["obs"])
    em.op("vector", lambda e: e.memset(onec[:], 1.0), writes=["onec"])
    em.op("vector", lambda e: e.memset(oner[:], 1.0), writes=["oner"])
    em.op("vector", lambda e: e.memset(lncb[:, 0:1], D * 1e-5), writes=["lncb"])
    em.op("vector", lambda e: e.memset(lncb[:, 1:2], LND), writes=["lncb"])
    for j in range(NT):
        b, bk = xt[j % 2], f"xt{j % 2}"
        em.dma("sync", lambda e, j=j, b=b: e.dma_start(
            out=b[:], in_=x1[j * 128:(j + 1) * 128, :]), reads=["g.x1"], writes=[bk])
        emit_ln_stats(em, b[:], bk, st, 0, lnbufs)
        em.op("vector", lambda e, j=j: e.tensor_scalar(
            sstash[:, 2 * j:2 * j + 1], st[:, 0:1], 1.0, 1.0, ALU.mult, ALU.add),
            reads=["st"], writes=["sstash"])
        em.op("vector", lambda e, j=j: e.tensor_copy(
            sstash[:, 2 * j + 1:2 * j + 2], st[:, 1:2]),
            reads=["st"], writes=["sstash"])
        em.op("vector", lambda e, b=b: e.tensor_scalar(
            xmt[:], b[:], st[:, 0:1], st[:, 1:2], ALU.mult, ALU.add),
            reads=[bk, "st"], writes=["xmt"])
        emit_ln_stats(em, xmt[:], "xmt", st, 2, lnbufs)
        em.op("vector", lambda e: e.tensor_scalar(
            xmt[:], xmt[:], st[:, 2:3], st[:, 3:4], ALU.mult, ALU.add),
            reads=["xmt", "st"], writes=["xmt"])
        em.op("vector", lambda e: e.tensor_mul(xmt[:], xmt[:], wbcs[:]),
              reads=["xmt", "wbcs"], writes=["xmt"])
        em.op("vector", lambda e: e.tensor_add(ybf[:], xmt[:], bbcs[:]),
              reads=["xmt", "bbcs"], writes=["ybf"])
        em.op("tensor", [
            (lambda e, k=k: e.transpose(ps_t[:, k * 128:(k + 1) * 128],
                                        ybf[:, k * 128:(k + 1) * 128], sbid[:]))
            for k in range(8)],
            reads=["ybf", "sbid"], writes=["ps_t"])
        em.op("vector", lambda e, j=j: e.tensor_copy(
            xiTv[:, :, j * 128: j * 128 + 128],
            ps_t[:].rearrange("p (c n) -> p c n", n=128)),
            reads=["ps_t"], writes=["xiT"])

    for h in range(8):
        for wdram, slab, sk in [(qwT, qwh, "qwh"), (kwT, kwh, "kwh"), (vwT, vwh, "vwh")]:
            b, bk = wstg[0], "wstg0"
            em.dma("sync", lambda e, wdram=wdram, h=h, b=b: e.dma_start(
                out=b[:], in_=wdram[h * 128:(h + 1) * 128, :]), writes=[bk])
            em.op("vector", lambda e, slab=slab, b=b: e.tensor_copy(slab[:], b[:]),
                  reads=[bk], writes=[sk])
        for tc in range(NTC):
            ex = tc // 2
            nkc = 8 >> ex
            em.op("tensor", [
                (lambda e, kc=kc, tc=tc, last=(kc == nkc - 1): e.matmul(
                    ps_a[:, 0:512], qwh[:, kc * 128:(kc + 1) * 128],
                    xiTv[:, kc, tc * TC:(tc + 1) * TC], start=(kc == 0), stop=last))
                for kc in range(nkc)],
                reads=["qwh", "xiT"], writes=["ps_s2a"])
            em.op("vector", lambda e, tc=tc, h=h: e.tensor_scalar_add(
                qTh[:, tc * TC:(tc + 1) * TC], ps_a[:, 0:512], qkb[:, h:h + 1]),
                reads=["ps_s2a", "qkb"], writes=["qTh"])
            em.op("tensor", [
                (lambda e, kc=kc, tc=tc, last=(kc == nkc - 1): e.matmul(
                    ps_a[:, 512:1024], kwh[:, kc * 128:(kc + 1) * 128],
                    xiTv[:, kc, tc * TC:(tc + 1) * TC], start=(kc == 0), stop=last))
                for kc in range(nkc)],
                reads=["kwh", "xiT"], writes=["ps_s2a"])
            em.op("vector", lambda e, tc=tc, h=h: e.tensor_scalar_add(
                kTh[:, tc * TC:(tc + 1) * TC], ps_a[:, 512:1024], qkb[:, 8 + h:8 + h + 1]),
                reads=["ps_s2a", "qkb"], writes=["kTh"])
            em.op("tensor", [
                (lambda e, kc=kc, tc=tc, last=(kc == nkc - 1): e.matmul(
                    ps_s2[1][:, 0:512], vwh[:, kc * 128:(kc + 1) * 128],
                    xiTv[:, kc, tc * TC:(tc + 1) * TC], start=(kc == 0), stop=last))
                for kc in range(nkc)],
                reads=["vwh", "xiT"], writes=["ps_s2b"])
            em.op("vector", lambda e, tc=tc, h=h: e.tensor_scalar_add(
                vstg[:, 0:512], ps_s2[1][:, 0:512], qkb[:, 16 + h:16 + h + 1]),
                reads=["ps_s2b", "qkb"], writes=["pt2"])
            em.op("tensor", [
                (lambda e, k=k: e.transpose(ps_t[:, k * 128:(k + 1) * 128],
                                            vstg[:, k * 128:(k + 1) * 128], sbid[:]))
                for k in range(4)],
                reads=["pt2", "sbid"], writes=["ps_t"])
            em.op("vector", lambda e, tc=tc: e.tensor_copy(
                Vhv[:, tc * 4:tc * 4 + 4, :],
                ps_t[:, 0:512].rearrange("p (s c) -> p s c", c=128)),
                reads=["ps_t"], writes=["Vh"])
        for src, srck, off in [(qTh, "qTh", 0), (kTh, "kTh", 8)]:
            for tc in range(NTC):
                qb, qbk = (q2, "pt3") if tc % 2 == 0 else (pt[1], "pt1")
                em.op("scalar", lambda e, src=src, tc=tc, qb=qb: e.activation(
                    qb[:, 0:512], src[:, tc * TC:(tc + 1) * TC], AF.Square),
                    reads=[srck], writes=[qbk])
                em.op("tensor", lambda e, qb=qb: e.matmul(
                    ps_bc[0:1, :], onec[:], qb[:, 0:512], start=True, stop=True),
                    reads=[qbk, "onec"], writes=["ps_l"])
                em.op("vector", lambda e, tc=tc, off=off: e.tensor_reduce(
                    nmx[:, off + tc:off + tc + 1], ps_bc[0:1, :], AX.X, ALU.max),
                    reads=["ps_l"], writes=["nmx"])
        em.op("vector", lambda e: e.tensor_reduce(cst[:, 0:1], nmx[:, 0:8], AX.X, ALU.max),
              reads=["nmx"], writes=["cst"])
        em.op("vector", lambda e: e.tensor_reduce(cst[:, 1:2], nmx[:, 8:16], AX.X, ALU.max),
              reads=["nmx"], writes=["cst"])
        em.op("vector", lambda e: e.tensor_mul(cst[:, 2:3], cst[:, 0:1], cst[:, 1:2]),
              reads=["cst"], writes=["cst"])
        em.op("scalar", lambda e: e.activation(cst[:, 3:4], cst[:, 2:3], AF.Ln),
              reads=["cst"], writes=["cst"])
        em.op("scalar", lambda e: e.activation(cst[:, 4:5], cst[:, 3:4], AF.Exp, scale=0.5),
              reads=["cst"], writes=["cst"])
        em.op("vector", lambda e: e.tensor_scalar(
            cst[:, 5:6], cst[:, 4:5], -1.02 / 32.0, -1e-3, ALU.mult, ALU.add),
            reads=["cst"], writes=["cst"])
        em.op("tensor", lambda e: e.matmul(ps_bc[:, 0:1], oner[:], cst[:, 5:6],
                                           start=True, stop=True),
              reads=["oner", "cst"], writes=["ps_l"])
        em.op("scalar", lambda e: e.copy(bias_h[:], ps_bc[:, 0:1]),
              reads=["ps_l"], writes=["bias_h"])

        pvq = []
        epiq = []
        def _flush(queue, n_keep=0):
            while len(queue) > n_keep:
                eng, fn, rd, wr = queue.pop(0)
                em.op(eng, fn, reads=rd, writes=wr)
        for qc in range(NTC):
            for sp in range(0, 32, 2):
                if sp >= 2:
                    _flush(epiq, max(0, len(epiq) - 2) if sp < 30 else 0)
                pi = (sp // 2) % 2
                pss, pssk = ps_s2[pi], f"ps_s2{'ab'[pi]}"
                pb, pbk = pt[(sp // 2) % 4], f"pt{(sp // 2) % 4}"
                em.op("tensor", [
                    (lambda e, sbi=sp + d, d=d, qc=qc, pss=pss: e.matmul(
                        pss[:, d * TC:(d + 1) * TC],
                        kTh[:, sbi * 128:(sbi + 1) * 128],
                        qTh[:, qc * TC:(qc + 1) * TC], start=True, stop=True))
                    for d in range(2)],
                    reads=["kTh", "qTh"], writes=[pssk])
                em.op("scalar", lambda e, pb=pb, pss=pss: e.activation(
                    pb[:], pss[:], AF.Exp, bias=bias_h[:, 0:1], scale=1.0 / 32.0),
                    reads=[pssk, "bias_h"], writes=[pbk])
                gi = sp // 16
                if sp % 16 == 0:
                    em.op("vector", lambda e, pb=pb, gi=gi: e.tensor_add(
                        lg[gi][:], pb[:, 0:512], pb[:, 512:1024]),
                        reads=[pbk], writes=[f"lg{gi}"])
                else:
                    em.op("vector", lambda e, pb=pb, gi=gi: e.tensor_add(
                        lg[gi][:], lg[gi][:], pb[:, 0:512]),
                        reads=[pbk, f"lg{gi}"], writes=[f"lg{gi}"])
                    em.op("vector", lambda e, pb=pb, gi=gi: e.tensor_add(
                        lg[gi][:], lg[gi][:], pb[:, 512:1024]),
                        reads=[pbk, f"lg{gi}"], writes=[f"lg{gi}"])
                fns = [
                    (lambda e, sbi=sp + d, d=d, pb=pb: e.matmul(
                        ps_o[:], Vhv[:, sbi, :], pb[:, d * TC:(d + 1) * TC],
                        start=(sbi == 0), stop=(sbi == 31)))
                    for d in range(2)]
                pvq.append(("tensor", fns, ["Vh", pbk], ["ps_o"]))
                while len(pvq) > 2:
                    eng, fn, rd, wr = pvq.pop(0)
                    em.op(eng, fn, reads=rd, writes=wr)
            _flush(pvq)
            em.op("vector", lambda e: e.tensor_add(lg[0][:], lg[0][:], lg[1][:]),
                  reads=["lg0", "lg1"], writes=["lg0"])
            em.op("tensor", lambda e: e.matmul(
                ps_l[0:1, :], onec[:], lg[0][:], start=True, stop=True),
                reads=["lg0", "onec"], writes=["ps_l"])
            em.op("vector", lambda e: e.tensor_copy(zu[:, 0:512], ps_o[:]),
                  reads=["ps_o"], writes=["wstg0"])
            em.op("scalar", lambda e: e.activation(linv[:], ps_l[0:1, :], AF.Ln),
                  reads=["ps_l"], writes=["linv"])
            em.op("scalar", lambda e: e.activation(linv[:], linv[:], AF.Exp, scale=-1.0),
                  reads=["linv"], writes=["linv"])
            em.op("tensor", lambda e: e.matmul(
                ps_bc[:, :], oner[:], linv[:], start=True, stop=True),
                reads=["oner", "linv"], writes=["ps_l"])
            em.op("vector", lambda e: e.tensor_copy(lbc[:], ps_bc[:, :]),
                  reads=["ps_l"], writes=["lbc"])
            epiq.append(("vector", (lambda e, h=h, qc=qc: e.tensor_mul(
                aoTv[:, h, qc * TC:(qc + 1) * TC], zu[:, 0:512], lbc[:])),
                ["wstg0", "lbc"], ["aoT"]))
        _flush(epiq)
    for kc in range(8):
        b, bk = wstg[0], "wstg0"
        em.dma("sync", lambda e, kc=kc, b=b: e.dma_start(
            out=b[:], in_=owT[kc * 128:(kc + 1) * 128, :]), writes=[bk])
        em.op("vector", lambda e, kc=kc, b=b: e.tensor_copy(
            owTs[:, kc * 1024:(kc + 1) * 1024], b[:]), reads=[bk], writes=["xiT"])

    for j in range(NT):
        ex = j // 8
        m = 1024 >> ex
        nkc = 8 >> ex
        b, bk = xt[j % 2], f"xt{j % 2}"
        em.dma("sync", lambda e, j=j, b=b: e.dma_start(
            out=b[:], in_=x1[j * 128:(j + 1) * 128, :]), reads=["g.x1"], writes=[bk])
        em.op("vector", lambda e, j=j, b=b: e.tensor_scalar(
            ot[:], b[:], sstash[:, 2 * j:2 * j + 1], sstash[:, 2 * j + 1:2 * j + 2],
            ALU.mult, ALU.add),
            reads=[bk, "sstash"], writes=["ot0"])
        fns = []
        for oc in range((m + 511) // 512):
            ow = min(512, m - oc * 512)
            fns += [
                (lambda e, kc=kc, j=j, oc=oc, ow=ow: e.matmul(
                    ps_pj[:, oc * 512:oc * 512 + ow], aoTv[:, kc, j * 128:(j + 1) * 128],
                    owTs[:, kc * 1024 + oc * 512: kc * 1024 + oc * 512 + ow],
                    start=(kc == 0), stop=False))
                for kc in range(nkc)]
            fns.append(lambda e, j=j, oc=oc, ow=ow: e.matmul(
                ps_pj[:, oc * 512:oc * 512 + ow], oner[:, 0:128], obs[:, oc * 512:oc * 512 + ow],
                start=False, stop=True))
        em.op("tensor", fns, reads=["aoT", "xiT", "oner", "obs"], writes=["ps_s2a"])
        em.op("vector", lambda e, m=m: e.tensor_add(
            ot[:, 0:m], ot[:, 0:m], ps_pj[:, 0:m]),
            reads=["ps_s2a", "ot0"], writes=["ot0"])
        em.dma("sync", lambda e, j=j: e.dma_start(
            out=x2[j * 128:(j + 1) * 128, :], in_=ot[:]), reads=["ot0"],
            writes=["g.x2"], slot="st_ot0")


def _mk_qkv_evac(qTp, kTp, vstg, Vpv, ps_t, sbid, TCc):
    def _emit(em, job):
        kind, tc, psx, half, pxk = job
        if kind == "q":
            em.op("scalar", lambda e: e.copy(qTp[:, tc * TCc:(tc + 1) * TCc], psx[:, half:half + 512]),
                  reads=[pxk], writes=["qTp"])
        elif kind == "k":
            em.op("scalar", lambda e: e.copy(kTp[:, tc * TCc:(tc + 1) * TCc], psx[:, half:half + 512]),
                  reads=[pxk], writes=["kTp"])
        else:
            em.op("scalar", lambda e: e.copy(vstg[:, 0:512], psx[:, half:half + 512]),
                  reads=[pxk], writes=["pt2"])
            em.op("tensor", [
                (lambda e, k=k: e.transpose(ps_t[:, k * 128:(k + 1) * 128],
                                            vstg[:, k * 128:(k + 1) * 128], sbid[:]))
                for k in range(4)],
                reads=["pt2", "sbid"], writes=["ps_t"])
            for k in range(4):
                sbk = tc * 4 + k
                em.op("vector", lambda e, k=k, sbk=sbk: e.tensor_copy(
                    Vpv[:, 0, sbk, 0:64], ps_t[:, k * 128:k * 128 + 64]),
                    reads=["ps_t"], writes=["Vp"])
                em.op("vector", lambda e, k=k, sbk=sbk: e.tensor_copy(
                    Vpv[:, 1, sbk, 0:64], ps_t[:, k * 128 + 64:k * 128 + 128]),
                    reads=["ps_t"], writes=["Vp"])
    return _emit


def emit_phase_c(em, nc, es, x2, x3, wq, wk, wv, wo, bob, id16, tag=""):
    sb = lambda n, s, d: es.enter_context(nc.sbuf_tensor(tag + n, s, d))
    ps = lambda n, s, d: es.enter_context(nc.psum_tensor(tag + n, s, d))

    xiT = sb("xiT", [128, 8 * T], BF16)
    zT = sb("zT", [128, 8 * T], BF16)
    wos = sb("wos", [128, 8192], BF16)
    bo = sb("bo", [1, 1024], F32)
    sbid = sb("sbid", [128, 128], BF16)
    wstg = [sb("wstg0", [128, 1024], F32)] * 2
    wqp = sb("wqp", [128, 1024], BF16)
    wkp = sb("wkp", [128, 1024], BF16)
    wvp = sb("wvp", [128, 1024], BF16)
    xt = [sb(f"xt{i}", [128, 1024], F32) for i in range(2)]
    ot = [sb("ot0", [128, 1024], F32)] * 2
    ybf2 = [sb(f"ybf{i}", [128, 1024], BF16) for i in range(2)]
    s12 = [sb(f"s1{i}", [128, 1], F32) for i in range(2)]
    ssq2 = [sb(f"ssq{i}", [128, 1], F32) for i in range(2)]
    st2 = [sb(f"st{i}", [128, 8], F32) for i in range(2)]
    qTp = sb("qTp", [128, 4096], BF16)
    kTp = sb("kTp", [128, 4096], BF16)
    Vp = sb("Vp", [128, 2 * 32 * 65], BF16)
    onec = sb("onec", [128, 1], BF16)
    oner = sb("oner", [1, 128], F32)
    nmx = sb("nmx", [1, 32], F32)
    cst = sb("cst", [1, 8], F32)
    bias_p = sb("bias_p", [128, 1], F32)
    pt = [sb(f"pt{i}", [128, 1024], BF16) for i in range(4)]

    ps_s2 = [ps("ps_s2a", [128, 1024], F32), ps("ps_s2b", [128, 1024], F32)]
    ps_o2 = [ps("ps_oa", [128, 512], F32), ps("ps_ob", [128, 512], F32)]
    ps_bc = ps("ps_bc", [128, 512], F32)
    ps_t = ps("ps_t", [128, 1024], BF16)
    ps_pj = ps_s2[0]

    lncb = sb("lncb", [128, 2], F32)
    lnsets = [
        ((s12[0], "s10"), (ssq2[0], "ssq0"), (wstg[0], "wstg0"), (st2[0], "st0"), lncb),
        ((s12[1], "s11"), (ssq2[1], "ssq1"), (ot[0], "ot0"), (st2[1], "st1"), lncb),
    ]

    xiTv = xiT[:].rearrange("p (c n) -> p c n", n=T)
    zTv = zT[:].rearrange("p (c n) -> p c n", n=T)
    Vpv = Vp[:].rearrange("p (h s c) -> p h s c", h=2, s=32)
    vstg = pt[2]
    q2 = pt[3]
    lbc = wstg[0]
    linv2 = [(xt[0], "xt0"), (xt[1], "xt1")]
    _emit_qkv_evac = _mk_qkv_evac(qTp, kTp, vstg, Vpv, ps_t, sbid, TC)

    em.dma("sync", lambda e: e.dma_start(out=sbid[:], in_=id16), writes=["sbid"])
    em.dma("sync", lambda e: e.dma_start(out=bo[:], in_=bob[0:1, :]), writes=["bo"])
    em.op("vector", lambda e: e.memset(onec[:], 1.0), writes=["onec"])
    em.op("vector", lambda e: e.memset(lncb[:, 0:1], D * 1e-5), writes=["lncb"])
    em.op("vector", lambda e: e.memset(lncb[:, 1:2], LND), writes=["lncb"])
    em.op("vector", lambda e: e.memset(oner[:], 1.0), writes=["oner"])
    em.op("vector", lambda e: e.memset(Vpv[:, :, :, 64:65], 1.0), writes=["Vp"])
    for kc in range(8):
        b = wstg[0]
        bk = "wstg0"
        em.dma("sync", lambda e, kc=kc, b=b: e.dma_start(
            out=b[:], in_=wo[kc * 128:(kc + 1) * 128, :]), writes=[bk])
        em.op("vector", lambda e, kc=kc, b=b: e.tensor_copy(
            wos[:, kc * 1024:(kc + 1) * 1024], b[:]), reads=[bk], writes=["wos"])

    for j in range(NT):
        b = xt[j % 2]
        bk = f"xt{j % 2}"
        em.dma("sync", lambda e, j=j, b=b: e.dma_start(
            out=b[:], in_=x2[j * 128:(j + 1) * 128, :]), reads=["g.x2"], writes=[bk])
        ybf, ybfk = ybf2[j % 2], f"ybf{j % 2}"
        emit_ln_tile(em, b, bk, ybf, ybfk, lnsets[j % 2])
        em.op("tensor", [
            (lambda e, k=k, ybf=ybf: e.transpose(ps_t[:, k * 128:(k + 1) * 128],
                                        ybf[:, k * 128:(k + 1) * 128], sbid[:]))
            for k in range(8)],
            reads=[ybfk, "sbid"], writes=["ps_t"])
        em.op("vector", lambda e, j=j: e.tensor_copy(
            xiTv[:, :, j * 128: j * 128 + 128],
            ps_t[:].rearrange("p (c n) -> p c n", n=128)),
            reads=["ps_t"], writes=["xiT"])

    for hp in range(8):
        for wdram, slab, sk in [(wq, wqp, "wqp"), (wk, wkp, "wkp"), (wv, wvp, "wvp")]:
            b = wstg[0]
            bk = "wstg0"
            em.dma("sync", lambda e, wdram=wdram, hp=hp, b=b: e.dma_start(
                out=b[:], in_=wdram[hp * 128:(hp + 1) * 128, :]), writes=[bk])
            em.op("vector", lambda e, slab=slab, b=b: e.tensor_copy(slab[:], b[:]),
                  reads=[bk], writes=[sk])
        qkv_jobs = []
        for tc in range(NTC):
            qkv_jobs.append(("q", tc, wqp, "wqp"))
            qkv_jobs.append(("k", tc, wkp, "wkp"))
            qkv_jobs.append(("v", tc, wvp, "wvp"))
        pending = None
        for gi, (kind, tc, wslab, wkey) in enumerate(qkv_jobs):
            psx = ps_s2[(gi // 2) % 2]
            half = (gi % 2) * 512
            pxk = f"ps_s2{'ab'[(gi // 2) % 2]}"
            em.op("tensor", [
                (lambda e, kc=kc, tc=tc, wslab=wslab, psx=psx, half=half: e.matmul(
                    psx[:, half:half + 512], wslab[:, kc * 128:(kc + 1) * 128],
                    xiTv[:, kc, tc * TC:(tc + 1) * TC], start=(kc == 0), stop=(kc == 7)))
                for kc in range(8)],
                reads=[wkey, "xiT"], writes=[pxk])
            if pending is not None:
                _emit_qkv_evac(em, pending)
            pending = (kind, tc, psx, half, pxk)
        _emit_qkv_evac(em, pending)
        for src, srck, off in [(qTp, "qTp", 0), (kTp, "kTp", 8)]:
            for tc in range(NTC):
                qb, qbk = (q2, "pt3") if tc % 2 == 0 else (pt[1], "pt1")
                em.op("scalar", lambda e, src=src, tc=tc, qb=qb: e.activation(
                    qb[:, 0:512], src[:, tc * TC:(tc + 1) * TC], AF.Square),
                    reads=[srck], writes=[qbk])
                em.op("tensor", lambda e, qb=qb: e.matmul(
                    ps_bc[0:1, :], onec[:], qb[:, 0:512],
                    start=True, stop=True), reads=[qbk, "onec"], writes=["ps_bc"])
                em.op("vector", lambda e, tc=tc, off=off: e.tensor_reduce(
                    nmx[:, off + tc:off + tc + 1], ps_bc[0:1, :], AX.X, ALU.max),
                    reads=["ps_bc"], writes=["nmx"])
        em.op("vector", lambda e: e.tensor_reduce(cst[:, 0:1], nmx[:, 0:8], AX.X, ALU.max),
              reads=["nmx"], writes=["cst"])
        em.op("vector", lambda e: e.tensor_reduce(cst[:, 1:2], nmx[:, 8:16], AX.X, ALU.max),
              reads=["nmx"], writes=["cst"])
        em.op("vector", lambda e: e.tensor_mul(cst[:, 2:3], cst[:, 0:1], cst[:, 1:2]),
              reads=["cst"], writes=["cst"])
        em.op("scalar", lambda e: e.activation(cst[:, 3:4], cst[:, 2:3], AF.Ln),
              reads=["cst"], writes=["cst"])
        em.op("scalar", lambda e: e.activation(cst[:, 4:5], cst[:, 3:4], AF.Exp, scale=0.5),
              reads=["cst"], writes=["cst"])
        em.op("vector", lambda e: e.tensor_scalar(
            cst[:, 5:6], cst[:, 4:5], -1.02 / 8.0, -1e-3, ALU.mult, ALU.add),
            reads=["cst"], writes=["cst"])
        em.op("tensor", lambda e: e.matmul(ps_bc[:, 0:1], oner[:], cst[:, 5:6],
                                           start=True, stop=True),
              reads=["oner", "cst"], writes=["ps_bc"])
        em.op("scalar", lambda e: e.copy(bias_p[:], ps_bc[:, 0:1]),
              reads=["ps_bc"], writes=["bias_p"])

        pvq = []
        epiq = []
        def _flush(queue, n_keep=0):
            while len(queue) > n_keep:
                eng, fn, rd, wr = queue.pop(0)
                em.op(eng, fn, reads=rd, writes=wr)
        for qc in range(NTC):
            nsb = 4 * qc + 4
            for sbi in range(nsb):
                if sbi >= 2:
                    _flush(epiq, max(0, len(epiq) - 1) if sbi < nsb - 1 else 0)

                par = sbi % 2
                pss, pssk = ps_s2[par], f"ps_s2{'ab'[par]}"
                pb, pbk = pt[sbi % 4], f"pt{sbi % 4}"
                em.op("tensor", [
                    (lambda e, sbi=sbi, hh=hh, qc=qc, pss=pss: e.matmul(
                        pss[:, hh * TC:(hh + 1) * TC],
                        kTp[hh * 64:hh * 64 + 64, sbi * 128:(sbi + 1) * 128],
                        qTp[hh * 64:hh * 64 + 64, qc * TC:(qc + 1) * TC],
                        start=True, stop=True))
                    for hh in range(2)],
                    reads=["kTp", "qTp"], writes=[pssk])
                em.op("scalar", lambda e, pb=pb, pss=pss: e.activation(
                    pb[:], pss[:], AF.Exp, bias=bias_p[:, 0:1], scale=0.125),
                    reads=[pssk, "bias_p"], writes=[pbk])
                if sbi >= 4 * qc:
                    em.op("gpsimd", lambda e, pb=pb, qc=qc, sbi=sbi: e.affine_select(
                        pb[:].rearrange("p (h n) -> p h n", h=2),
                        pb[:].rearrange("p (h n) -> p h n", h=2),
                        [[0, 2], [1, TC]], ALU.is_ge, 0.0,
                        base=qc * TC - sbi * 128, channel_multiplier=-1),
                        reads=[pbk], writes=[pbk])
                pvq.append(("tensor", [
                    (lambda e, sbi=sbi, hh=hh, nsb=nsb, pb=pb: e.matmul(
                        ps_o2[hh][0:65, :], Vpv[:, hh, sbi, :],
                        pb[:, hh * TC:(hh + 1) * TC],
                        start=(sbi == 0), stop=(sbi == nsb - 1)))
                    for hh in range(2)],
                    ["Vp", pbk], ["ps_oa", "ps_ob"]))
                _flush(pvq, 3)
            _flush(pvq)
            zu = ot[0]
            em.op("vector", lambda e: e.tensor_copy(zu[0:64, 0:512], ps_o2[0][0:64, :]),
                  reads=["ps_oa"], writes=["ot0"])
            em.op("vector", lambda e: e.tensor_copy(zu[64:128, 0:512], ps_o2[1][0:64, :]),
                  reads=["ps_ob"], writes=["ot0"])
            em.op("vector", lambda e: e.tensor_copy(xt[0][0:1, 0:512], ps_o2[0][64:65, :]),
                  reads=["ps_oa"], writes=["xt0"])
            em.op("vector", lambda e: e.tensor_copy(xt[1][0:1, 0:512], ps_o2[1][64:65, :]),
                  reads=["ps_ob"], writes=["xt1"])
            for hh in range(2):
                r0 = hh * 64
                lv, lvk = linv2[hh]
                lb = lbc[r0:r0 + 64, hh * 512:hh * 512 + 512]
                pbc = ps_bc[r0:r0 + 64, :]
                em.op("scalar", lambda e, lv=lv: e.activation(
                    lv[0:1, 0:512], lv[0:1, 0:512], AF.Ln),
                    reads=[lvk], writes=[lvk])
                em.op("scalar", lambda e, lv=lv: e.activation(
                    lv[0:1, 0:512], lv[0:1, 0:512], AF.Exp, scale=-1.0),
                    reads=[lvk], writes=[lvk])
                epiq.append(("tensor", (lambda e, lv=lv, pbc=pbc: e.matmul(
                    pbc, oner[:, 0:64], lv[0:1, 0:512],
                    start=True, stop=True)),
                    ["oner", lvk], ["ps_bc"]))
                epiq.append(("vector", (lambda e, lb=lb, pbc=pbc: e.tensor_copy(
                    lb, pbc)),
                    ["ps_bc"], ["wstg0"]))
                epiq.append(("vector", (lambda e, hp=hp, r0=r0, qc=qc, lb=lb: e.tensor_mul(
                    zTv[r0:r0 + 64, hp, qc * TC:(qc + 1) * TC],
                    zu[r0:r0 + 64, 0:512], lb)),
                    ["ot0", "wstg0"], ["zT"]))
        _flush(epiq)

    for j in range(NT):
        b = xt[j % 2]
        bk = f"xt{j % 2}"
        ob = ot[0]
        obk = "ot0"
        em.dma("sync", lambda e, j=j, b=b: e.dma_start(
            out=b[:], in_=x2[j * 128:(j + 1) * 128, :]), reads=["g.x2"], writes=[bk])
        fns = [
            (lambda e, kc=kc, j=j, oc=oc: e.matmul(
                ps_pj[:, oc * 512:(oc + 1) * 512], zTv[:, kc, j * 128:(j + 1) * 128],
                wos[:, kc * 1024 + oc * 512: kc * 1024 + (oc + 1) * 512],
                start=(kc == 0), stop=False))
            for oc in range(2) for kc in range(8)]
        fns.append(lambda e: e.matmul(ps_pj[:, 0:512], oner[:, 0:128], bo[:, 0:512],
                                      start=False, stop=False))
        fns.append(lambda e: e.matmul(ps_pj[:, 512:1024], oner[:, 0:128], bo[:, 512:1024],
                                      start=False, stop=True))
        em.op("tensor", fns, reads=["zT", "wos", "oner", "bo"], writes=["ps_s2a"])
        em.op("vector", lambda e, ob=ob, b=b: e.tensor_add(ob[:], b[:], ps_pj[:]),
              reads=["ps_s2a", bk], writes=[obk])
        em.dma("sync", lambda e, j=j, ob=ob: e.dma_start(
            out=x3[j * 128:(j + 1) * 128, :], in_=ob[:]), reads=[obk],
            writes=["g.x3"], slot="st_" + obk)


def emit_phase_d(em, nc, es, x3, out, wrT, wkT, wvT, mu8, id16, id32, tag=""):
    sb = lambda n, s, d: es.enter_context(nc.sbuf_tensor(tag + n, s, d))
    ps = lambda n, s, d: es.enter_context(nc.psum_tensor(tag + n, s, d))

    yT = sb("yT", [128, NC * (T + 1)], BF16)
    w_r = sb("w_r", [128, 8192], BF16)
    w_k = sb("w_k", [128, 8192], BF16)
    w_v = sb("w_v", [128, 8192], BF16)
    sbid = sb("sbid", [128, 128], BF16)
    sbid32 = sb("sbid32", [128, 128], F32)
    mu = sb("mu", [128, 16], F32)
    negmu = sb("negmu", [128, 16], F32)
    wf = [sb(f"wf{i}", [128, 1024], F32) for i in range(2)]
    xt = [sb(f"xt{i}", [128, 1024], F32) for i in range(2)]
    ot = [sb(f"ot{i}", [128, 1024], F32) for i in range(2)]
    xr = xt
    s1 = sb("s1", [128, 1], F32)
    ssq = sb("ssq", [128, 1], F32)
    st = sb("st", [128, 8], F32)
    ybf = sb("ybf", [128, 1024], BF16)
    t1r = sb("t1r", [128, 512], F32)
    t1k = sb("t1k", [128, 512], F32)
    t2 = sb("t2", [128, 512], F32)
    xrT = sb("xrT", [128, 4096], BF16)
    xkT = sb("xkT", [128, 4096], BF16)
    rT = sb("rT", [128, 4096], F32)
    kaT = sb("kaT", [128, 4096], BF16)
    outT = rT

    ps_t = ps("ps_t", [128, 1024], BF16)
    ps_r2 = [ps("ps_r", [128, 512], F32), ps("ps_rb", [128, 512], F32)]
    ps_k = ps("ps_k", [128, 512], F32)
    ps_v2 = [ps("ps_v", [128, 512], F32), ps("ps_vb", [128, 512], F32)]
    ps_tb = ps("ps_tb", [128, 1024], F32)

    yTv = yT[:].rearrange("p (c n) -> p c n", n=T + 1)

    em.dma("sync", lambda e: e.dma_start(out=sbid[:], in_=id16), writes=["sbid"])
    em.dma("sync", lambda e: e.dma_start(out=sbid32[:], in_=id32), writes=["sbid32"])
    em.dma("sync", lambda e: e.dma_start(out=mu[:], in_=mu8), writes=["mu"])
    em.op("vector", lambda e: e.tensor_scalar(negmu[:], mu[:], -1.0, 1.0, ALU.mult, ALU.add),
          reads=["mu"], writes=["negmu"])
    wjobs = []
    for wi, (slab, wd) in enumerate([(w_r, wrT), (w_k, wkT), (w_v, wvT)]):
        sn = f"w{wi}"
        for kc in range(8):
            b = wf[(wi * 8 + kc) % 2]
            bk = f"wf{(wi * 8 + kc) % 2}"
            wjobs.append((slab, sn, kc, b, bk, wd))
    em.op("vector", lambda e: e.memset(yTv[:, :, 0:1], 0.0), writes=["yT"])
    sq = ot[0]

    for j in range(NT):
        b = xt[j % 2]
        bk = f"xt{j % 2}"
        em.dma("sync", lambda e, j=j, b=b: e.dma_start(
            out=b[:], in_=x3[j * 128:(j + 1) * 128, :]), reads=["g.x3"], writes=[bk])
        em.op("vector", lambda e, b=b: e.tensor_reduce(s1[:], b[:], AX.X, ALU.add),
              reads=[bk], writes=["s1"])
        em.op("scalar", lambda e, b=b: e.activation(sq[:], b[:], AF.Square, accum_out=ssq[:]),
              reads=[bk], writes=["ot0", "ssq"])
        if 0 <= j - 2 < len(wjobs):
            slab, sn, kc, wb, wbk, wd = wjobs[j - 2]
            em.op("scalar", lambda e, slab=slab, kc=kc, wb=wb: e.copy(
                slab[:, kc * 1024:(kc + 1) * 1024], wb[:]),
                reads=[wbk], writes=[sn + "s"])
        if j < len(wjobs):
            slab, sn, kc, wb, wbk, wd = wjobs[j]
            em.dma("sync", lambda e, wd=wd, kc=kc, wb=wb: e.dma_start(
                out=wb[:], in_=wd[kc * 128:(kc + 1) * 128, :]), writes=[wbk])
        em.op("vector", lambda e: e.tensor_scalar_mul(st[:, 0:1], s1[:], 1.0 / D),
              reads=["s1"], writes=["st"])
        em.op("vector", lambda e: e.tensor_scalar_mul(st[:, 1:2], ssq[:], 1.0 / D),
              reads=["ssq"], writes=["st"])
        em.op("vector", lambda e: e.tensor_mul(st[:, 2:3], st[:, 0:1], st[:, 0:1]),
              reads=["st"], writes=["st"])
        em.op("vector", lambda e: e.tensor_sub(st[:, 2:3], st[:, 1:2], st[:, 2:3]),
              reads=["st"], writes=["st"])
        em.op("vector", lambda e: e.tensor_scalar_add(st[:, 2:3], st[:, 2:3], 1e-5),
              reads=["st"], writes=["st"])
        em.op("scalar", lambda e: e.activation(st[:, 3:4], st[:, 2:3], AF.Ln),
              reads=["st"], writes=["st"])
        em.op("scalar", lambda e: e.activation(st[:, 3:4], st[:, 3:4], AF.Exp, scale=-0.5),
              reads=["st"], writes=["st"])
        em.op("vector", lambda e: e.scalar_tensor_tensor(
            st[:, 4:5], st[:, 0:1], -1.0, st[:, 3:4], ALU.mult, ALU.mult),
            reads=["st"], writes=["st"])
        em.op("vector", lambda e, b=b: e.tensor_scalar(
            ybf[:], b[:], st[:, 3:4], st[:, 4:5], ALU.mult, ALU.add),
            reads=[bk, "st"], writes=["ybf"])
        em.op("tensor", [
            (lambda e, k=k: e.transpose(ps_t[:, k * 128:(k + 1) * 128],
                                        ybf[:, k * 128:(k + 1) * 128], sbid[:]))
            for k in range(8)],
            reads=["ybf", "sbid"], writes=["ps_t"])
        em.op("vector", lambda e, j=j: e.tensor_copy(
            yTv[:, :, 1 + j * 128: 1 + j * 128 + 128],
            ps_t[:].rearrange("p (c n) -> p c n", n=128)),
            reads=["ps_t"], writes=["yT"])

    def _emit_prep(tc):
        for kc in range(NC):
            ysh = yTv[:, kc, tc * TC: tc * TC + TC]
            yun = yTv[:, kc, 1 + tc * TC: 1 + tc * TC + TC]
            em.op("vector", lambda e, ysh=ysh, kc=kc: e.tensor_scalar_mul(
                t1r[:], ysh, negmu[:, kc:kc + 1]), reads=["yT", "negmu"], writes=["t1r"])
            em.op("vector", lambda e, yun=yun, kc=kc: e.scalar_tensor_tensor(
                xrT[:, kc * TC:(kc + 1) * TC], yun, mu[:, kc:kc + 1], t1r[:], ALU.mult, ALU.add),
                reads=["yT", "mu", "t1r"], writes=["xrT"])
            em.op("vector", lambda e, ysh=ysh, kc=kc: e.tensor_scalar_mul(
                t1k[:], ysh, negmu[:, 8 + kc:8 + kc + 1]), reads=["yT", "negmu"], writes=["t1k"])
            em.op("vector", lambda e, yun=yun, kc=kc: e.scalar_tensor_tensor(
                xkT[:, kc * TC:(kc + 1) * TC], yun, mu[:, 8 + kc:8 + kc + 1], t1k[:], ALU.mult, ALU.add),
                reads=["yT", "mu", "t1k"], writes=["xkT"])

    _emit_prep(0)
    for tc in range(NTC):
        for mc in range(8):
            pr, prk = ps_r2[mc % 2], ("ps_r" if mc % 2 == 0 else "ps_rb")
            em.op("tensor", [
                (lambda e, kc=kc, mc=mc, pr=pr: e.matmul(
                    pr[:], w_r[:, kc * 1024 + mc * 128: kc * 1024 + (mc + 1) * 128],
                    xrT[:, kc * TC:(kc + 1) * TC], start=(kc == 0), stop=(kc == 7)))
                for kc in range(8)],
                reads=["w0s", "xrT"], writes=[prk])
            em.op("scalar", lambda e, mc=mc, pr=pr: e.activation(
                rT[:, mc * TC:(mc + 1) * TC], pr[:], AF.Sigmoid),
                reads=[prk], writes=["rT"])
            em.op("tensor", [
                (lambda e, kc=kc, mc=mc: e.matmul(
                    ps_k[:], w_k[:, kc * 1024 + mc * 128: kc * 1024 + (mc + 1) * 128],
                    xkT[:, kc * TC:(kc + 1) * TC], start=(kc == 0), stop=(kc == 7)))
                for kc in range(8)],
                reads=["w1s", "xkT"], writes=["ps_k"])
            em.op("vector", lambda e: e.tensor_scalar_max(t2[:], ps_k[:], 0.0),
                  reads=["ps_k"], writes=["t2"])
            em.op("scalar", lambda e, mc=mc: e.activation(
                kaT[:, mc * TC:(mc + 1) * TC], t2[:], AF.Square),
                reads=["t2"], writes=["kaT"])
        for mc in range(8):
            pv, pvk = ps_v2[mc % 2], ("ps_v" if mc % 2 == 0 else "ps_vb")
            em.op("tensor", [
                (lambda e, kc=kc, mc=mc, pv=pv: e.matmul(
                    pv[:], w_v[:, kc * 1024 + mc * 128: kc * 1024 + (mc + 1) * 128],
                    kaT[:, kc * TC:(kc + 1) * TC], start=(kc == 0), stop=(kc == 7)))
                for kc in range(8)],
                reads=["w2s", "kaT"], writes=[pvk])
            em.op("vector", lambda e, mc=mc, pv=pv: e.tensor_mul(
                outT[:, mc * TC:(mc + 1) * TC], rT[:, mc * TC:(mc + 1) * TC], pv[:]),
                reads=["rT", pvk], writes=["rT"])
        if tc + 1 < NTC:
            _emit_prep(tc + 1)
        for tb in range(4):
            j = tc * 4 + tb
            b = xr[tb % 2]
            bk = f"xt{tb % 2}"
            ob = ot[tb % 2]
            obk = f"ot{tb % 2}"
            em.dma("sync", lambda e, j=j, b=b: e.dma_start(
                out=b[:], in_=x3[j * 128:(j + 1) * 128, :]), reads=["g.x3"], writes=[bk])
            em.op("tensor", [
                (lambda e, mc=mc, tb=tb: e.transpose(
                    ps_tb[:, mc * 128:(mc + 1) * 128],
                    outT[:, mc * TC + tb * 128: mc * TC + tb * 128 + 128], sbid32[:]))
                for mc in range(8)],
                reads=["rT", "sbid32"], writes=["ps_tb"])
            em.op("vector", lambda e, b=b, ob=ob: e.tensor_add(ob[:], b[:], ps_tb[:]),
                  reads=[bk, "ps_tb"], writes=[obk])
            em.dma("sync", lambda e, j=j, ob=ob: e.dma_start(
                out=out[j * 128:(j + 1) * 128, :], in_=ob[:]), reads=[obk], writes=["outd"],
                slot="st_" + obk)



def _build_fused():
    nc = bass.Bass("TRN2", target_bir_lowering=False)
    g = {}
    g["xall"] = nc.dram_tensor("xall", [B * T, D], F32, kind="ExternalInput").ap()
    g["xc"] = nc.dram_tensor("xc", [T, D], F32, kind="ExternalInput").ap()
    g["wqkvT"] = nc.dram_tensor("wqkvT", [3 * 128, 1024], F32, kind="ExternalInput").ap()
    g["qkvb"] = nc.dram_tensor("qkvb", [1, 384], F32, kind="ExternalInput").ap()
    g["projT"] = nc.dram_tensor("projT", [D, D], F32, kind="ExternalInput").ap()
    g["pbr"] = nc.dram_tensor("pbr", [1, D], F32, kind="ExternalInput").ap()
    g["cosd"] = nc.dram_tensor("cosd", [128, T], BF16, kind="ExternalInput").ap()
    g["sind"] = nc.dram_tensor("sind", [128, T], BF16, kind="ExternalInput").ap()
    g["r2d"] = nc.dram_tensor("r2d", [128, 128], F32, kind="ExternalInput").ap()
    g["id16"] = nc.dram_tensor("id16", [128, 128], BF16, kind="ExternalInput").ap()
    g["id32"] = nc.dram_tensor("id32", [128, 128], F32, kind="ExternalInput").ap()
    for nm in ["qwT", "kwT", "vwT", "owT"]:
        g[nm] = nc.dram_tensor(nm, [D, D], F32, kind="ExternalInput").ap()
    g["qkvb8"] = nc.dram_tensor("qkvb8", [128, 24], F32, kind="ExternalInput").ap()
    g["wbc"] = nc.dram_tensor("wbc", [128, D], F32, kind="ExternalInput").ap()
    g["bbc"] = nc.dram_tensor("bbc", [128, D], F32, kind="ExternalInput").ap()
    g["obr"] = nc.dram_tensor("obr", [1, D], F32, kind="ExternalInput").ap()
    for nm in ["wq", "wk", "wv", "wo"]:
        g[nm] = nc.dram_tensor(nm, [D, D], F32, kind="ExternalInput").ap()
    g["bob"] = nc.dram_tensor("bob", [128, D], F32, kind="ExternalInput").ap()
    for nm in ["wrT", "wkT2", "wvT2"]:
        g[nm] = nc.dram_tensor(nm, [D, D], F32, kind="ExternalInput").ap()
    g["mu8"] = nc.dram_tensor("mu8", [128, 16], F32, kind="ExternalInput").ap()
    g["y"] = nc.dram_tensor("y", [T, D], F32, kind="ExternalOutput").ap()
    x1i = nc.dram_tensor("x1i", [T, D], F32).ap()
    x2i = nc.dram_tensor("x2i", [T, D], F32).ap()
    x3i = nc.dram_tensor("x3i", [T, D], F32).ap()

    es = ExitStack()
    with es:
        em = Em(nc, es)
        em.tagk = "a."
        with ExitStack() as esp:
            emit_phase_a(em, nc, esp, g["xall"], g["xc"], x1i, g["wqkvT"], g["qkvb"],
                         g["projT"], g["pbr"], g["cosd"], g["sind"], g["r2d"],
                         g["id16"], tag="a_")
        em.phase_barrier()
        em.tagk = "b."
        with ExitStack() as esp:
            emit_phase_b(em, nc, esp, x1i, x2i, g["qwT"], g["kwT"], g["vwT"], g["owT"],
                         g["qkvb8"], g["wbc"], g["bbc"], g["obr"], g["id16"], tag="b_")
        em.phase_barrier()
        em.tagk = "c."
        with ExitStack() as esp:
            emit_phase_c(em, nc, esp, x2i, x3i, g["wq"], g["wk"], g["wv"], g["wo"],
                         g["bob"], g["id16"], tag="c_")
        em.phase_barrier()
        em.tagk = "d."
        with ExitStack() as esp:
            emit_phase_d(em, nc, esp, x3i, g["y"], g["wrT"], g["wkT2"], g["wvT2"],
                         g["mu8"], g["id16"], g["id32"], tag="d_")
        em.wait_all("sync")
        block = es.enter_context(nc.Block())
        em.finalize(block)
    return nc


_NC_CACHE = {}


def _get_nc():
    if "f" not in _NC_CACHE:
        _NC_CACHE["f"] = _build_fused()
    return _NC_CACHE["f"]


def _f32c(x):
    return np.ascontiguousarray(np.asarray(x, dtype=np.float32))


def kernel(L0_attn_x, L0_attn_qkv_w, L0_attn_qkv_b, L0_attn_proj_w, L0_attn_proj_b,
           L0_ffn_router_prob, L0_ffn_norm_w, L0_ffn_norm_b, L0_ffn_q_w, L0_ffn_q_b,
           L0_ffn_k_w, L0_ffn_k_b, L0_ffn_v_w, L0_ffn_v_b, L0_ffn_o_w, L0_ffn_o_b,
           L1_attn_W_Q, L1_attn_W_K, L1_attn_W_V, L1_attn_W_out, L1_attn_b_out,
           L1_ffn_W_r, L1_ffn_W_k, L1_ffn_W_v, L1_ffn_mu_r, L1_ffn_mu_k):
    cores = list(range(8))
    e_cnt = int(np.asarray(L0_ffn_router_prob).shape[2])
    assert e_cnt == 4, "kernel specialized for 4 experts"
    x = np.asarray(L0_attn_x, dtype=np.float32)
    xflat = np.ascontiguousarray(x.reshape(B * T, D))
    id16 = np.eye(128).astype(ml_dtypes.bfloat16)
    id32 = np.eye(128, dtype=np.float32)

    qkv_w = np.asarray(L0_attn_qkv_w, dtype=np.float32)
    qkv_b = np.asarray(L0_attn_qkv_b, dtype=np.float32)
    projT = _f32c(np.asarray(L0_attn_proj_w, dtype=np.float32).T)
    pbr = _f32c(np.asarray(L0_attn_proj_b, dtype=np.float32)[None, :])
    hd = 64
    inv_freq = 1.0 / 10000.0 ** (np.arange(0, hd, 2, dtype=np.float32) / hd)
    pos = np.arange(T, dtype=np.float32)
    freqs = pos[:, None] * inv_freq[None, :]
    cosd = np.ascontiguousarray(np.tile(np.cos(freqs).T, (4, 1))).astype(ml_dtypes.bfloat16)
    sind = np.ascontiguousarray(np.tile(np.sin(freqs).T, (4, 1))).astype(ml_dtypes.bfloat16)
    R2T = np.zeros((128, 128), np.float32)
    for r in range(128):
        blk = (r // 64) * 64
        dd = r % 64
        if dd < 32:
            R2T[r, blk + dd + 32] = -1.0
        else:
            R2T[r, blk + dd - 32] = 1.0
    r2d = np.ascontiguousarray(R2T.T)

    def pair_arr(w):
        wT = np.asarray(w, dtype=np.float32).T
        return np.ascontiguousarray(
            wT.reshape(8, 128, 8, 128).transpose(2, 1, 0, 3).reshape(1024, 1024))
    qwT = pair_arr(L0_ffn_q_w)
    kwT = pair_arr(L0_ffn_k_w)
    vwT = pair_arr(L0_ffn_v_w)
    owT = _f32c(np.asarray(L0_ffn_o_w, dtype=np.float32).T)
    qkvb8 = _f32c(np.concatenate([np.asarray(L0_ffn_q_b, np.float32).reshape(8, 128).T,
                                  np.asarray(L0_ffn_k_b, np.float32).reshape(8, 128).T,
                                  np.asarray(L0_ffn_v_b, np.float32).reshape(8, 128).T], axis=1))
    wbc = _f32c(np.broadcast_to(np.asarray(L0_ffn_norm_w, np.float32), (128, 1024)))
    bbc = _f32c(np.broadcast_to(np.asarray(L0_ffn_norm_b, np.float32), (128, 1024)))
    obr = _f32c(np.asarray(L0_ffn_o_b, np.float32)[None, :])

    def pair_arr2(w):
        return np.ascontiguousarray(
            np.asarray(w, np.float32).reshape(8, 128, 8, 128)
            .transpose(2, 1, 0, 3).reshape(1024, 1024))
    wq = pair_arr2(np.asarray(L1_attn_W_Q, np.float32).reshape(1024, 1024))
    wk = pair_arr2(np.asarray(L1_attn_W_K, np.float32).reshape(1024, 1024))
    wv = pair_arr2(np.asarray(L1_attn_W_V, np.float32).reshape(1024, 1024))
    wo = _f32c(np.asarray(L1_attn_W_out, np.float32).transpose(1, 0, 2).reshape(1024, 1024))
    bob = _f32c(np.broadcast_to(np.asarray(L1_attn_b_out, np.float32), (128, 1024)))

    wrT = _f32c(np.asarray(L1_ffn_W_r, np.float32).T)
    wkT2 = _f32c(np.asarray(L1_ffn_W_k, np.float32).T)
    wvT2 = _f32c(np.asarray(L1_ffn_W_v, np.float32).T)
    mu8 = _f32c(np.concatenate([np.asarray(L1_ffn_mu_r, np.float32).reshape(8, 128).T,
                                np.asarray(L1_ffn_mu_k, np.float32).reshape(8, 128).T], axis=1))

    shared = {"projT": projT, "pbr": pbr, "cosd": cosd, "sind": sind, "r2d": r2d,
              "id16": id16, "id32": id32, "qwT": qwT, "kwT": kwT, "vwT": vwT,
              "owT": owT, "qkvb8": qkvb8, "wbc": wbc, "bbc": bbc, "obr": obr,
              "wq": wq, "wk": wk, "wv": wv, "wo": wo, "bob": bob,
              "wrT": wrT, "wkT2": wkT2, "wvT2": wvT2, "mu8": mu8, "xall": xflat}
    in_maps = []
    for c in cores:
        wq_parts = []
        for t3 in range(3):
            W = qkv_w[1024 * t3 + 128 * c: 1024 * t3 + 128 * c + 128, :]
            wq_parts.append(np.ascontiguousarray(
                W.T.reshape(8, 128, 128).transpose(1, 0, 2).reshape(128, 1024)))
        wqkvT = _f32c(np.concatenate(wq_parts, axis=0))
        qkvbc = _f32c(np.concatenate(
            [qkv_b[1024 * t3 + 128 * c: 1024 * t3 + 128 * c + 128] for t3 in range(3)])[None, :])
        in_maps.append(dict(shared, wqkvT=wqkvT, qkvb=qkvbc,
                            xc=np.ascontiguousarray(x[c])))
    res = run_bass_kernel_spmd(_get_nc(), in_maps, core_ids=cores)
    return np.stack([np.asarray(r["y"]) for r in res.results], axis=0)
